# version 01
# baseline (speedup 1.0000x reference)
import os
import numpy as np
from contextlib import ExitStack

import concourse.bass as bass
import concourse.mybir as mybir
from concourse.bass_utils import run_bass_kernel_spmd

F32 = mybir.dt.float32
BF16 = mybir.dt.bfloat16
I16 = mybir.dt.int16
ALU = mybir.AluOpType
AF = mybir.ActivationFunctionType
AX = mybir.AxisListType

MAGIC = float(np.float32(1.5 * 2 ** 23))
DEBUG_OUT = False
PHASES = "full"
KSTOP = ""

DM, NCH, T, NSEQ, NG, G, NQT = 1024, 8, 2048, 2, 4, 512, 4
H, KVH, HD, DFF, NOCH = 16, 4, 64, 3072, 24
TOK = NSEQ * T
NGT = NSEQ * NG
PERM = [0, 4, 1, 5, 2, 6, 3, 7, 8, 12, 9, 13, 10, 14, 11, 15]
ENGS = ("sync", "tensor", "vector", "scalar", "gpsimd")
SEM = {}


class Sched:
    def __init__(self):
        self.streams = {k: [] for k in ENGS}
        self.meta = {k: [] for k in ENGS}
        self.cnt = {}

    def newsem(self, name):
        self.cnt[name] = 0

    def wait(self, eng, sem, val=None):
        v = self.cnt[sem] if val is None else val
        if v > 0:
            self.streams[eng].append(lambda e, s=sem, v=v: e.wait_ge(SEM[s], v))
            self.meta[eng].append(("wait", sem, v, eng))
        return v

    def put(self, eng, build, sem=None, inc=1):
        if sem is None:
            self.streams[eng].append(lambda e, b=build: b(e))
            self.meta[eng].append(("op", None, 0, eng))
            return None
        self.cnt[sem] += inc
        v = self.cnt[sem]
        self.streams[eng].append(lambda e, b=build, s=sem, i=inc: b(e).then_inc(SEM[s], i))
        self.meta[eng].append(("inc", sem, v, eng))
        return v

    def chain(self, eng, builds, sem):
        last = None
        for b in builds:
            if last is not None:
                self.wait(eng, sem, last)
            last = self.put(eng, b, sem)
        return last

    def dma(self, eng, out, in_, sem):
        return self.put(eng, lambda e, o=out, i=in_: e.dma_start(out=o, in_=i), sem, 16)

    def flush(self, block):
        for eng in ENGS:
            st = self.streams[eng]
            if st:
                getattr(block, eng)(lambda e, st=list(st): [f(e) for f in st])
                self.streams[eng] = []

    def check(self):
        pos = {k: 0 for k in ENGS}
        cur = {s: 0 for s in self.cnt}
        progress = True
        while progress:
            progress = False
            for eng in ENGS:
                while pos[eng] < len(self.meta[eng]):
                    kind, sem, v, _ = self.meta[eng][pos[eng]]
                    if kind == "wait" and cur[sem] < v:
                        break
                    if kind == "inc":
                        cur[sem] = max(cur[sem], v)
                    pos[eng] += 1
                    progress = True
        stuck = {k: (pos[k], len(self.meta[k])) for k in ENGS if pos[k] < len(self.meta[k])}
        if stuck:
            detail = {k: self.meta[k][pos[k]] for k in stuck}
            raise RuntimeError(f"deadlock: {stuck}; blocked on {detail}; sems {cur}")


def build_program():
    nc = bass.Bass("TRN2", target_bir_lowering=False, detect_race_conditions=False)
    S = Sched()
    for s in ("LD", "LDC", "LDK", "LDW0", "LDW1", "LW0", "LW1", "PS0", "PS1",
              "PL0", "PL1", "WD0", "WD1", "WS", "ST", "GD", "PE", "VE", "SE", "GE", "CC"):
        S.newsem(s)

    ei = lambda n, sh, dt=F32: nc.dram_tensor(n, sh, dt, kind="ExternalInput").ap()
    xt = ei("xt", [NSEQ, DM, T])
    wqT, wkT = ei("wqT", [DM, DM]), ei("wkT", [DM, KVH * HD])
    wvT, woT = ei("wvT", [DM, KVH * HD]), ei("woT", [DM, DM])
    wup, wgate = ei("wup", [DFF, DM]), ei("wgate", [DFF, DM])
    wdown, wmod = ei("wdown", [DM, DFF]), ei("wmod", [DM, 6 * DM])
    cTd = ei("cTd", [DM, NSEQ])
    qnwd, knwd = ei("qnwd", [128, 1]), ei("knwd", [128, 1])
    identfd, identbd = ei("identfd", [128, 128]), ei("identbd", [128, 128], BF16)
    ind2d, p2d, indqd = ei("ind2d", [128, 2]), ei("p2d", [2, 128]), ei("indqd", [16, 1024])
    ind16d = ei("ind16d", [128, 128])

    yTd = nc.dram_tensor("yTd", [NSEQ, DM, T], F32, kind="ExternalOutput").ap()
    dbg = "ExternalOutput" if DEBUG_OUT else "Internal"
    x2d = nc.dram_tensor("x2d", [NSEQ, DM, T], F32, kind=dbg).ap()
    xhd = nc.dram_tensor("xhd", [NSEQ, DM, T], F32, kind=dbg).ap()
    scald = nc.dram_tensor("scald", [1, 16], F32, kind=dbg).ap()
    pud = nc.dram_tensor("pud", [NOCH, 128, TOK], F32, kind="Internal").ap()
    pgd = nc.dram_tensor("pgd", [NOCH, 128, TOK], F32, kind="Internal").ap()
    rud = nc.dram_tensor("rud", [NOCH, 128, TOK], I16, kind="Internal").ap()
    p3d = nc.dram_tensor("p3d", [NOCH, 128, TOK], F32, kind="Internal").ap()
    p4d = nc.dram_tensor("p4d", [NOCH, 128, TOK], F32, kind="Internal").ap()
    wdThd = nc.dram_tensor("wdThd", [NOCH, 128, DM], BF16, kind="Internal").ap()
    wdTld = nc.dram_tensor("wdTld", [NOCH, 128, DM], BF16, kind="Internal").ap()
    mrst_d = nc.dram_tensor("mrst_d", [NSEQ, 2, T], F32, kind="Internal").ap()
    srd = nc.dram_tensor("srd", [16, 128], F32, kind="Internal").ap()
    qdbg = nc.dram_tensor("qdbg", [NCH, 128, G], F32, kind=dbg).ap()
    kdbg = nc.dram_tensor("kdbg", [2, 128, T], F32, kind=dbg).ap()
    ydbg = nc.dram_tensor("ydbg", [128, NCH * G], F32, kind=dbg).ap()
    qsdbg = nc.dram_tensor("qsdbg", [16, G], F32, kind=dbg).ap()
    wodbg = nc.dram_tensor("wodbg", [128, 2048], F32, kind=dbg).ap()
    aodbg = nc.dram_tensor("aodbg", [NCH, 128, G], F32, kind=dbg).ap()
    moddbg1 = nc.dram_tensor("moddbg1", [128, 96], F32, kind=dbg).ap()
    moddbg2 = nc.dram_tensor("moddbg2", [128, 96], F32, kind=dbg).ap()
    ydbg2 = nc.dram_tensor("ydbg2", [128, NCH * G], F32, kind=dbg).ap()
    ccin = [nc.dram_tensor(f"cc{i}in", [1, 8], F32).ap() for i in range(4)]
    ccout = [nc.dram_tensor(f"cc{i}out", [1, 8], F32).ap() for i in range(4)]

    es = ExitStack()
    with es:
        for name in S.cnt:
            SEM[name] = es.enter_context(nc.semaphore(name))
        block = es.enter_context(nc.Block())
        sbuf = lambda n, sh, dt=F32: es.enter_context(nc.sbuf_tensor(n, sh, dt))

        ones128 = sbuf("ones128", [128, 1])
        ones1 = sbuf("ones1", [1, 128])
        csts = sbuf("csts", [128, 8])
        qnw_t, knw_t = sbuf("qnw_t", [128, 1]), sbuf("knw_t", [128, 1])
        identf_t = sbuf("identf_t", [128, 128])
        identb_t = sbuf("identb_t", [128, 128], BF16)
        ind2_t, p2_t = sbuf("ind2_t", [128, 2]), sbuf("p2_t", [2, 128])
        modT = sbuf("modT", [128, 96])
        siluc, ctile = sbuf("siluc", [128, 16]), sbuf("ctile", [128, 16])
        rows = sbuf("rows", [1, 16])
        z8 = sbuf("z8", [1, 8])
        vecb = sbuf("vecb", [128, 8])
        avec, bvec = sbuf("avec", [128, 16]), sbuf("bvec", [128, 16])
        g2vec = sbuf("g2vec", [128, 16])
        maxacc = sbuf("maxacc", [128, 8])
        bufD = sbuf("bufD", [128, 2 * G])
        att = ExitStack()
        sba = lambda n, sh, dt=F32: att.enter_context(nc.sbuf_tensor(n, sh, dt))
        indq_t = sba("indq_t", [16, 1024])
        ind16_t = sba("ind16_t", [128, 128])
        hl_lo, hl_hi = sba("hl_lo", [1, 128]), sba("hl_hi", [1, 128])
        mask_diag, mask_far = sba("mask_diag", [128, 128]), sba("mask_far", [128, 128])
        mrow, vrow, rrow = sba("mrow", [1, G]), sba("vrow", [1, G]), sba("rrow", [1, G])

        cst = lambda i: csts[:, i:i + 1]

        for dst, src in ((qnw_t, qnwd), (knw_t, knwd), (identf_t, identfd),
                         (identb_t, identbd), (ind2_t, ind2d), (p2_t, p2d),
                         (indq_t, indqd), (ind16_t, ind16d)):
            S.dma("sync", dst[:, :], src[:, :], "LDK")
        S.dma("sync", ctile[:, :].rearrange("p (a s) -> p a s", a=NCH),
              cTd.rearrange("(a p) s -> p a s", p=128), "LDK")
        ld_consts = S.cnt["LDK"]

        def _consts(e):
            e.memset(ones128[:, :], 1.0)
            e.memset(ones1[:, :], 1.0)
            for i, v in enumerate((0.125, 1e-5, 1e-6, 1.0 / 64.0, -0.5, 1.0 / 1024.0,
                                   float(np.float32(1.0 / 7.0)),
                                   float(np.float32(1.0 / 32767.0)))):
                e.memset(csts[:, i:i + 1], v)
            e.memset(mask_diag[:, :], 1.0)
            e.memset(mask_far[:, :], 1.0)
            e.memset(z8[:, :], 0.0)
            e.memset(maxacc[:, :], 0.0)
            e.memset(hl_lo[0:1, 0:64], 1.0)
            e.memset(hl_lo[0:1, 64:128], 0.0)
            e.memset(hl_hi[0:1, 0:64], 0.0)
            e.memset(hl_hi[0:1, 64:128], 1.0)
            return e.engine_nop()
        ve_consts = S.put("vector", _consts, "VE")

        S.wait("gpsimd", "VE", ve_consts)
        S.put("gpsimd", lambda e: e.affine_select(
            mask_diag[:, :], mask_diag[:, :], pattern=[[1, 128]],
            compare_op=ALU.is_ge, fill=0.0, channel_multiplier=-1), "GE")
        ge_masks = S.put("gpsimd", lambda e: e.affine_select(
            mask_far[:, :], mask_far[:, :], pattern=[[-1, 128]],
            compare_op=ALU.is_ge, fill=0.0, channel_multiplier=1, base=-1), "GE")

        S.wait("scalar", "LDK", ld_consts)
        se_sig = S.put("scalar", lambda e: e.activation(
            out=siluc[:, :], in_=ctile[:, :], func=AF.Sigmoid), "SE")
        S.wait("vector", "SE", se_sig)
        S.wait("vector", "LDK", ld_consts)
        ve_silu = S.put("vector", lambda e: e.tensor_mul(
            siluc[:, :], siluc[:, :], ctile[:, :]), "VE")

        ph0 = ExitStack()
        wmt = ph0.enter_context(nc.sbuf_tensor("wmt", [128, NCH * 128], F32))
        ps_mod = ph0.enter_context(nc.psum_tensor("ps_mod", [128, 2], F32))
        for oc in range(48):
            if oc > 0:
                S.wait("sync", "PE", S.cnt["PE"])
            for d in range(NCH):
                S.dma("sync", wmt[:, 128 * d:128 * (d + 1)],
                      wmod[128 * d:128 * (d + 1), 128 * oc:128 * (oc + 1)], "LDC")
            S.wait("tensor", "LDC")
            S.wait("tensor", "VE", ve_silu)
            if oc > 0:
                S.wait("tensor", "VE")
            def _mm(e, oc=oc):
                for d in range(NCH):
                    ins = e.matmul(ps_mod[:, :], wmt[:, 128 * d:128 * (d + 1)],
                                   siluc[:, 2 * d:2 * d + 2],
                                   start=(d == 0), stop=(d == NCH - 1))
                return ins
            S.put("tensor", _mm, "PE")
            S.wait("vector", "PE")
            S.put("vector", lambda e, oc=oc: e.tensor_copy(
                out=modT[:, 2 * oc:2 * oc + 2], in_=ps_mod[:, :]), "VE")

        ops = [lambda e: e.tensor_scalar(out=modT[:, 16:32], in0=modT[:, 16:32],
                                         scalar1=1.0, scalar2=None, op0=ALU.add),
               lambda e: e.tensor_scalar(out=modT[:, 64:80], in0=modT[:, 64:80],
                                         scalar1=1.0, scalar2=None, op0=ALU.add)]
        for c0 in (32, 80):
            sl = modT[:, c0:c0 + 16]
            t, m, m2 = g2vec[:, 0:16], avec[:, 0:16], bvec[:, 0:16]
            ops += [
                lambda e, sl=sl, t=t: e.tensor_scalar(out=t, in0=sl, scalar1=-1.0,
                                                      scalar2=None, op0=ALU.mult),
                lambda e, sl=sl, t=t: e.tensor_tensor(out=t, in0=sl, in1=t, op=ALU.max),
                lambda e, t=t: e.tensor_scalar(out=t, in0=t, scalar1=0.25,
                                               scalar2=None, op0=ALU.max),
                lambda e, sl=sl, m=m: e.tensor_scalar(out=m, in0=sl, scalar1=0.0,
                                                      scalar2=None, op0=ALU.is_ge),
                lambda e, m=m, m2=m2: e.tensor_scalar(out=m2, in0=m, scalar1=2.0,
                                                      scalar2=-1.0, op0=ALU.mult, op1=ALU.add),
                lambda e, sl=sl, t=t, m2=m2: e.tensor_mul(sl, t, m2),
            ]
        ve_mod = S.chain("vector", ops, "VE")
        if DEBUG_OUT:
            S.wait("sync", "VE", ve_mod)
            S.dma("sync", moddbg1[:, :], modT[:, :], "ST")
        S.flush(block)
        ph0.close()
        if KSTOP == "mod":
            S.wait("sync", "LDK", S.cnt["LDK"])
            S.wait("sync", "LDC", S.cnt["LDC"])
            S.wait("sync", "VE", S.cnt["VE"])
            S.flush(block)
            S.check()
            return nc

        def mod_ap(p, ch, s):
            c = (8 * p + ch) * 2 + s
            return modT[:, c:c + 1]

        bufA = sba("bufA", [128, NCH * G])
        bufB = sba("bufB", [128, NCH * G])
        bufC = sba("bufC", [128, NCH * G])
        ysb = sba("ysb", [128, NCH * G])
        kvp = [sba(f"kvp{i}", [128, T]) for i in range(2)]
        qp = [sba(f"qp{i}", [128, G]) for i in range(NCH)]
        vext = sba("vext", [128, 16 * 260])
        attnT = sba("attnT", [128, 640])
        qssr = sba("qssr", [16, G])
        srow16 = sba("srow16", [16, 128])
        recip16 = sba("recip16", [16, 128])
        sr65 = sba("sr65", [65, 2048])
        wk_s = sba("wk_s", [128, NCH * 256])
        wv_s = sba("wv_s", [128, NCH * 256])
        wq_s = sba("wq_s", [128, NCH * 1024])
        wo_g = sba("wo_g", [128, 2 * 1024])
        ps_sc = att.enter_context(nc.psum_tensor("ps_sc", [128, 640], F32))
        ps_y = att.enter_context(nc.psum_tensor("ps_y", [128, 2048], F32))
        ps_a = att.enter_context(nc.psum_tensor("ps_a", [128, 512], F32))
        ps_b = att.enter_context(nc.psum_tensor("ps_b", [128, 512], F32))

        Atile = lambda d: bufA[:, G * d:G * (d + 1)]
        Btile = lambda d: bufB[:, G * d:G * (d + 1)]
        Ctile = lambda d: bufC[:, G * d:G * (d + 1)]

        S.wait("sync", "PE", S.cnt["PE"])
        for d in range(NCH):
            S.dma("sync", wk_s[:, 256 * d:256 * (d + 1)], wkT[128 * d:128 * (d + 1), :], "LDC")
            S.dma("sync", wv_s[:, 256 * d:256 * (d + 1)], wvT[128 * d:128 * (d + 1), :], "LDC")
            S.dma("sync", wq_s[:, 1024 * d:1024 * (d + 1)], wqT[128 * d:128 * (d + 1), :], "LDC")
        ld_attw = S.cnt["LDC"]
        S.wait("vector", "VE", ve_mod)
        ve_vext1 = S.put("vector", lambda e: e.memset(
            vext[:, :].rearrange("p (t k c) -> p t k c", t=16, k=4)[:, :, :, 64:65], 1.0), "VE")

        def ln_stats(src_tiles, wait_ve=None):
            pe_mm = 0
            for d in range(NCH):
                half = bufD[:, G * (d % 2):G * (d % 2) + G]
                if wait_ve is not None and d == 0:
                    S.wait("scalar", "VE", wait_ve)
                if d >= 2:
                    S.wait("scalar", "PE", pe_mm_prev)
                se_sq = S.put("scalar", lambda e, d=d, half=half: e.activation(
                    out=half, in_=src_tiles[d], func=AF.Square), "SE")
                S.wait("tensor", "SE", se_sq)
                def _mm(e, d=d, half=half):
                    e.matmul(ps_sc[0:1, 0:G], ones128[:, :], src_tiles[d],
                             start=(d == 0), stop=(d == NCH - 1))
                    return e.matmul(ps_y[0:1, 0:G], ones128[:, :], half,
                                    start=(d == 0), stop=(d == NCH - 1))
                pe_mm_prev = pe_mm
                pe_mm = S.put("tensor", _mm, "PE")
            S.wait("vector", "PE", pe_mm)
            S.wait("vector", "GD")
            def _rows(e):
                e.tensor_scalar(out=mrow[:, :], in0=ps_sc[0:1, 0:G],
                                scalar1=cst(5)[0:1, :], scalar2=None, op0=ALU.mult)
                e.tensor_scalar(out=vrow[:, :], in0=ps_y[0:1, 0:G],
                                scalar1=cst(5)[0:1, :], scalar2=None, op0=ALU.mult)
                e.tensor_mul(rrow[:, :], mrow[:, :], mrow[:, :])
                return e.tensor_sub(vrow[:, :], vrow[:, :], rrow[:, :])
            ve_rows = S.put("vector", _rows, "VE")
            S.wait("scalar", "VE", ve_rows)
            def _rst(e):
                e.activation(out=rrow[:, :], in_=vrow[:, :], func=AF.Ln,
                             scale=1.0, bias=cst(1)[0:1, :])
                return e.activation(out=rrow[:, :], in_=rrow[:, :], func=AF.Exp,
                                    scale=cst(4)[0:1, :])
            return S.put("scalar", _rst, "SE"), ve_rows

        def bcast_mr(m_ap, r_ap, wait_se=None, wait_ve=None):
            if wait_se is not None:
                S.wait("tensor", "SE", wait_se)
            if wait_ve is not None:
                S.wait("tensor", "VE", wait_ve)
            def _bc(e):
                e.matmul(ps_sc[:, 0:G], ones1[:, :], m_ap, start=True, stop=True)
                return e.matmul(ps_y[:, 0:G], ones1[:, :], r_ap, start=True, stop=True)
            return S.put("tensor", _bc, "PE")

        def h1_build(x_tiles, out_tiles, s, pe_bc):
            S.wait("vector", "PE", pe_bc)
            def _h1(e, s=s):
                for d in range(NCH):
                    e.tensor_sub(out_tiles[d], x_tiles[d], ps_sc[:, 0:G])
                    e.tensor_mul(out_tiles[d], out_tiles[d], ps_y[:, 0:G])
                    ins = e.tensor_scalar(out=out_tiles[d], in0=out_tiles[d],
                                          scalar1=mod_ap(1, d, s), scalar2=mod_ap(0, d, s),
                                          op0=ALU.mult, op1=ALU.add)
                return ins
            return S.put("vector", _h1, "VE")

        prev = {"VE": 0, "PE": 0}

        A2STOPS = ('a2g1', 'a2q', 'a2att', 'a2sc', 'a2av', 'a2sr')
        seqs = [0] if KSTOP in ('a1', 'a2') or KSTOP in A2STOPS else list(range(NSEQ))
        for s in seqs:
            for g in range(NG):
                t0 = g * G
                S.wait("sync", "VE", prev["VE"])
                S.wait("sync", "PE", prev["PE"])
                for d in range(NCH):
                    S.dma("sync", Atile(d), xt[s, 128 * d:128 * (d + 1), t0:t0 + G], "LD")
                S.wait("scalar", "LD")
                S.wait("tensor", "LDC", ld_attw)
                S.wait("tensor", "LD")
                se_r, ve_rows = ln_stats([Atile(d) for d in range(NCH)],
                                         wait_ve=prev["VE"])
                S.wait("gpsimd", "SE", se_r)
                S.wait("gpsimd", "VE", ve_rows)
                S.dma("gpsimd", mrst_d[s, 0:1, t0:t0 + G], mrow[:, :], "GD")
                S.dma("gpsimd", mrst_d[s, 1:2, t0:t0 + G], rrow[:, :], "GD")
                pe_bc = bcast_mr(mrow[:, :], rrow[:, :], wait_se=se_r, wait_ve=ve_rows)
                ve_h1 = h1_build([Atile(d) for d in range(NCH)],
                                 [Btile(d) for d in range(NCH)], s, pe_bc)

                for kv in range(2):
                    S.wait("tensor", "VE", ve_h1)
                    if kv == 1:
                        S.wait("tensor", "VE")
                    def _kmm(e, kv=kv):
                        for d in range(NCH):
                            ins = e.matmul(ps_a[:, 0:G],
                                           wk_s[:, 256 * d + 128 * kv:256 * d + 128 * (kv + 1)],
                                           Btile(d), start=(d == 0), stop=(d == NCH - 1))
                        return ins
                    pe_k = S.put("tensor", _kmm, "PE")
                    S.wait("vector", "PE", pe_k)
                    ve_kev = S.put("vector", lambda e: e.tensor_copy(
                        out=bufD[:, 0:G], in_=ps_a[:, 0:G]), "VE")
                    S.wait("scalar", "VE", ve_kev)
                    se_ksq = S.put("scalar", lambda e: e.activation(
                        out=bufD[:, G:2 * G], in_=bufD[:, 0:G], func=AF.Square), "SE")
                    S.wait("tensor", "SE", se_ksq)
                    pe_kss = S.put("tensor", lambda e: e.matmul(
                        ps_b[0:2, 0:G], ind2_t[:, :], bufD[:, G:2 * G],
                        start=True, stop=True), "PE")
                    S.wait("vector", "PE", pe_kss)
                    ve_kssr = S.put("vector", lambda e: e.tensor_copy(
                        out=qssr[0:2, :], in_=ps_b[0:2, 0:G]), "VE")
                    S.wait("scalar", "VE", ve_kssr)
                    def _krst(e):
                        e.activation(out=qssr[0:2, :], in_=qssr[0:2, :], func=AF.Ln,
                                     scale=cst(3)[0:2, :], bias=cst(2)[0:2, :])
                        return e.activation(out=qssr[0:2, :], in_=qssr[0:2, :],
                                            func=AF.Exp, scale=cst(4)[0:2, :])
                    se_krst = S.put("scalar", _krst, "SE")
                    S.wait("tensor", "SE", se_krst)
                    pe_kbc = S.put("tensor", lambda e: e.matmul(
                        ps_b[:, 0:G], p2_t[:, :], qssr[0:2, :], start=True, stop=True), "PE")
                    S.wait("vector", "PE", pe_kbc)
                    def _knrm(e, kv=kv, t0=t0):
                        e.tensor_mul(bufD[:, 0:G], bufD[:, 0:G], ps_b[:, 0:G])
                        return e.tensor_scalar(
                            out=kvp[kv][:, t0:t0 + G], in0=bufD[:, 0:G],
                            scalar1=knw_t[:, :], scalar2=None, op0=ALU.mult)
                    S.put("vector", _knrm, "VE")

                for ts_ in range(4):
                    ti = g * 4 + ts_
                    S.wait("tensor", "VE")
                    def _vmm(e, ts_=ts_):
                        for d in range(NCH):
                            ins = e.matmul(ps_a[:, 0:256],
                                           Btile(d)[:, 128 * ts_:128 * (ts_ + 1)],
                                           wv_s[:, 256 * d:256 * (d + 1)],
                                           start=(d == 0), stop=(d == NCH - 1))
                        return ins
                    pe_v = S.put("tensor", _vmm, "PE")
                    S.wait("vector", "PE", pe_v)
                    S.wait("vector", "VE", ve_vext1)
                    def _vev(e, ti=ti):
                        dst = vext[:, 260 * ti:260 * (ti + 1)].rearrange(
                            "p (k c) -> p k c", k=4)[:, :, 0:64]
                        src = ps_a[:, 0:256].rearrange("p (k c) -> p k c", k=4)
                        return e.tensor_copy(out=dst, in_=src)
                    S.put("vector", _vev, "VE")
                prev = {"VE": S.cnt["VE"], "PE": S.cnt["PE"]}

            if KSTOP == "a1":
                break
            for g in [0] if KSTOP in A2STOPS else range(NG):
                t0 = g * G
                S.wait("sync", "VE", prev["VE"])
                S.wait("sync", "PE", prev["PE"])
                for d in range(NCH):
                    S.dma("sync", Atile(d), xt[s, 128 * d:128 * (d + 1), t0:t0 + G], "LD")
                S.wait("sync", "GD")
                S.wait("sync", "PE")
                S.dma("sync", mrow[:, :], mrst_d[s, 0:1, t0:t0 + G], "LD")
                S.dma("sync", rrow[:, :], mrst_d[s, 1:2, t0:t0 + G], "LD")
                for yc in range(NCH):
                    S.dma("sync", wo_g[:, 128 * yc:128 * (yc + 1)],
                          woT[128 * yc:128 * (yc + 1), 0:128], "LDW0")
                S.wait("tensor", "LD")
                pe_bc = bcast_mr(mrow[:, :], rrow[:, :], wait_ve=prev["VE"])
                ve_h1 = h1_build([Atile(d) for d in range(NCH)],
                                 [Btile(d) for d in range(NCH)], s, pe_bc)

                for qc in range(NCH):
                    S.wait("tensor", "VE", ve_h1)
                    if qc > 0:
                        S.wait("tensor", "VE")
                    def _qmm(e, qc=qc):
                        for d in range(NCH):
                            ins = e.matmul(ps_a[:, 0:G],
                                           wq_s[:, 1024 * d + 128 * qc:1024 * d + 128 * (qc + 1)],
                                           Btile(d), start=(d == 0), stop=(d == NCH - 1))
                        return ins
                    pe_q = S.put("tensor", _qmm, "PE")
                    S.wait("vector", "PE", pe_q)
                    ve_qev = S.put("vector", lambda e, qc=qc: e.tensor_copy(
                        out=Ctile(qc), in_=ps_a[:, 0:G]), "VE")
                    S.wait("scalar", "VE", ve_qev)
                    se_qsq = S.put("scalar", lambda e, qc=qc: e.activation(
                        out=bufD[:, 0:G], in_=Ctile(qc), func=AF.Square), "SE")
                    S.wait("tensor", "SE", se_qsq)
                    pe_qss = S.put("tensor", lambda e, qc=qc: e.matmul(
                        ps_b[0:16, 0:G], ind16_t[:, 16 * qc:16 * (qc + 1)], bufD[:, 0:G],
                        start=(qc == 0), stop=(qc == NCH - 1)), "PE")
                S.wait("vector", "PE", pe_qss)
                ve_qssr = S.put("vector", lambda e: e.tensor_copy(
                    out=qssr[:, :], in_=ps_b[0:16, 0:G]), "VE")
                S.wait("scalar", "VE")
                def _qrst(e):
                    e.activation(out=qssr[:, :], in_=qssr[:, :], func=AF.Ln,
                                 scale=cst(3)[0:16, :], bias=cst(2)[0:16, :])
                    return e.activation(out=qssr[:, :], in_=qssr[:, :],
                                        func=AF.Exp, scale=cst(4)[0:16, :])
                se_qrst = S.put("scalar", _qrst, "SE")
                for qc in range(NCH):
                    S.wait("tensor", "SE", se_qrst)
                    S.wait("tensor", "VE")
                    pe_qbc = S.put("tensor", lambda e, qc=qc: e.matmul(
                        ps_b[:, 0:G], indq_t[:, 128 * qc:128 * (qc + 1)],
                        qssr[:, :], start=True, stop=True), "PE")
                    S.wait("vector", "PE", pe_qbc)
                    def _qn(e, qc=qc):
                        e.tensor_mul(bufD[:, 0:G], Ctile(qc)[:, :], ps_b[:, 0:G])
                        return e.tensor_scalar(
                            out=qp[qc][:, :], in0=bufD[:, 0:G],
                            scalar1=qnw_t[:, :], scalar2=None, op0=ALU.mult)
                    S.put("vector", _qn, "VE")
                ve_qn = S.cnt["VE"]
                S.wait("vector", "GE", ge_masks)
                if KSTOP == "a2q":
                    continue

                for qt in range(NQT):
                    qi = g * NQT + qt
                    kts = list(range(max(0, qi - 4), qi + 1))
                    n = len(kts)
                    se_exp_prev = None
                    ve_mask_prev = None
                    pe_av_last = None
                    for hi in range(H + 1):
                        if hi < H:
                            r = hi
                            h = PERM[r]
                            S.wait("tensor", "VE", ve_qn)
                            if se_exp_prev is not None:
                                S.wait("tensor", "SE", se_exp_prev)
                            if hi == 0:
                                S.wait("tensor", "VE")
                                S.wait("tensor", "SE")
                            def _sc(e, h=h, r=r, qt=qt, kts=kts):
                                ko = 64 * ((h // 4) % 2)
                                for j, kt in enumerate(kts):
                                    ins = e.matmul(
                                        ps_sc[:, 128 * j:128 * (j + 1)],
                                        kvp[(h // 4) // 2][ko:ko + 64, 128 * kt:128 * (kt + 1)],
                                        qp[r // 2][64 * (r % 2):64 * (r % 2) + 64,
                                                   128 * qt:128 * (qt + 1)],
                                        start=True, stop=True)
                                return ins
                            pe_sc = S.put("tensor", _sc, "PE")
                        if hi > 0 and KSTOP != "a2sc":
                            rp = hi - 1
                            hpv = PERM[rp]
                            S.wait("tensor", "VE", ve_mask_prev)
                            def _av(e, rp=rp, hpv=hpv, kts=kts, n=n):
                                for j, kt in enumerate(kts):
                                    v0 = 260 * kt + 65 * (hpv // 4)
                                    ins = e.matmul(
                                        ps_y[0:65, 128 * rp:128 * (rp + 1)],
                                        vext[:, v0:v0 + 65],
                                        attnT[:, 128 * j:128 * (j + 1)],
                                        start=(j == 0), stop=(j == n - 1))
                                return ins
                            pe_av_last = S.put("tensor", _av, "PE")
                        if hi < H:
                            S.wait("scalar", "PE", pe_av_last if pe_av_last is not None else pe_sc)
                            se_exp_prev = S.put("scalar", lambda e, n=n: e.activation(
                                out=attnT[:, 0:128 * n], in_=ps_sc[:, 0:128 * n],
                                func=AF.Exp, scale=cst(0)), "SE")
                            S.wait("vector", "SE", se_exp_prev)
                            def _msk(e, n=n, qi=qi):
                                ins = e.tensor_mul(attnT[:, 128 * (n - 1):128 * n],
                                                   attnT[:, 128 * (n - 1):128 * n],
                                                   mask_diag[:, :])
                                if qi >= 4:
                                    ins = e.tensor_mul(attnT[:, 0:128],
                                                       attnT[:, 0:128], mask_far[:, :])
                                return ins
                            ve_mask_prev = S.put("vector", _msk, "VE")

                    if KSTOP in ("a2sc", "a2av"):
                        continue
                    S.wait("scalar", "PE", pe_av_last)
                    S.wait("scalar", "GD")
                    se_sr = S.put("scalar", lambda e: e.copy(
                        out=sr65[:, :], in_=ps_y[0:65, 0:2048]), "SE")
                    S.wait("gpsimd", "SE", se_sr)
                    S.dma("gpsimd", srd.rearrange("r q -> (r q)").rearrange("(p f) -> p f", p=1),
                          sr65[64:65, 0:2048], "GD")
                    S.wait("gpsimd", "GD")
                    S.dma("gpsimd", srow16[:, :], srd[:, :], "GD")
                    if KSTOP == "a2sr":
                        continue
                    def _yev(e, qt=qt):
                        for r in range(0, H, 2):
                            dst = ysb[0:64, 512 * (r // 2) + 128 * qt:512 * (r // 2) + 128 * (qt + 1)]
                            ins = e.tensor_copy(out=dst, in_=sr65[0:64, 128 * r:128 * (r + 1)])
                        return ins
                    S.wait("vector", "SE", se_sr)
                    se_yev = S.put("vector", _yev, "VE")
                    for r in range(1, H, 2):
                        S.dma("gpsimd",
                              ysb[64:128, 512 * (r // 2) + 128 * qt:512 * (r // 2) + 128 * (qt + 1)],
                              sr65[0:64, 128 * r:128 * (r + 1)], "GD")
                    gd_yev = S.cnt["GD"]
                    S.wait("vector", "GD", gd_yev)
                    ve_rec = S.put("vector", lambda e: e.reciprocal(
                        out=recip16[:, :], in_=srow16[:, :]), "VE")
                    for half in range(2):
                        S.wait("tensor", "VE", ve_rec)
                        if half == 0:
                            S.wait("tensor", "VE", se_yev)
                        else:
                            S.wait("tensor", "VE")
                        def _rb(e, half=half):
                            for qc in range(4 * half, 4 * half + 4):
                                ins = e.matmul(ps_b[:, 128 * (qc % 4):128 * (qc % 4 + 1)],
                                               indq_t[:, 128 * qc:128 * (qc + 1)],
                                               recip16[:, :], start=True, stop=True)
                            return ins
                        pe_rb = S.put("tensor", _rb, "PE")
                        S.wait("vector", "PE", pe_rb)
                        if half == 0:
                            S.wait("vector", "GD", gd_yev)
                        def _nrm(e, half=half, qt=qt):
                            for qc in range(4 * half, 4 * half + 4):
                                sl = ysb[:, 512 * qc + 128 * qt:512 * qc + 128 * (qt + 1)]
                                ins = e.tensor_mul(sl, sl, ps_b[:, 128 * (qc % 4):128 * (qc % 4 + 1)])
                            return ins
                        S.put("vector", _nrm, "VE")

                if KSTOP in ("a2att", "a2sc", "a2av", "a2sr"):
                    continue
                if DEBUG_OUT and s == 0 and g == 0:
                    S.wait("sync", "VE", S.cnt["VE"])
                    S.dma("sync", qsdbg[:, :], qssr[:, :], "ST")
                    for qc in range(NCH):
                        S.dma("sync", qdbg[qc], qp[qc][:, :], "ST")
                    for i in range(2):
                        S.dma("sync", kdbg[i], kvp[i][:, :], "ST")
                    S.dma("sync", ydbg[:, :], ysb[:, :], "ST")
                ve_norm_all = S.cnt["VE"]
                pe_o = None
                for dd in range(NCH):
                    if dd > 0:
                        S.wait("sync", "PE", pe_o)
                        for yc in range(NCH):
                            S.dma("sync", wo_g[:, 128 * yc:128 * (yc + 1)],
                                  woT[128 * yc:128 * (yc + 1), 128 * dd:128 * (dd + 1)], "LDW0")
                    S.wait("tensor", "VE", ve_norm_all)
                    if dd > 0:
                        S.wait("tensor", "VE")
                    S.wait("tensor", "LDW0", S.cnt["LDW0"])
                    def _omm(e, dd=dd):
                        for yc in range(NCH):
                            ins = e.matmul(ps_a[:, 0:G],
                                           wo_g[:, 128 * yc:128 * (yc + 1)],
                                           ysb[:, 512 * yc:512 * (yc + 1)],
                                           start=(yc == 0), stop=(yc == NCH - 1))
                        return ins
                    pe_o = S.put("tensor", _omm, "PE")
                    S.wait("vector", "PE", pe_o)
                    def _x2(e, d=dd, s=s):
                        return e.scalar_tensor_tensor(
                            out=Btile(d), in0=ps_a[:, 0:G], scalar=mod_ap(2, d, s),
                            in1=Atile(d), op0=ALU.mult, op1=ALU.add)
                    ve_x2 = S.put("vector", _x2, "VE")
                    S.wait("sync", "VE", ve_x2)
                    S.dma("sync", x2d[s, 128 * dd:128 * (dd + 1), t0:t0 + G], Btile(dd), "ST")

                se_r2, _ = ln_stats([Btile(d) for d in range(NCH)], wait_ve=S.cnt["VE"])
                pe_bc2 = bcast_mr(mrow[:, :], rrow[:, :], wait_se=se_r2)
                S.wait("vector", "PE", pe_bc2)
                def _xh(e, s=s):
                    for d in range(NCH):
                        e.tensor_sub(Ctile(d), Btile(d), ps_sc[:, 0:G])
                        e.tensor_mul(Ctile(d), Ctile(d), ps_y[:, 0:G])
                        e.tensor_scalar(out=bufD[:, 0:G], in0=Ctile(d),
                                        scalar1=mod_ap(4, d, s), scalar2=mod_ap(3, d, s),
                                        op0=ALU.mult, op1=ALU.add)
                        e.tensor_reduce(out=maxacc[:, 1:2], in_=bufD[:, 0:G],
                                        axis=AX.X, op=ALU.max, apply_absolute_value=True)
                        ins = e.tensor_tensor(out=maxacc[:, 0:1], in0=maxacc[:, 0:1],
                                              in1=maxacc[:, 1:2], op=ALU.max)
                    return ins
                ve_xh = S.put("vector", _xh, "VE")
                S.wait("sync", "VE", ve_xh)
                for d in range(NCH):
                    S.dma("sync", xhd[s, 128 * d:128 * (d + 1), t0:t0 + G], Ctile(d), "ST")
                prev = {"VE": S.cnt["VE"], "PE": S.cnt["PE"]}

        if KSTOP in ("a1", "a2", "a2q") or KSTOP in A2STOPS:
            S.wait("sync", "LD", S.cnt["LD"])
            S.wait("sync", "LDC", S.cnt["LDC"])
            S.wait("sync", "ST", S.cnt["ST"])
            S.wait("gpsimd", "GD", S.cnt["GD"])
            S.wait("sync", "VE", S.cnt["VE"])
            S.wait("sync", "PE", S.cnt["PE"])
            S.wait("sync", "SE", S.cnt["SE"])
            S.flush(block)
            att.close()
            S.check()
            return nc
        S.wait("tensor", "VE")
        pe_tr = S.put("tensor", lambda e: e.transpose(
            ps_b[0:1, 0:128], maxacc[:, 0:1], identf_t[:, :]), "PE")
        S.wait("vector", "PE", pe_tr)
        ve_mh = S.put("vector", lambda e: e.tensor_reduce(
            out=z8[0:1, 0:1], in_=ps_b[0:1, 0:128], axis=AX.X, op=ALU.max), "VE")
        S.wait("gpsimd", "VE", ve_mh)
        S.dma("gpsimd", ccin[0][:, :], z8[:, :], "GD")
        S.wait("gpsimd", "GD")
        cc1 = S.put("gpsimd", lambda e: e.collective_compute(
            "AllReduce", ALU.max, replica_groups=[list(range(8))],
            ins=[ccin[0]], outs=[ccout[0]]), "CC")
        S.wait("gpsimd", "CC", cc1)
        S.dma("gpsimd", z8[:, :], ccout[0][:, :], "GD")
        S.wait("vector", "GD")
        ve_sx = S.chain("vector", [
            lambda e: e.tensor_scalar(out=rows[0:1, 1:2], in0=z8[0:1, 0:1],
                                      scalar1=cst(7)[0:1, :], scalar2=1e-12,
                                      op0=ALU.mult, op1=ALU.max),
            lambda e: e.reciprocal(out=rows[0:1, 2:3], in_=rows[0:1, 1:2]),
        ], "VE")
        S.wait("tensor", "VE", ve_sx)
        pe_sxb = S.put("tensor", lambda e: e.matmul(
            ps_b[:, 0:1], ones1[:, :], rows[0:1, 2:3], start=True, stop=True), "PE")
        S.wait("vector", "PE", pe_sxb)
        def _avec(e):
            for s in range(NSEQ):
                for d in range(NCH):
                    e.tensor_scalar(out=avec[:, 2 * d + s:2 * d + s + 1],
                                    in0=mod_ap(4, d, s),
                                    scalar1=vecb[:, 0:1], scalar2=None, op0=ALU.mult)
                    ins = e.tensor_scalar(out=bvec[:, 2 * d + s:2 * d + s + 1],
                                          in0=mod_ap(3, d, s),
                                          scalar1=vecb[:, 0:1], scalar2=None, op0=ALU.mult)
            return ins
        ve_avec = S.chain("vector", [
            lambda e: e.tensor_copy(out=vecb[:, 0:1], in_=ps_b[:, 0:1]),
            _avec,
        ], "VE")
        if DEBUG_OUT:
            S.wait("sync", "VE", ve_avec)
            S.dma("sync", scald[0:1, 0:8], z8[:, :], "ST")
            S.dma("sync", moddbg2[:, :], modT[:, :], "ST")

        S.flush(block)
        att.close()

        if PHASES == "attn":
            S.wait("sync", "ST", S.cnt["ST"])
            for s in range(NSEQ):
                S.dma("sync", yTd[s], x2d[s], "ST")
            S.wait("sync", "LD", S.cnt["LD"])
            S.wait("sync", "ST", S.cnt["ST"])
            S.wait("gpsimd", "GD", S.cnt["GD"])
            S.flush(block)
            S.check()
            return nc


        ffn = ExitStack()
        sbf = lambda n, sh, dt=F32: ffn.enter_context(nc.sbuf_tensor(n, sh, dt))
        xga = [sbf(f"xga{i}", [128, G]) for i in range(2)]
        wr = [sbf(f"wr{i}", [128, 1024]) for i in range(2)]
        wdq = bufD
        wh_n = sbf("wh_n", [128, 1024], BF16)
        wl_n = sbf("wl_n", [128, 1024], BF16)
        lhsTh = sbf("lhsTh", [128, 2 * 1024], BF16)
        lhsTl = sbf("lhsTl", [128, 2 * 1024], BF16)
        pstg = [sbf(f"pstg{i}", [128, G]) for i in range(2)]
        pu_t = [sbf(f"pu_t{i}", [128, G]) for i in range(2)]
        pg_t = [sbf(f"pg_t{i}", [128, G]) for i in range(2)]
        rg1 = sbf("rg1", [128, G])
        ru1 = sbf("ru1", [128, G])
        sg1 = sbf("sg1", [128, G])
        s161 = sbf("s161", [128, G])
        rg_t = [rg1, rg1]
        ru_t = [ru1, ru1]
        sg_t = [sg1, sg1]
        s16_t = [s161, s161]
        p3_t = [sbf(f"p3_t{i}", [128, G]) for i in range(2)]
        p4_t = [sbf(f"p4_t{i}", [128, G]) for i in range(2)]
        roh = [sbf(f"roh{i}", [128, G], BF16) for i in range(2)]
        rol = [sbf(f"rol{i}", [128, G], BF16) for i in range(2)]
        wdt_s = [sbf(f"wdt_s{i}", [128, 2 * NCH * 128], BF16) for i in range(2)]
        sblk_g = sbf("sblk_g", [128, 96])
        rblk_g = sbf("rblk_g", [128, 96])
        rows2 = sbf("rows2", [2, 1])
        psF = ffn.enter_context(nc.psum_tensor("psF", [128, 4096], F32))
        f2s = ExitStack()
        rxh = f2s.enter_context(nc.sbuf_tensor("rxh", [128, NCH * TOK], BF16))
        rxl = f2s.enter_context(nc.sbuf_tensor("rxl", [128, NCH * TOK], BF16))

        st_attn = S.cnt["ST"]

        ve_f2a = 0
        ve_xpar = [0, 0]
        S.wait("sync", "ST", st_attn)
        for g in range(NGT):
            s_, gg = divmod(g, NG)
            t0 = gg * G
            for d in range(NCH):
                par = (g * NCH + d) % 2
                S.wait("sync", "VE", ve_xpar[par])
                S.dma("sync", xga[par][:, :],
                      xhd[s_, 128 * d:128 * (d + 1), t0:t0 + G], f"LDW{par}")
                S.wait("vector", f"LDW{par}")
                if g == 0 and d == 0:
                    S.wait("vector", "VE", ve_avec)
                def _rx(e, s_=s_, g=g, d=d, par=par):
                    col = d * TOK + s_ * T + (g % NG) * G
                    t = xga[par][:, :]
                    e.tensor_scalar(out=t, in0=t,
                                    scalar1=avec[:, 2 * d + s_:2 * d + s_ + 1],
                                    scalar2=bvec[:, 2 * d + s_:2 * d + s_ + 1],
                                    op0=ALU.mult, op1=ALU.add)
                    e.tensor_scalar(out=t, in0=t, scalar1=MAGIC, scalar2=MAGIC,
                                    op0=ALU.add, op1=ALU.subtract)
                    e.tensor_copy(out=rxh[:, col:col + G], in_=t)
                    ins = e.tensor_tensor(out=rxl[:, col:col + G], in0=t,
                                          in1=rxh[:, col:col + G], op=ALU.subtract)
                    return ins
                ve_xpar[par] = S.put("vector", _rx, "VE")
        ve_f2a = S.cnt["VE"]

        wsrc = [wup, wgate]
        pdst = [pud, pgd]
        mcol = [2, 3]
        deq_ve = [0, 0]
        tr_pe = 0
        ev_se = 0
        mm_pe = {0: 0, 1: 0}
        ps_dma = [0, 0]

        def dequant_1024(par_buf, ve_sem_tag="VE"):
            n_blk = 32
            S.put("vector", lambda e: e.tensor_reduce(
                out=sblk[:, 0:n_blk],
                in_=par_buf.rearrange("p (b c) -> p b c", b=n_blk),
                axis=AX.X, op=ALU.max, apply_absolute_value=True), "VE")
            S.chain("vector", [
                lambda e: e.tensor_scalar(out=sblk[:, 0:n_blk], in0=sblk[:, 0:n_blk],
                                          scalar1=cst(6), scalar2=1e-12,
                                          op0=ALU.mult, op1=ALU.max),
                lambda e: e.reciprocal(out=rblk[:, 0:n_blk], in_=sblk[:, 0:n_blk]),
                lambda e: e.tensor_tensor(
                    out=wdq[:, 0:1024].rearrange("p (b c) -> p b c", b=n_blk),
                    in0=par_buf.rearrange("p (b c) -> p b c", b=n_blk),
                    in1=rblk[:, 0:n_blk].unsqueeze(2).broadcast_to([128, n_blk, 32]),
                    op=ALU.mult),
            ], "VE")
            def _fin(e):
                e.tensor_scalar(out=wdq[:, 0:1024], in0=wdq[:, 0:1024], scalar1=MAGIC,
                                scalar2=MAGIC, op0=ALU.add, op1=ALU.subtract)
                e.tensor_tensor(
                    out=wdq[:, 0:1024].rearrange("p (b c) -> p b c", b=n_blk),
                    in0=wdq[:, 0:1024].rearrange("p (b c) -> p b c", b=n_blk),
                    in1=sblk[:, 0:n_blk].unsqueeze(2).broadcast_to([128, n_blk, 32]),
                    op=ALU.mult)
                e.tensor_copy(out=wh_n[:, :], in_=wdq[:, 0:1024])
                return e.tensor_tensor(out=wl_n[:, :], in0=wdq[:, 0:1024],
                                       in1=wh_n[:, :], op=ALU.subtract)
            return S.put("vector", _fin, "VE")

        sblk = sblk_g
        rblk = rblk_g

        first_mm = True
        for wi in range(2):
            for oc in range(NOCH):
                it = wi * NOCH + oc
                par = it % 2
                S.wait("sync", "VE", deq_ve[par])
                S.dma("sync", wr[par][:, :], wsrc[wi][128 * oc:128 * (oc + 1), :], f"LW{par}")
                S.wait("vector", f"LW{par}")
                S.wait("vector", "PE", tr_pe)
                deq_ve[par] = dequant_1024(wr[par][:, :])
                S.wait("tensor", "VE", deq_ve[par])
                S.wait("tensor", "SE", ev_se)
                if not first_mm:
                    S.wait("tensor", "PE", mm_pe[0])
                def _tr(e):
                    for ih in range(NCH):
                        e.transpose(psF[:, 64 * ih:64 * ih + 64].bitcast(BF16),
                                    wh_n[:, 128 * ih:128 * (ih + 1)], identb_t[:, :])
                        ins = e.transpose(psF[:, 512 + 64 * ih:512 + 64 * ih + 64].bitcast(BF16),
                                          wl_n[:, 128 * ih:128 * (ih + 1)], identb_t[:, :])
                    return ins
                tr_pe = S.put("tensor", _tr, "PE")
                S.wait("scalar", "PE", tr_pe)
                S.wait("scalar", "PE", mm_pe[1])
                def _trev(e, par=par):
                    for ih in range(NCH):
                        e.copy(out=lhsTh[:, 1024 * par + 128 * ih:1024 * par + 128 * (ih + 1)],
                               in_=psF[:, 64 * ih:64 * ih + 64].bitcast(BF16))
                        ins = e.copy(out=lhsTl[:, 1024 * par + 128 * ih:1024 * par + 128 * (ih + 1)],
                                     in_=psF[:, 512 + 64 * ih:512 + 64 * ih + 64].bitcast(BF16))
                    return ins
                trev_se = S.put("scalar", _trev, "SE")

                for half in range(2):
                    S.wait("tensor", "SE", trev_se)
                    S.wait("tensor", "SE", ev_se)
                    if first_mm:
                        S.wait("tensor", "VE", ve_f2a)
                    def _mm(e, par=par, half=half):
                        for pi, (lt, rx) in enumerate(((lhsTh, rxh), (lhsTh, rxl), (lhsTl, rxh))):
                            for ih in range(NCH):
                                step = pi * NCH + ih
                                for tt in range(4):
                                    ins = e.matmul(
                                        psF[:, 2048 * half + 512 * tt:2048 * half + 512 * (tt + 1)],
                                        lt[:, 1024 * par + 128 * ih:1024 * par + 128 * (ih + 1)],
                                        rx[:, ih * TOK + 2048 * half + 512 * tt:
                                           ih * TOK + 2048 * half + 512 * (tt + 1)],
                                        start=(step == 0), stop=(step == 23))
                        return ins
                    mm_pe[half] = S.put("tensor", _mm, "PE")
                    first_mm = False
                    for tt in range(4):
                        sp = tt % 2
                        S.wait("scalar", "PE", mm_pe[half])
                        S.wait("scalar", f"PS{sp}", S.cnt[f"PS{sp}"])
                        S.wait("scalar", "VE")
                        ev_se = S.put("scalar", lambda e, half=half, tt=tt, sp=sp: e.copy(
                            out=pstg[sp][:, :],
                            in_=psF[:, 2048 * half + 512 * tt:2048 * half + 512 * (tt + 1)]), "SE")
                        S.wait("vector", "SE", ev_se)
                        S.put("vector", lambda e, sp=sp: e.tensor_reduce(
                            out=maxacc[:, 1:2], in_=pstg[sp][:, :], axis=AX.X,
                            op=ALU.max, apply_absolute_value=True), "VE")
                        S.chain("vector", [
                            lambda e, wi=wi: e.tensor_tensor(
                                out=maxacc[:, mcol[wi]:mcol[wi] + 1],
                                in0=maxacc[:, mcol[wi]:mcol[wi] + 1],
                                in1=maxacc[:, 1:2], op=ALU.max)], "VE")
                        S.wait("gpsimd", "SE", ev_se)
                        ps_dma[sp] = S.dma(
                            "gpsimd",
                            pdst[wi][oc, :, 2048 * half + 512 * tt:2048 * half + 512 * (tt + 1)],
                            pstg[sp][:, :], f"PS{sp}")

        S.flush(block)
        f2s.close()
        S.wait("tensor", "VE")
        S.wait("tensor", "SE")
        pe_tr2 = S.put("tensor", lambda e: e.transpose(
            psF[0:2, 0:128], maxacc[:, 2:4], identf_t[:, :]), "PE")
        S.wait("vector", "PE", pe_tr2)
        ve_m2 = S.put("vector", lambda e: e.tensor_reduce(
            out=rows2[0:2, 0:1], in_=psF[0:2, 0:128], axis=AX.X, op=ALU.max), "VE")
        S.wait("gpsimd", "VE", ve_m2)
        S.wait("gpsimd", "PS0", S.cnt["PS0"])
        S.wait("gpsimd", "PS1", S.cnt["PS1"])
        S.wait("gpsimd", "VE", ve_consts)
        S.dma("gpsimd", ccin[1][0:1, 2:8], z8[0:1, 2:8], "GD")
        S.dma("gpsimd", ccin[1][0:1, 0:2], rows2[0:2, 0:1], "GD")
        S.wait("gpsimd", "GD")
        cc2 = S.put("gpsimd", lambda e: e.collective_compute(
            "AllReduce", ALU.max, replica_groups=[list(range(8))],
            ins=[ccin[1]], outs=[ccout[1]]), "CC")
        S.wait("gpsimd", "CC", cc2)
        S.dma("gpsimd", z8[:, :], ccout[1][:, :], "GD")
        S.wait("vector", "GD")
        ve_ar2 = S.chain("vector", [
            lambda e: e.tensor_scalar(out=rows[0:1, 3:4], in0=z8[0:1, 0:1],
                                      scalar1=rows[0:1, 1:2], scalar2=None, op0=ALU.mult),
            lambda e: e.tensor_scalar(out=rows[0:1, 4:5], in0=z8[0:1, 1:2],
                                      scalar1=rows[0:1, 1:2], scalar2=None, op0=ALU.mult),
            lambda e: e.tensor_scalar(out=rows[0:1, 5:6], in0=rows[0:1, 3:4],
                                      scalar1=cst(7)[0:1, :], scalar2=1e-12,
                                      op0=ALU.mult, op1=ALU.max),
            lambda e: e.tensor_scalar(out=rows[0:1, 8:9], in0=rows[0:1, 4:5],
                                      scalar1=cst(7)[0:1, :], scalar2=1e-12,
                                      op0=ALU.mult, op1=ALU.max),
            lambda e: e.reciprocal(out=rows[0:1, 6:7], in_=rows[0:1, 5:6]),
            lambda e: e.tensor_scalar(out=rows[0:1, 7:8], in0=rows[0:1, 6:7],
                                      scalar1=rows[0:1, 1:2], scalar2=None, op0=ALU.mult),
            lambda e: e.reciprocal(out=rows[0:1, 9:10], in_=rows[0:1, 8:9]),
            lambda e: e.tensor_scalar(out=rows[0:1, 10:11], in0=rows[0:1, 9:10],
                                      scalar1=rows[0:1, 1:2], scalar2=None, op0=ALU.mult),
        ], "VE")
        last_ve = ve_ar2
        for col, slot in ((1, 7), (2, 10), (3, 8)):
            S.wait("tensor", "VE", last_ve)
            pe_b = S.put("tensor", lambda e, slot=slot: e.matmul(
                psF[:, 0:1], ones1[:, :], rows[0:1, slot:slot + 1],
                start=True, stop=True), "PE")
            S.wait("vector", "PE", pe_b)
            last_ve = S.put("vector", lambda e, col=col: e.tensor_copy(
                out=vecb[:, col:col + 1], in_=psF[:, 0:1]), "VE")
        ve_vecb2 = last_ve

        cons3 = [0, 0]
        gp3 = [0, 0]
        wd_tile = 0
        ws_marks = 0
        for it in range(NOCH * NCH):
            oc, tt = divmod(it, NCH)
            par = it % 2
            S.wait("gpsimd", "VE", cons3[par])
            S.wait("gpsimd", "GE", gp3[par])
            S.dma("gpsimd", pg_t[par][:, :], pgd[oc, :, 512 * tt:512 * (tt + 1)], f"PL{par}")
            S.wait("vector", f"PL{par}")
            S.wait("vector", "VE", ve_vecb2)
            S.wait("vector", "SE")
            S.wait("vector", "GE")
            def _rg(e, par=par):
                e.tensor_scalar(out=rg_t[par][:, :], in0=pg_t[par][:, :],
                                scalar1=vecb[:, 2:3], scalar2=MAGIC,
                                op0=ALU.mult, op1=ALU.add)
                return e.tensor_scalar(out=rg_t[par][:, :], in0=rg_t[par][:, :],
                                       scalar1=MAGIC, scalar2=None, op0=ALU.subtract)
            ve_rg = S.put("vector", _rg, "VE")
            S.wait("scalar", "VE", ve_rg)
            S.wait("scalar", "GE")
            se_sg2 = S.put("scalar", lambda e, par=par: e.activation(
                out=sg_t[par][:, :], in_=rg_t[par][:, :], func=AF.Sigmoid,
                scale=vecb[:, 3:4]), "SE")
            S.wait("gpsimd", "SE", se_sg2)
            S.wait("gpsimd", "VE")
            def _s16(e, par=par):
                e.tensor_scalar(out=s16_t[par][:, :], in0=sg_t[par][:, :],
                                scalar1=65536.0, scalar2=MAGIC,
                                op0=ALU.mult, op1=ALU.add)
                return e.tensor_scalar(out=s16_t[par][:, :], in0=s16_t[par][:, :],
                                       scalar1=MAGIC, scalar2=65535.0,
                                       op0=ALU.subtract, op1=ALU.min)
            gp3[par] = S.put("gpsimd", _s16, "GE")
            S.wait("vector", "GE", gp3[par])
            def _p3(e, par=par):
                e.tensor_tensor(out=p3_t[par][:, :], in0=rg_t[par][:, :],
                                in1=s16_t[par][:, :], op=ALU.mult)
                return e.tensor_reduce(out=maxacc[:, 1:2], in_=p3_t[par][:, :],
                                       axis=AX.X, op=ALU.max, apply_absolute_value=True)
            ve_p3 = S.put("vector", _p3, "VE")
            cons3[par] = S.chain("vector", [
                lambda e: e.tensor_tensor(out=maxacc[:, 4:5], in0=maxacc[:, 4:5],
                                          in1=maxacc[:, 1:2], op=ALU.max)], "VE")
            S.wait("gpsimd", "VE", cons3[par])
            S.dma("gpsimd", p3d[oc, :, 512 * tt:512 * (tt + 1)], p3_t[par][:, :], f"PS{par}")

            if it % 8 == 0 and wd_tile < NCH * 3:
                dt_, third = divmod(wd_tile, 3)
                wd_tile += 1
                S.wait("sync", "VE", S.cnt["VE"])
                S.wait("sync", "WS", S.cnt["WS"])
                S.dma("sync", wr[0][:, :],
                      wdown[128 * dt_:128 * (dt_ + 1), 1024 * third:1024 * (third + 1)], "LW0")
                S.wait("vector", "LW0")
                S.wait("vector", "SE", S.cnt["SE"])
                dequant_1024(wr[0][:, :])
                ve_wd = S.cnt["VE"]
                S.wait("tensor", "VE", ve_wd)
                S.wait("tensor", "SE", S.cnt["SE"])
                def _trw8(e):
                    for j in range(8):
                        e.transpose(psF[:, 3072 + 64 * j:3072 + 64 * j + 64].bitcast(BF16),
                                    wh_n[:, 128 * j:128 * (j + 1)], identb_t[:, :])
                        ins = e.transpose(psF[:, 3584 + 64 * j:3584 + 64 * j + 64].bitcast(BF16),
                                          wl_n[:, 128 * j:128 * (j + 1)], identb_t[:, :])
                    return ins
                pe_tw = S.put("tensor", _trw8, "PE")
                S.wait("scalar", "PE", pe_tw)
                def _twev(e):
                    for j in range(8):
                        e.copy(out=wh_n[:, 128 * j:128 * (j + 1)],
                               in_=psF[:, 3072 + 64 * j:3072 + 64 * j + 64].bitcast(BF16))
                        ins = e.copy(out=wl_n[:, 128 * j:128 * (j + 1)],
                                     in_=psF[:, 3584 + 64 * j:3584 + 64 * j + 64].bitcast(BF16))
                    return ins
                se_tw = S.put("scalar", _twev, "SE")
                S.wait("sync", "SE", se_tw)
                for j in range(8):
                    ihh = third * 8 + j
                    S.dma("sync", wdThd[ihh, :, 128 * dt_:128 * (dt_ + 1)],
                          wh_n[:, 128 * j:128 * (j + 1)], "WS")
                    S.dma("sync", wdTld[ihh, :, 128 * dt_:128 * (dt_ + 1)],
                          wl_n[:, 128 * j:128 * (j + 1)], "WS")
        ws_marks = S.cnt["WS"]

        S.wait("tensor", "VE")
        pe_tr3 = S.put("tensor", lambda e: e.transpose(
            psF[0:1, 0:128], maxacc[:, 4:5], identf_t[:, :]), "PE")
        S.wait("vector", "PE", pe_tr3)
        ve_m3 = S.put("vector", lambda e: e.tensor_reduce(
            out=z8[0:1, 2:3], in_=psF[0:1, 0:128], axis=AX.X, op=ALU.max), "VE")
        ve_m3b = S.chain("vector", [
            lambda e: e.tensor_scalar(out=rows[0:1, 11:12], in0=rows[0:1, 8:9],
                                      scalar1=1.0 / 65536.0, scalar2=None, op0=ALU.mult),
            lambda e: e.tensor_scalar(out=z8[0:1, 0:1], in0=z8[0:1, 2:3],
                                      scalar1=rows[0:1, 11:12], scalar2=None, op0=ALU.mult),
        ], "VE")
        S.wait("gpsimd", "VE", ve_m3b)
        S.wait("gpsimd", "PS0", S.cnt["PS0"])
        S.wait("gpsimd", "PS1", S.cnt["PS1"])
        S.dma("gpsimd", ccin[2][:, :], z8[:, :], "GD")
        S.wait("gpsimd", "GD")
        cc3 = S.put("gpsimd", lambda e: e.collective_compute(
            "AllReduce", ALU.max, replica_groups=[list(range(8))],
            ins=[ccin[2]], outs=[ccout[2]]), "CC")
        S.wait("gpsimd", "CC", cc3)
        S.dma("gpsimd", z8[:, :], ccout[2][:, :], "GD")
        S.wait("vector", "GD")
        ve_ar3 = S.chain("vector", [
            lambda e: e.tensor_scalar(out=rows[0:1, 12:13], in0=z8[0:1, 0:1],
                                      scalar1=cst(7)[0:1, :], scalar2=1e-12,
                                      op0=ALU.mult, op1=ALU.max),
            lambda e: e.reciprocal(out=rows[0:1, 13:14], in_=rows[0:1, 12:13]),
            lambda e: e.tensor_scalar(out=rows[0:1, 14:15], in0=rows[0:1, 13:14],
                                      scalar1=rows[0:1, 11:12], scalar2=None, op0=ALU.mult),
        ], "VE")
        S.wait("tensor", "VE", ve_ar3)
        pe_b3 = S.put("tensor", lambda e: e.matmul(
            psF[:, 0:1], ones1[:, :], rows[0:1, 14:15], start=True, stop=True), "PE")
        S.wait("vector", "PE", pe_b3)
        ve_vecb4 = S.put("vector", lambda e: e.tensor_copy(
            out=vecb[:, 4:5], in_=psF[:, 0:1]), "VE")

        cons4 = [0, 0]
        gp4 = [0, 0]
        for it in range(NOCH * NCH):
            oc, tt = divmod(it, NCH)
            par = it % 2
            S.wait("gpsimd", "VE", cons4[par])
            S.dma("gpsimd", pu_t[par][:, :], pud[oc, :, 512 * tt:512 * (tt + 1)], f"PL{par}")
            S.dma("gpsimd", p3_t[par][:, :], p3d[oc, :, 512 * tt:512 * (tt + 1)], f"PL{par}")
            S.wait("gpsimd", f"PL{par}")
            S.wait("gpsimd", "VE")
            def _ru(e, par=par):
                e.tensor_scalar(out=ru_t[par][:, :], in0=pu_t[par][:, :],
                                scalar1=vecb[:, 1:2], scalar2=MAGIC,
                                op0=ALU.mult, op1=ALU.add)
                return e.tensor_scalar(out=ru_t[par][:, :], in0=ru_t[par][:, :],
                                       scalar1=MAGIC, scalar2=None, op0=ALU.subtract)
            gp4[par] = S.put("gpsimd", _ru, "GE")
            S.wait("vector", f"PL{par}")
            S.wait("vector", "VE", ve_vecb4)
            def _r3(e, par=par):
                e.tensor_scalar(out=rg_t[par][:, :], in0=p3_t[par][:, :],
                                scalar1=vecb[:, 4:5], scalar2=MAGIC,
                                op0=ALU.mult, op1=ALU.add)
                return e.tensor_scalar(out=rg_t[par][:, :], in0=rg_t[par][:, :],
                                       scalar1=MAGIC, scalar2=None, op0=ALU.subtract)
            ve_r3 = S.put("vector", _r3, "VE")
            S.wait("vector", "GE", gp4[par])
            def _p4(e, par=par):
                e.tensor_tensor(out=p4_t[par][:, :], in0=rg_t[par][:, :],
                                in1=ru_t[par][:, :], op=ALU.mult)
                return e.tensor_reduce(out=maxacc[:, 1:2], in_=p4_t[par][:, :],
                                       axis=AX.X, op=ALU.max, apply_absolute_value=True)
            ve_p4 = S.put("vector", _p4, "VE")
            cons4[par] = S.chain("vector", [
                lambda e: e.tensor_tensor(out=maxacc[:, 5:6], in0=maxacc[:, 5:6],
                                          in1=maxacc[:, 1:2], op=ALU.max)], "VE")
            S.wait("gpsimd", "VE", cons4[par])
            S.dma("gpsimd", p4d[oc, :, 512 * tt:512 * (tt + 1)], p4_t[par][:, :], f"PS{par}")

        S.wait("tensor", "VE")
        pe_tr4 = S.put("tensor", lambda e: e.transpose(
            psF[0:1, 0:128], maxacc[:, 5:6], identf_t[:, :]), "PE")
        S.wait("vector", "PE", pe_tr4)
        ve_m4 = S.put("vector", lambda e: e.tensor_reduce(
            out=z8[0:1, 2:3], in_=psF[0:1, 0:128], axis=AX.X, op=ALU.max), "VE")
        ve_m4b = S.chain("vector", [
            lambda e: e.tensor_scalar(out=rows[0:1, 15:16], in0=rows[0:1, 12:13],
                                      scalar1=rows[0:1, 5:6], scalar2=None, op0=ALU.mult),
            lambda e: e.tensor_scalar(out=z8[0:1, 0:1], in0=z8[0:1, 2:3],
                                      scalar1=rows[0:1, 15:16], scalar2=None, op0=ALU.mult),
        ], "VE")
        S.wait("gpsimd", "VE", ve_m4b)
        S.wait("gpsimd", "PS0", S.cnt["PS0"])
        S.wait("gpsimd", "PS1", S.cnt["PS1"])
        S.dma("gpsimd", ccin[3][:, :], z8[:, :], "GD")
        S.wait("gpsimd", "GD")
        cc4 = S.put("gpsimd", lambda e: e.collective_compute(
            "AllReduce", ALU.max, replica_groups=[list(range(8))],
            ins=[ccin[3]], outs=[ccout[3]]), "CC")
        S.wait("gpsimd", "CC", cc4)
        S.dma("gpsimd", z8[:, :], ccout[3][:, :], "GD")
        S.wait("vector", "GD")
        ve_ar4 = S.chain("vector", [
            lambda e: e.tensor_scalar(out=rows[0:1, 0:1], in0=z8[0:1, 0:1],
                                      scalar1=cst(7)[0:1, :], scalar2=1e-12,
                                      op0=ALU.mult, op1=ALU.max),
            lambda e: e.reciprocal(out=rows[0:1, 2:3], in_=rows[0:1, 0:1]),
            lambda e: e.tensor_scalar(out=rows[0:1, 3:4], in0=rows[0:1, 2:3],
                                      scalar1=rows[0:1, 15:16], scalar2=None, op0=ALU.mult),
        ], "VE")
        last_ve = ve_ar4
        for col, slot in ((5, 3), (6, 0)):
            S.wait("tensor", "VE", last_ve)
            pe_b = S.put("tensor", lambda e, slot=slot: e.matmul(
                psF[:, 0:1], ones1[:, :], rows[0:1, slot:slot + 1],
                start=True, stop=True), "PE")
            S.wait("vector", "PE", pe_b)
            last_ve = S.put("vector", lambda e, col=col: e.tensor_copy(
                out=vecb[:, col:col + 1], in_=psF[:, 0:1]), "VE")
        def _g2(e):
            for s2 in range(NSEQ):
                for d in range(NCH):
                    ins = e.tensor_scalar(out=g2vec[:, 2 * d + s2:2 * d + s2 + 1],
                                          in0=mod_ap(5, d, s2),
                                          scalar1=vecb[:, 6:7], scalar2=None, op0=ALU.mult)
            return ins
        ve_g2 = S.chain("vector", [lambda e: e.engine_nop(), _g2], "VE")

        pe_f5 = 0
        ve_ro = [0, 0]
        ve_x2used = 0
        st_prev = st_attn
        for g in range(NGT):
            s_, gg = divmod(g, NG)
            t0 = gg * G
            gcol = s_ * T + t0
            for oc in range(NOCH):
                par = oc % 2
                S.wait("gpsimd", "VE", ve_ro[par])
                S.dma("gpsimd", p4_t[par][:, :], p4d[oc, :, gcol:gcol + G], f"PL{par}")
                S.wait("sync", "PE", pe_f5)
                S.wait("sync", "WS", ws_marks)
                S.dma("sync", wdt_s[par][:, 0:1024], wdThd[oc, :, :], f"WD{par}")
                S.dma("sync", wdt_s[par][:, 1024:2048], wdTld[oc, :, :], f"WD{par}")
                S.wait("vector", f"PL{par}")
                S.wait("vector", "VE", ve_g2)
                S.wait("vector", "PE", pe_f5)
                def _ro(e, par=par):
                    e.tensor_scalar(out=bufD[:, 0:G], in0=p4_t[par][:, :],
                                    scalar1=vecb[:, 5:6], scalar2=MAGIC,
                                    op0=ALU.mult, op1=ALU.add)
                    e.tensor_scalar(out=bufD[:, 0:G], in0=bufD[:, 0:G],
                                    scalar1=MAGIC, scalar2=None, op0=ALU.subtract)
                    e.tensor_copy(out=roh[par][:, :], in_=bufD[:, 0:G])
                    return e.tensor_tensor(out=rol[par][:, :], in0=bufD[:, 0:G],
                                           in1=roh[par][:, :], op=ALU.subtract)
                ve_ro[par] = S.put("vector", _ro, "VE")
                S.wait("tensor", "VE", ve_ro[par])
                S.wait("tensor", f"WD{par}")
                if oc < 2 and g > 0:
                    S.wait("tensor", "VE", ve_x2used)
                def _dmm(e, par=par, oc=oc):
                    for pi, (lofs, rx) in enumerate(((0, roh), (0, rol), (1024, roh))):
                        for d in range(NCH):
                            ins = e.matmul(
                                psF[:, 512 * d:512 * (d + 1)],
                                wdt_s[par][:, lofs + 128 * d:lofs + 128 * (d + 1)],
                                rx[par][:, :],
                                start=(oc == 0 and pi == 0), stop=(oc == NOCH - 1 and pi == 2))
                    return ins
                pe_f5 = S.put("tensor", _dmm, "PE")
            for d in range(NCH):
                par = d % 2
                S.wait("sync", "VE", ve_xpar[par])
                S.wait("sync", "ST", S.cnt["ST"])
                S.dma("sync", xga[par][:, :],
                      x2d[s_, 128 * d:128 * (d + 1), t0:t0 + G], f"LDW{par}")
                S.wait("vector", "PE", pe_f5)
                S.wait("vector", f"LDW{par}")
                def _yev5(e, s_=s_, d=d, par=par):
                    return e.scalar_tensor_tensor(
                        out=xga[par][:, :], in0=psF[:, 512 * d:512 * (d + 1)],
                        scalar=g2vec[:, 2 * d + s_:2 * d + s_ + 1],
                        in1=xga[par][:, :], op0=ALU.mult, op1=ALU.add)
                ve_xpar[par] = S.put("vector", _yev5, "VE")
                S.wait("sync", "VE", ve_xpar[par])
                S.dma("sync", yTd[s_, 128 * d:128 * (d + 1), t0:t0 + G],
                      xga[par][:, :], "ST")
            ve_x2used = S.cnt["VE"]

        if DEBUG_OUT:
            S.wait("sync", "VE", S.cnt["VE"])
            S.dma("sync", scald[0:1, 0:16], rows[0:1, 0:16], "ST")
        S.wait("sync", "LD", S.cnt["LD"])
        S.wait("sync", "ST", S.cnt["ST"])
        S.wait("sync", "WS", S.cnt["WS"])
        S.wait("gpsimd", "GD", S.cnt["GD"])
        S.wait("gpsimd", "PS0", S.cnt["PS0"])
        S.wait("gpsimd", "PS1", S.cnt["PS1"])
        S.flush(block)
        ffn.close()
        S.check()
    return nc



def prep_core(inputs, c):
    import ml_dtypes
    f32 = np.float32
    x = np.asarray(inputs["L0_attn_x"])[2 * c:2 * c + 2]
    ind2 = np.zeros((128, 2), f32)
    ind2[0:64, 0] = 1.0
    ind2[64:128, 1] = 1.0
    indq = np.zeros((16, 1024), f32)
    for ch in range(8):
        for mm_ in range(128):
            indq[2 * ch + mm_ // 64, ch * 128 + mm_] = 1.0
    ind16 = np.zeros((128, 128), f32)
    for ch in range(8):
        for p in range(128):
            ind16[p, ch * 16 + 2 * ch + p // 64] = 1.0
    perm = PERM
    colperm = np.concatenate([np.arange(64 * h, 64 * h + 64) for h in perm])
    m = {
        "xt": x.transpose(0, 2, 1),
        "wqT": np.asarray(inputs["L0_attn_wq"]).T[:, colperm],
        "wkT": np.asarray(inputs["L0_attn_wk"]).T,
        "wvT": np.asarray(inputs["L0_attn_wv"]).T,
        "woT": np.asarray(inputs["L0_attn_wo"]).T[colperm, :],
        "wup": np.asarray(inputs["L0_ffn_w_up"]),
        "wgate": np.asarray(inputs["L0_ffn_w_gate"]),
        "wdown": np.asarray(inputs["L0_ffn_w_down"]),
        "wmod": np.asarray(inputs["L0_wmod"]),
        "cTd": np.asarray(inputs["L0_c"])[2 * c:2 * c + 2].T,
        "qnwd": np.tile(np.asarray(inputs["L0_attn_q_norm_w"]).reshape(-1), 2)[:, None],
        "knwd": np.tile(np.asarray(inputs["L0_attn_k_norm_w"]).reshape(-1), 2)[:, None],
        "identfd": np.eye(128, dtype=f32),
        "identbd": np.eye(128, dtype=ml_dtypes.bfloat16),
        "ind2d": ind2,
        "p2d": ind2.T,
        "indqd": indq,
        "ind16d": ind16,
    }
    return {k: np.ascontiguousarray(np.asarray(v, dtype=v.dtype if hasattr(v, "dtype") else f32))
            for k, v in m.items()}


_cache = {}


def kernel(**inputs):
    key = (PHASES, DEBUG_OUT)
    if key not in _cache:
        _cache[key] = build_program()
    nc = _cache[key]
    in_maps = [prep_core(inputs, c) for c in range(8)]
    res = run_bass_kernel_spmd(nc, in_maps, core_ids=list(range(8)))
    outs = [res.results[c]["yTd"].transpose(0, 2, 1) for c in range(8)]
    out = np.concatenate(outs, axis=0).astype(np.float32)
    kernel.dbg = res.results
    return out


# version 02
# speedup vs baseline: 1.4006x; 1.4006x over previous
import os
import numpy as np
from contextlib import ExitStack

import concourse.bass as bass
import concourse.mybir as mybir
from concourse.bass_utils import run_bass_kernel_spmd

F32 = mybir.dt.float32
BF16 = mybir.dt.bfloat16
I16 = mybir.dt.int16
ALU = mybir.AluOpType
AF = mybir.ActivationFunctionType
AX = mybir.AxisListType

MAGIC = float(np.float32(1.5 * 2 ** 23))
DEBUG_OUT = False
PHASES = "full"
KSTOP = ""

DM, NCH, T, NSEQ, NG, G, NQT = 1024, 8, 2048, 2, 4, 512, 4
H, KVH, HD, DFF, NOCH = 16, 4, 64, 3072, 24
TOK = NSEQ * T
NGT = NSEQ * NG
PERM = [0, 4, 1, 5, 2, 6, 3, 7, 8, 12, 9, 13, 10, 14, 11, 15]
ENGS = ("sync", "tensor", "vector", "scalar", "gpsimd")
SEM = {}


class Sched:
    def __init__(self):
        self.streams = {k: [] for k in ENGS}
        self.meta = {k: [] for k in ENGS}
        self.cnt = {}

    def newsem(self, name):
        self.cnt[name] = 0

    def wait(self, eng, sem, val=None):
        v = self.cnt[sem] if val is None else val
        if v > 0:
            self.streams[eng].append(lambda e, s=sem, v=v: e.wait_ge(SEM[s], v))
            self.meta[eng].append(("wait", sem, v, eng))
        return v

    def put(self, eng, build, sem=None, inc=1):
        if sem is None:
            self.streams[eng].append(lambda e, b=build: b(e))
            self.meta[eng].append(("op", None, 0, eng))
            return None
        self.cnt[sem] += inc
        v = self.cnt[sem]
        self.streams[eng].append(lambda e, b=build, s=sem, i=inc: b(e).then_inc(SEM[s], i))
        self.meta[eng].append(("inc", sem, v, eng))
        return v

    def chain(self, eng, builds, sem):
        last = None
        for b in builds:
            if last is not None:
                self.wait(eng, sem, last)
            last = self.put(eng, b, sem)
        return last

    def dma(self, eng, out, in_, sem):
        return self.put(eng, lambda e, o=out, i=in_: e.dma_start(out=o, in_=i), sem, 16)

    def flush(self, block):
        for eng in ENGS:
            st = self.streams[eng]
            if st:
                getattr(block, eng)(lambda e, st=list(st): [f(e) for f in st])
                self.streams[eng] = []

    def check(self):
        pos = {k: 0 for k in ENGS}
        cur = {s: 0 for s in self.cnt}
        progress = True
        while progress:
            progress = False
            for eng in ENGS:
                while pos[eng] < len(self.meta[eng]):
                    kind, sem, v, _ = self.meta[eng][pos[eng]]
                    if kind == "wait" and cur[sem] < v:
                        break
                    if kind == "inc":
                        cur[sem] = max(cur[sem], v)
                    pos[eng] += 1
                    progress = True
        stuck = {k: (pos[k], len(self.meta[k])) for k in ENGS if pos[k] < len(self.meta[k])}
        if stuck:
            detail = {k: self.meta[k][pos[k]] for k in stuck}
            raise RuntimeError(f"deadlock: {stuck}; blocked on {detail}; sems {cur}")


def build_program():
    nc = bass.Bass("TRN2", target_bir_lowering=False, detect_race_conditions=False)
    S = Sched()
    for s in ("LD", "LDC", "LDK", "LDW0", "LDW1", "LW0", "LW1", "PS0", "PS1",
              "PL0", "PL1", "WD0", "WD1", "WS", "ST", "GD", "PE", "VE", "SE", "GE", "CC"):
        S.newsem(s)

    ei = lambda n, sh, dt=F32: nc.dram_tensor(n, sh, dt, kind="ExternalInput").ap()
    xt = ei("xt", [NSEQ, DM, T])
    wqT, wkT = ei("wqT", [DM, DM]), ei("wkT", [DM, KVH * HD])
    wvT, woT = ei("wvT", [DM, KVH * HD]), ei("woT", [DM, DM])
    wup, wgate = ei("wup", [DFF, DM]), ei("wgate", [DFF, DM])
    wdown, wmod = ei("wdown", [DM, DFF]), ei("wmod", [DM, 6 * DM])
    cTd = ei("cTd", [DM, NSEQ])
    qnwd, knwd = ei("qnwd", [128, 1]), ei("knwd", [128, 1])
    identfd, identbd = ei("identfd", [128, 128]), ei("identbd", [128, 128], BF16)
    ind2d, p2d, indqd = ei("ind2d", [128, 2]), ei("p2d", [2, 128]), ei("indqd", [16, 1024])
    ind16d = ei("ind16d", [128, 128])

    yTd = nc.dram_tensor("yTd", [NSEQ, DM, T], F32, kind="ExternalOutput").ap()
    dbg = "ExternalOutput" if DEBUG_OUT else "Internal"
    x2d = nc.dram_tensor("x2d", [NSEQ, DM, T], F32, kind=dbg).ap()
    xhd = nc.dram_tensor("xhd", [NSEQ, DM, T], F32, kind=dbg).ap()
    scald = nc.dram_tensor("scald", [1, 16], F32, kind=dbg).ap()
    pud = nc.dram_tensor("pud", [NOCH, 128, TOK], F32, kind="Internal").ap()
    pgd = nc.dram_tensor("pgd", [NOCH, 128, TOK], F32, kind="Internal").ap()
    rud = nc.dram_tensor("rud", [NOCH, 128, TOK], I16, kind="Internal").ap()
    p3d = nc.dram_tensor("p3d", [NOCH, 128, TOK], F32, kind="Internal").ap()
    p4d = nc.dram_tensor("p4d", [NOCH, 128, TOK], F32, kind="Internal").ap()
    wdThd = nc.dram_tensor("wdThd", [NOCH, 128, DM], BF16, kind="Internal").ap()
    wdTld = nc.dram_tensor("wdTld", [NOCH, 128, DM], BF16, kind="Internal").ap()
    mrst_d = nc.dram_tensor("mrst_d", [NSEQ, 2, T], F32, kind="Internal").ap()
    srd = nc.dram_tensor("srd", [16, 128], F32, kind="Internal").ap()
    qdbg = nc.dram_tensor("qdbg", [NCH, 128, G], F32, kind=dbg).ap()
    kdbg = nc.dram_tensor("kdbg", [2, 128, T], F32, kind=dbg).ap()
    ydbg = nc.dram_tensor("ydbg", [128, NCH * G], F32, kind=dbg).ap()
    qsdbg = nc.dram_tensor("qsdbg", [16, G], F32, kind=dbg).ap()
    wodbg = nc.dram_tensor("wodbg", [128, 2048], F32, kind=dbg).ap()
    aodbg = nc.dram_tensor("aodbg", [NCH, 128, G], F32, kind=dbg).ap()
    moddbg1 = nc.dram_tensor("moddbg1", [128, 96], F32, kind=dbg).ap()
    moddbg2 = nc.dram_tensor("moddbg2", [128, 96], F32, kind=dbg).ap()
    ydbg2 = nc.dram_tensor("ydbg2", [128, NCH * G], F32, kind=dbg).ap()
    ccin = [nc.dram_tensor(f"cc{i}in", [1, 8], F32).ap() for i in range(4)]
    ccout = [nc.dram_tensor(f"cc{i}out", [1, 8], F32).ap() for i in range(4)]

    es = ExitStack()
    with es:
        for name in S.cnt:
            SEM[name] = es.enter_context(nc.semaphore(name))
        block = es.enter_context(nc.Block())
        sbuf = lambda n, sh, dt=F32: es.enter_context(nc.sbuf_tensor(n, sh, dt))

        ones128 = sbuf("ones128", [128, 1])
        ones1 = sbuf("ones1", [1, 128])
        csts = sbuf("csts", [128, 8])
        qnw_t, knw_t = sbuf("qnw_t", [128, 1]), sbuf("knw_t", [128, 1])
        identf_t = sbuf("identf_t", [128, 128])
        identb_t = sbuf("identb_t", [128, 128], BF16)
        ind2_t, p2_t = sbuf("ind2_t", [128, 2]), sbuf("p2_t", [2, 128])
        modT = sbuf("modT", [128, 96])
        siluc, ctile = sbuf("siluc", [128, 16]), sbuf("ctile", [128, 16])
        rows = sbuf("rows", [1, 16])
        z8 = sbuf("z8", [1, 8])
        vecb = sbuf("vecb", [128, 8])
        avec, bvec = sbuf("avec", [128, 16]), sbuf("bvec", [128, 16])
        g2vec = sbuf("g2vec", [128, 16])
        maxacc = sbuf("maxacc", [128, 8])
        bufD = sbuf("bufD", [128, 2 * G])
        att = ExitStack()
        sba = lambda n, sh, dt=F32: att.enter_context(nc.sbuf_tensor(n, sh, dt))
        indq_t = sba("indq_t", [16, 1024])
        ind16_t = sba("ind16_t", [128, 128])
        hl_lo, hl_hi = sba("hl_lo", [1, 128]), sba("hl_hi", [1, 128])
        mask_diag, mask_far = sba("mask_diag", [128, 128]), sba("mask_far", [128, 128])
        mrow, vrow, rrow = sba("mrow", [1, G]), sba("vrow", [1, G]), sba("rrow", [1, G])

        cst = lambda i: csts[:, i:i + 1]

        for dst, src in ((qnw_t, qnwd), (knw_t, knwd), (identf_t, identfd),
                         (identb_t, identbd), (ind2_t, ind2d), (p2_t, p2d),
                         (indq_t, indqd), (ind16_t, ind16d)):
            S.dma("sync", dst[:, :], src[:, :], "LDK")
        S.dma("sync", ctile[:, :].rearrange("p (a s) -> p a s", a=NCH),
              cTd.rearrange("(a p) s -> p a s", p=128), "LDK")
        ld_consts = S.cnt["LDK"]

        def _consts(e):
            e.memset(ones128[:, :], 1.0)
            e.memset(ones1[:, :], 1.0)
            for i, v in enumerate((0.125, 1e-5, 1e-6, 1.0 / 64.0, -0.5, 1.0 / 1024.0,
                                   float(np.float32(1.0 / 7.0)),
                                   float(np.float32(1.0 / 32767.0)))):
                e.memset(csts[:, i:i + 1], v)
            e.memset(mask_diag[:, :], 1.0)
            e.memset(mask_far[:, :], 1.0)
            e.memset(z8[:, :], 0.0)
            e.memset(maxacc[:, :], 0.0)
            e.memset(hl_lo[0:1, 0:64], 1.0)
            e.memset(hl_lo[0:1, 64:128], 0.0)
            e.memset(hl_hi[0:1, 0:64], 0.0)
            e.memset(hl_hi[0:1, 64:128], 1.0)
            return e.engine_nop()
        ve_consts = S.put("vector", _consts, "VE")

        S.wait("gpsimd", "VE", ve_consts)
        S.put("gpsimd", lambda e: e.affine_select(
            mask_diag[:, :], mask_diag[:, :], pattern=[[1, 128]],
            compare_op=ALU.is_ge, fill=0.0, channel_multiplier=-1), "GE")
        ge_masks = S.put("gpsimd", lambda e: e.affine_select(
            mask_far[:, :], mask_far[:, :], pattern=[[-1, 128]],
            compare_op=ALU.is_ge, fill=0.0, channel_multiplier=1, base=-1), "GE")

        S.wait("scalar", "LDK", ld_consts)
        se_sig = S.put("scalar", lambda e: e.activation(
            out=siluc[:, :], in_=ctile[:, :], func=AF.Sigmoid), "SE")
        S.wait("vector", "SE", se_sig)
        S.wait("vector", "LDK", ld_consts)
        ve_silu = S.put("vector", lambda e: e.tensor_mul(
            siluc[:, :], siluc[:, :], ctile[:, :]), "VE")

        ph0 = ExitStack()
        wmt = ph0.enter_context(nc.sbuf_tensor("wmt", [128, NCH * 128], F32))
        ps_mod = ph0.enter_context(nc.psum_tensor("ps_mod", [128, 2], F32))
        for oc in range(48):
            if oc > 0:
                S.wait("sync", "PE", S.cnt["PE"])
            for d in range(NCH):
                S.dma("sync", wmt[:, 128 * d:128 * (d + 1)],
                      wmod[128 * d:128 * (d + 1), 128 * oc:128 * (oc + 1)], "LDC")
            S.wait("tensor", "LDC")
            S.wait("tensor", "VE", ve_silu)
            if oc > 0:
                S.wait("tensor", "VE")
            def _mm(e, oc=oc):
                for d in range(NCH):
                    ins = e.matmul(ps_mod[:, :], wmt[:, 128 * d:128 * (d + 1)],
                                   siluc[:, 2 * d:2 * d + 2],
                                   start=(d == 0), stop=(d == NCH - 1))
                return ins
            S.put("tensor", _mm, "PE")
            S.wait("vector", "PE")
            S.put("vector", lambda e, oc=oc: e.tensor_copy(
                out=modT[:, 2 * oc:2 * oc + 2], in_=ps_mod[:, :]), "VE")

        ops = [lambda e: e.tensor_scalar(out=modT[:, 16:32], in0=modT[:, 16:32],
                                         scalar1=1.0, scalar2=None, op0=ALU.add),
               lambda e: e.tensor_scalar(out=modT[:, 64:80], in0=modT[:, 64:80],
                                         scalar1=1.0, scalar2=None, op0=ALU.add)]
        for c0 in (32, 80):
            sl = modT[:, c0:c0 + 16]
            t, m, m2 = g2vec[:, 0:16], avec[:, 0:16], bvec[:, 0:16]
            ops += [
                lambda e, sl=sl, t=t: e.tensor_scalar(out=t, in0=sl, scalar1=-1.0,
                                                      scalar2=None, op0=ALU.mult),
                lambda e, sl=sl, t=t: e.tensor_tensor(out=t, in0=sl, in1=t, op=ALU.max),
                lambda e, t=t: e.tensor_scalar(out=t, in0=t, scalar1=0.25,
                                               scalar2=None, op0=ALU.max),
                lambda e, sl=sl, m=m: e.tensor_scalar(out=m, in0=sl, scalar1=0.0,
                                                      scalar2=None, op0=ALU.is_ge),
                lambda e, m=m, m2=m2: e.tensor_scalar(out=m2, in0=m, scalar1=2.0,
                                                      scalar2=-1.0, op0=ALU.mult, op1=ALU.add),
                lambda e, sl=sl, t=t, m2=m2: e.tensor_mul(sl, t, m2),
            ]
        ve_mod = S.chain("vector", ops, "VE")
        if DEBUG_OUT:
            S.wait("sync", "VE", ve_mod)
            S.dma("sync", moddbg1[:, :], modT[:, :], "ST")
        S.flush(block)
        ph0.close()
        if KSTOP == "mod":
            S.wait("sync", "LDK", S.cnt["LDK"])
            S.wait("sync", "LDC", S.cnt["LDC"])
            S.wait("sync", "VE", S.cnt["VE"])
            S.flush(block)
            S.check()
            return nc

        def mod_ap(p, ch, s):
            c = (8 * p + ch) * 2 + s
            return modT[:, c:c + 1]

        bufA = sba("bufA", [128, NCH * G])
        bufB = sba("bufB", [128, NCH * G])
        bufC = sba("bufC", [128, NCH * G])
        ysb = sba("ysb", [128, NCH * G])
        kvp = [sba(f"kvp{i}", [128, T]) for i in range(2)]
        qp = [sba(f"qp{i}", [128, G]) for i in range(NCH)]
        vext = sba("vext", [128, 16 * 260])
        attnT = sba("attnT", [128, 640])
        qssr = sba("qssr", [16, G])
        srow16 = sba("srow16", [16, 128])
        recip16 = sba("recip16", [16, 128])
        sr65 = sba("sr65", [65, 2048])
        wk_s = sba("wk_s", [128, NCH * 256])
        wv_s = sba("wv_s", [128, NCH * 256])
        wq_s = sba("wq_s", [128, NCH * 1024])
        wo_g = sba("wo_g", [128, 2 * 1024])
        ps_sc = att.enter_context(nc.psum_tensor("ps_sc", [128, 640], F32))
        ps_y = att.enter_context(nc.psum_tensor("ps_y", [128, 2048], F32))
        ps_a = att.enter_context(nc.psum_tensor("ps_a", [128, 512], F32))
        ps_b = att.enter_context(nc.psum_tensor("ps_b", [128, 512], F32))

        Atile = lambda d: bufA[:, G * d:G * (d + 1)]
        Btile = lambda d: bufB[:, G * d:G * (d + 1)]
        Ctile = lambda d: bufC[:, G * d:G * (d + 1)]

        S.wait("sync", "PE", S.cnt["PE"])
        for d in range(NCH):
            S.dma("sync", wk_s[:, 256 * d:256 * (d + 1)], wkT[128 * d:128 * (d + 1), :], "LDC")
            S.dma("sync", wv_s[:, 256 * d:256 * (d + 1)], wvT[128 * d:128 * (d + 1), :], "LDC")
            S.dma("sync", wq_s[:, 1024 * d:1024 * (d + 1)], wqT[128 * d:128 * (d + 1), :], "LDC")
        ld_attw = S.cnt["LDC"]
        S.wait("vector", "VE", ve_mod)
        ve_vext1 = S.put("vector", lambda e: e.memset(
            vext[:, :].rearrange("p (t k c) -> p t k c", t=16, k=4)[:, :, :, 64:65], 1.0), "VE")

        def ln_stats(src_tiles, wait_ve=None):
            pe_mm = 0
            for d in range(NCH):
                half = bufD[:, G * (d % 2):G * (d % 2) + G]
                if wait_ve is not None and d == 0:
                    S.wait("scalar", "VE", wait_ve)
                if d >= 2:
                    S.wait("scalar", "PE", pe_mm_prev)
                se_sq = S.put("scalar", lambda e, d=d, half=half: e.activation(
                    out=half, in_=src_tiles[d], func=AF.Square), "SE")
                S.wait("tensor", "SE", se_sq)
                def _mm(e, d=d, half=half):
                    e.matmul(ps_sc[0:1, 0:G], ones128[:, :], src_tiles[d],
                             start=(d == 0), stop=(d == NCH - 1))
                    return e.matmul(ps_y[0:1, 0:G], ones128[:, :], half,
                                    start=(d == 0), stop=(d == NCH - 1))
                pe_mm_prev = pe_mm
                pe_mm = S.put("tensor", _mm, "PE")
            S.wait("vector", "PE", pe_mm)
            S.wait("vector", "GD")
            def _rows(e):
                e.tensor_scalar(out=mrow[:, :], in0=ps_sc[0:1, 0:G],
                                scalar1=cst(5)[0:1, :], scalar2=None, op0=ALU.mult)
                e.tensor_scalar(out=vrow[:, :], in0=ps_y[0:1, 0:G],
                                scalar1=cst(5)[0:1, :], scalar2=None, op0=ALU.mult)
                e.tensor_mul(rrow[:, :], mrow[:, :], mrow[:, :])
                return e.tensor_sub(vrow[:, :], vrow[:, :], rrow[:, :])
            ve_rows = S.put("vector", _rows, "VE")
            S.wait("scalar", "VE", ve_rows)
            def _rst(e):
                e.activation(out=rrow[:, :], in_=vrow[:, :], func=AF.Ln,
                             scale=1.0, bias=cst(1)[0:1, :])
                return e.activation(out=rrow[:, :], in_=rrow[:, :], func=AF.Exp,
                                    scale=cst(4)[0:1, :])
            return S.put("scalar", _rst, "SE"), ve_rows

        def bcast_mr(m_ap, r_ap, wait_se=None, wait_ve=None):
            if wait_se is not None:
                S.wait("tensor", "SE", wait_se)
            if wait_ve is not None:
                S.wait("tensor", "VE", wait_ve)
            def _bc(e):
                e.matmul(ps_sc[:, 0:G], ones1[:, :], m_ap, start=True, stop=True)
                return e.matmul(ps_y[:, 0:G], ones1[:, :], r_ap, start=True, stop=True)
            return S.put("tensor", _bc, "PE")

        def h1_build(x_tiles, out_tiles, s, pe_bc):
            S.wait("vector", "PE", pe_bc)
            def _h1(e, s=s):
                for d in range(NCH):
                    e.tensor_sub(out_tiles[d], x_tiles[d], ps_sc[:, 0:G])
                    e.tensor_mul(out_tiles[d], out_tiles[d], ps_y[:, 0:G])
                    ins = e.tensor_scalar(out=out_tiles[d], in0=out_tiles[d],
                                          scalar1=mod_ap(1, d, s), scalar2=mod_ap(0, d, s),
                                          op0=ALU.mult, op1=ALU.add)
                return ins
            return S.put("vector", _h1, "VE")

        prev = {"VE": 0, "PE": 0}

        A2STOPS = ('a2g1', 'a2q', 'a2att', 'a2sc', 'a2av', 'a2sr')
        seqs = [0] if KSTOP in ('a1', 'a2') or KSTOP in A2STOPS else list(range(NSEQ))
        for s in seqs:
            for g in range(NG):
                t0 = g * G
                S.wait("sync", "VE", prev["VE"])
                S.wait("sync", "PE", prev["PE"])
                for d in range(NCH):
                    S.dma("sync", Atile(d), xt[s, 128 * d:128 * (d + 1), t0:t0 + G], "LD")
                S.wait("scalar", "LD")
                S.wait("tensor", "LDC", ld_attw)
                S.wait("tensor", "LD")
                se_r, ve_rows = ln_stats([Atile(d) for d in range(NCH)],
                                         wait_ve=prev["VE"])
                S.wait("gpsimd", "SE", se_r)
                S.wait("gpsimd", "VE", ve_rows)
                S.dma("gpsimd", mrst_d[s, 0:1, t0:t0 + G], mrow[:, :], "GD")
                S.dma("gpsimd", mrst_d[s, 1:2, t0:t0 + G], rrow[:, :], "GD")
                pe_bc = bcast_mr(mrow[:, :], rrow[:, :], wait_se=se_r, wait_ve=ve_rows)
                ve_h1 = h1_build([Atile(d) for d in range(NCH)],
                                 [Btile(d) for d in range(NCH)], s, pe_bc)

                for kv in range(2):
                    S.wait("tensor", "VE", ve_h1)
                    if kv == 1:
                        S.wait("tensor", "VE")
                    def _kmm(e, kv=kv):
                        for d in range(NCH):
                            ins = e.matmul(ps_a[:, 0:G],
                                           wk_s[:, 256 * d + 128 * kv:256 * d + 128 * (kv + 1)],
                                           Btile(d), start=(d == 0), stop=(d == NCH - 1))
                        return ins
                    pe_k = S.put("tensor", _kmm, "PE")
                    S.wait("vector", "PE", pe_k)
                    ve_kev = S.put("vector", lambda e: e.tensor_copy(
                        out=bufD[:, 0:G], in_=ps_a[:, 0:G]), "VE")
                    S.wait("scalar", "VE", ve_kev)
                    se_ksq = S.put("scalar", lambda e: e.activation(
                        out=bufD[:, G:2 * G], in_=bufD[:, 0:G], func=AF.Square), "SE")
                    S.wait("tensor", "SE", se_ksq)
                    pe_kss = S.put("tensor", lambda e: e.matmul(
                        ps_b[0:2, 0:G], ind2_t[:, :], bufD[:, G:2 * G],
                        start=True, stop=True), "PE")
                    S.wait("vector", "PE", pe_kss)
                    ve_kssr = S.put("vector", lambda e: e.tensor_copy(
                        out=qssr[0:2, :], in_=ps_b[0:2, 0:G]), "VE")
                    S.wait("scalar", "VE", ve_kssr)
                    def _krst(e):
                        e.activation(out=qssr[0:2, :], in_=qssr[0:2, :], func=AF.Ln,
                                     scale=cst(3)[0:2, :], bias=cst(2)[0:2, :])
                        return e.activation(out=qssr[0:2, :], in_=qssr[0:2, :],
                                            func=AF.Exp, scale=cst(4)[0:2, :])
                    se_krst = S.put("scalar", _krst, "SE")
                    S.wait("tensor", "SE", se_krst)
                    pe_kbc = S.put("tensor", lambda e: e.matmul(
                        ps_b[:, 0:G], p2_t[:, :], qssr[0:2, :], start=True, stop=True), "PE")
                    S.wait("vector", "PE", pe_kbc)
                    def _knrm(e, kv=kv, t0=t0):
                        e.tensor_mul(bufD[:, 0:G], bufD[:, 0:G], ps_b[:, 0:G])
                        return e.tensor_scalar(
                            out=kvp[kv][:, t0:t0 + G], in0=bufD[:, 0:G],
                            scalar1=knw_t[:, :], scalar2=None, op0=ALU.mult)
                    S.put("vector", _knrm, "VE")

                for ts_ in range(4):
                    ti = g * 4 + ts_
                    S.wait("tensor", "VE")
                    def _vmm(e, ts_=ts_):
                        for d in range(NCH):
                            ins = e.matmul(ps_a[:, 0:256],
                                           Btile(d)[:, 128 * ts_:128 * (ts_ + 1)],
                                           wv_s[:, 256 * d:256 * (d + 1)],
                                           start=(d == 0), stop=(d == NCH - 1))
                        return ins
                    pe_v = S.put("tensor", _vmm, "PE")
                    S.wait("vector", "PE", pe_v)
                    S.wait("vector", "VE", ve_vext1)
                    def _vev(e, ti=ti):
                        dst = vext[:, 260 * ti:260 * (ti + 1)].rearrange(
                            "p (k c) -> p k c", k=4)[:, :, 0:64]
                        src = ps_a[:, 0:256].rearrange("p (k c) -> p k c", k=4)
                        return e.tensor_copy(out=dst, in_=src)
                    S.put("vector", _vev, "VE")
                prev = {"VE": S.cnt["VE"], "PE": S.cnt["PE"]}

            if KSTOP == "a1":
                break
            for g in [0] if KSTOP in A2STOPS else range(NG):
                t0 = g * G
                S.wait("sync", "VE", prev["VE"])
                S.wait("sync", "PE", prev["PE"])
                for d in range(NCH):
                    S.dma("sync", Atile(d), xt[s, 128 * d:128 * (d + 1), t0:t0 + G], "LD")
                S.wait("sync", "GD")
                S.wait("sync", "PE")
                S.dma("sync", mrow[:, :], mrst_d[s, 0:1, t0:t0 + G], "LD")
                S.dma("sync", rrow[:, :], mrst_d[s, 1:2, t0:t0 + G], "LD")
                for yc in range(NCH):
                    S.dma("sync", wo_g[:, 128 * yc:128 * (yc + 1)],
                          woT[128 * yc:128 * (yc + 1), 0:128], "LDW0")
                S.wait("tensor", "LD")
                pe_bc = bcast_mr(mrow[:, :], rrow[:, :], wait_ve=prev["VE"])
                ve_h1 = h1_build([Atile(d) for d in range(NCH)],
                                 [Btile(d) for d in range(NCH)], s, pe_bc)

                for qc in range(NCH):
                    S.wait("tensor", "VE", ve_h1)
                    if qc > 0:
                        S.wait("tensor", "VE")
                    def _qmm(e, qc=qc):
                        for d in range(NCH):
                            ins = e.matmul(ps_a[:, 0:G],
                                           wq_s[:, 1024 * d + 128 * qc:1024 * d + 128 * (qc + 1)],
                                           Btile(d), start=(d == 0), stop=(d == NCH - 1))
                        return ins
                    pe_q = S.put("tensor", _qmm, "PE")
                    S.wait("vector", "PE", pe_q)
                    ve_qev = S.put("vector", lambda e, qc=qc: e.tensor_copy(
                        out=Ctile(qc), in_=ps_a[:, 0:G]), "VE")
                    S.wait("scalar", "VE", ve_qev)
                    se_qsq = S.put("scalar", lambda e, qc=qc: e.activation(
                        out=bufD[:, 0:G], in_=Ctile(qc), func=AF.Square), "SE")
                    S.wait("tensor", "SE", se_qsq)
                    pe_qss = S.put("tensor", lambda e, qc=qc: e.matmul(
                        ps_b[0:16, 0:G], ind16_t[:, 16 * qc:16 * (qc + 1)], bufD[:, 0:G],
                        start=(qc == 0), stop=(qc == NCH - 1)), "PE")
                S.wait("vector", "PE", pe_qss)
                ve_qssr = S.put("vector", lambda e: e.tensor_copy(
                    out=qssr[:, :], in_=ps_b[0:16, 0:G]), "VE")
                S.wait("scalar", "VE")
                def _qrst(e):
                    e.activation(out=qssr[:, :], in_=qssr[:, :], func=AF.Ln,
                                 scale=cst(3)[0:16, :], bias=cst(2)[0:16, :])
                    return e.activation(out=qssr[:, :], in_=qssr[:, :],
                                        func=AF.Exp, scale=cst(4)[0:16, :])
                se_qrst = S.put("scalar", _qrst, "SE")
                for qc in range(NCH):
                    S.wait("tensor", "SE", se_qrst)
                    S.wait("tensor", "VE")
                    pe_qbc = S.put("tensor", lambda e, qc=qc: e.matmul(
                        ps_b[:, 0:G], indq_t[:, 128 * qc:128 * (qc + 1)],
                        qssr[:, :], start=True, stop=True), "PE")
                    S.wait("vector", "PE", pe_qbc)
                    def _qn(e, qc=qc):
                        e.tensor_mul(bufD[:, 0:G], Ctile(qc)[:, :], ps_b[:, 0:G])
                        return e.tensor_scalar(
                            out=qp[qc][:, :], in0=bufD[:, 0:G],
                            scalar1=qnw_t[:, :], scalar2=None, op0=ALU.mult)
                    S.put("vector", _qn, "VE")
                ve_qn = S.cnt["VE"]
                S.wait("vector", "GE", ge_masks)
                if KSTOP == "a2q":
                    continue

                for qt in range(NQT):
                    qi = g * NQT + qt
                    kts = list(range(max(0, qi - 4), qi + 1))
                    n = len(kts)
                    se_exp_prev = None
                    ve_mask_prev = None
                    pe_av_last = None
                    for hi in range(H + 1):
                        if hi < H:
                            r = hi
                            h = PERM[r]
                            S.wait("tensor", "VE", ve_qn)
                            if se_exp_prev is not None:
                                S.wait("tensor", "SE", se_exp_prev)
                            if hi == 0:
                                S.wait("tensor", "VE")
                                S.wait("tensor", "SE")
                            def _sc(e, h=h, r=r, qt=qt, kts=kts):
                                ko = 64 * ((h // 4) % 2)
                                for j, kt in enumerate(kts):
                                    ins = e.matmul(
                                        ps_sc[:, 128 * j:128 * (j + 1)],
                                        kvp[(h // 4) // 2][ko:ko + 64, 128 * kt:128 * (kt + 1)],
                                        qp[r // 2][64 * (r % 2):64 * (r % 2) + 64,
                                                   128 * qt:128 * (qt + 1)],
                                        start=True, stop=True)
                                return ins
                            pe_sc = S.put("tensor", _sc, "PE")
                        if hi > 0 and KSTOP != "a2sc":
                            rp = hi - 1
                            hpv = PERM[rp]
                            S.wait("tensor", "VE", ve_mask_prev)
                            def _av(e, rp=rp, hpv=hpv, kts=kts, n=n):
                                for j, kt in enumerate(kts):
                                    v0 = 260 * kt + 65 * (hpv // 4)
                                    ins = e.matmul(
                                        ps_y[0:65, 128 * rp:128 * (rp + 1)],
                                        vext[:, v0:v0 + 65],
                                        attnT[:, 128 * j:128 * (j + 1)],
                                        start=(j == 0), stop=(j == n - 1))
                                return ins
                            pe_av_last = S.put("tensor", _av, "PE")
                        if hi < H:
                            S.wait("scalar", "PE", pe_av_last if pe_av_last is not None else pe_sc)
                            se_exp_prev = S.put("scalar", lambda e, n=n: e.activation(
                                out=attnT[:, 0:128 * n], in_=ps_sc[:, 0:128 * n],
                                func=AF.Exp, scale=cst(0)), "SE")
                            S.wait("vector", "SE", se_exp_prev)
                            def _msk(e, n=n, qi=qi):
                                ins = e.tensor_mul(attnT[:, 128 * (n - 1):128 * n],
                                                   attnT[:, 128 * (n - 1):128 * n],
                                                   mask_diag[:, :])
                                if qi >= 4:
                                    ins = e.tensor_mul(attnT[:, 0:128],
                                                       attnT[:, 0:128], mask_far[:, :])
                                return ins
                            ve_mask_prev = S.put("vector", _msk, "VE")

                    if KSTOP in ("a2sc", "a2av"):
                        continue
                    S.wait("scalar", "PE", pe_av_last)
                    S.wait("scalar", "GD")
                    se_sr = S.put("scalar", lambda e: e.copy(
                        out=sr65[:, :], in_=ps_y[0:65, 0:2048]), "SE")
                    S.wait("gpsimd", "SE", se_sr)
                    S.dma("gpsimd", srd.rearrange("r q -> (r q)").rearrange("(p f) -> p f", p=1),
                          sr65[64:65, 0:2048], "GD")
                    S.wait("gpsimd", "GD")
                    S.dma("gpsimd", srow16[:, :], srd[:, :], "GD")
                    if KSTOP == "a2sr":
                        continue
                    def _yev(e, qt=qt):
                        for r in range(0, H, 2):
                            dst = ysb[0:64, 512 * (r // 2) + 128 * qt:512 * (r // 2) + 128 * (qt + 1)]
                            ins = e.tensor_copy(out=dst, in_=sr65[0:64, 128 * r:128 * (r + 1)])
                        return ins
                    S.wait("vector", "SE", se_sr)
                    se_yev = S.put("vector", _yev, "VE")
                    for r in range(1, H, 2):
                        S.dma("gpsimd",
                              ysb[64:128, 512 * (r // 2) + 128 * qt:512 * (r // 2) + 128 * (qt + 1)],
                              sr65[0:64, 128 * r:128 * (r + 1)], "GD")
                    gd_yev = S.cnt["GD"]
                    S.wait("vector", "GD", gd_yev)
                    ve_rec = S.put("vector", lambda e: e.reciprocal(
                        out=recip16[:, :], in_=srow16[:, :]), "VE")
                    for half in range(2):
                        S.wait("tensor", "VE", ve_rec)
                        if half == 0:
                            S.wait("tensor", "VE", se_yev)
                        else:
                            S.wait("tensor", "VE")
                        def _rb(e, half=half):
                            for qc in range(4 * half, 4 * half + 4):
                                ins = e.matmul(ps_b[:, 128 * (qc % 4):128 * (qc % 4 + 1)],
                                               indq_t[:, 128 * qc:128 * (qc + 1)],
                                               recip16[:, :], start=True, stop=True)
                            return ins
                        pe_rb = S.put("tensor", _rb, "PE")
                        S.wait("vector", "PE", pe_rb)
                        if half == 0:
                            S.wait("vector", "GD", gd_yev)
                        def _nrm(e, half=half, qt=qt):
                            for qc in range(4 * half, 4 * half + 4):
                                sl = ysb[:, 512 * qc + 128 * qt:512 * qc + 128 * (qt + 1)]
                                ins = e.tensor_mul(sl, sl, ps_b[:, 128 * (qc % 4):128 * (qc % 4 + 1)])
                            return ins
                        S.put("vector", _nrm, "VE")

                if KSTOP in ("a2att", "a2sc", "a2av", "a2sr"):
                    continue
                if DEBUG_OUT and s == 0 and g == 0:
                    S.wait("sync", "VE", S.cnt["VE"])
                    S.dma("sync", qsdbg[:, :], qssr[:, :], "ST")
                    for qc in range(NCH):
                        S.dma("sync", qdbg[qc], qp[qc][:, :], "ST")
                    for i in range(2):
                        S.dma("sync", kdbg[i], kvp[i][:, :], "ST")
                    S.dma("sync", ydbg[:, :], ysb[:, :], "ST")
                ve_norm_all = S.cnt["VE"]
                pe_o = None
                for dd in range(NCH):
                    if dd > 0:
                        S.wait("sync", "PE", pe_o)
                        for yc in range(NCH):
                            S.dma("sync", wo_g[:, 128 * yc:128 * (yc + 1)],
                                  woT[128 * yc:128 * (yc + 1), 128 * dd:128 * (dd + 1)], "LDW0")
                    S.wait("tensor", "VE", ve_norm_all)
                    if dd > 0:
                        S.wait("tensor", "VE")
                    S.wait("tensor", "LDW0", S.cnt["LDW0"])
                    def _omm(e, dd=dd):
                        for yc in range(NCH):
                            ins = e.matmul(ps_a[:, 0:G],
                                           wo_g[:, 128 * yc:128 * (yc + 1)],
                                           ysb[:, 512 * yc:512 * (yc + 1)],
                                           start=(yc == 0), stop=(yc == NCH - 1))
                        return ins
                    pe_o = S.put("tensor", _omm, "PE")
                    S.wait("vector", "PE", pe_o)
                    def _x2(e, d=dd, s=s):
                        return e.scalar_tensor_tensor(
                            out=Btile(d), in0=ps_a[:, 0:G], scalar=mod_ap(2, d, s),
                            in1=Atile(d), op0=ALU.mult, op1=ALU.add)
                    ve_x2 = S.put("vector", _x2, "VE")
                    S.wait("sync", "VE", ve_x2)
                    S.dma("sync", x2d[s, 128 * dd:128 * (dd + 1), t0:t0 + G], Btile(dd), "ST")

                se_r2, _ = ln_stats([Btile(d) for d in range(NCH)], wait_ve=S.cnt["VE"])
                pe_bc2 = bcast_mr(mrow[:, :], rrow[:, :], wait_se=se_r2)
                S.wait("vector", "PE", pe_bc2)
                def _xh(e, s=s):
                    for d in range(NCH):
                        e.tensor_sub(Ctile(d), Btile(d), ps_sc[:, 0:G])
                        e.tensor_mul(Ctile(d), Ctile(d), ps_y[:, 0:G])
                        e.tensor_scalar(out=bufD[:, 0:G], in0=Ctile(d),
                                        scalar1=mod_ap(4, d, s), scalar2=mod_ap(3, d, s),
                                        op0=ALU.mult, op1=ALU.add)
                        e.tensor_reduce(out=maxacc[:, 1:2], in_=bufD[:, 0:G],
                                        axis=AX.X, op=ALU.max, apply_absolute_value=True)
                        ins = e.tensor_tensor(out=maxacc[:, 0:1], in0=maxacc[:, 0:1],
                                              in1=maxacc[:, 1:2], op=ALU.max)
                    return ins
                ve_xh = S.put("vector", _xh, "VE")
                S.wait("sync", "VE", ve_xh)
                for d in range(NCH):
                    S.dma("sync", xhd[s, 128 * d:128 * (d + 1), t0:t0 + G], Ctile(d), "ST")
                prev = {"VE": S.cnt["VE"], "PE": S.cnt["PE"]}

        if KSTOP in ("a1", "a2", "a2q") or KSTOP in A2STOPS:
            S.wait("sync", "LD", S.cnt["LD"])
            S.wait("sync", "LDC", S.cnt["LDC"])
            S.wait("sync", "ST", S.cnt["ST"])
            S.wait("gpsimd", "GD", S.cnt["GD"])
            S.wait("sync", "VE", S.cnt["VE"])
            S.wait("sync", "PE", S.cnt["PE"])
            S.wait("sync", "SE", S.cnt["SE"])
            S.flush(block)
            att.close()
            S.check()
            return nc
        S.wait("tensor", "VE")
        pe_tr = S.put("tensor", lambda e: e.transpose(
            ps_b[0:1, 0:128], maxacc[:, 0:1], identf_t[:, :]), "PE")
        S.wait("vector", "PE", pe_tr)
        ve_mh = S.put("vector", lambda e: e.tensor_reduce(
            out=z8[0:1, 0:1], in_=ps_b[0:1, 0:128], axis=AX.X, op=ALU.max), "VE")
        S.wait("gpsimd", "VE", ve_mh)
        S.dma("gpsimd", ccin[0][:, :], z8[:, :], "GD")
        S.wait("gpsimd", "GD")
        cc1 = S.put("gpsimd", lambda e: e.collective_compute(
            "AllReduce", ALU.max, replica_groups=[list(range(8))],
            ins=[ccin[0]], outs=[ccout[0]]), "CC")
        S.wait("gpsimd", "CC", cc1)
        S.dma("gpsimd", z8[:, :], ccout[0][:, :], "GD")
        S.wait("vector", "GD")
        ve_sx = S.chain("vector", [
            lambda e: e.tensor_scalar(out=rows[0:1, 1:2], in0=z8[0:1, 0:1],
                                      scalar1=cst(7)[0:1, :], scalar2=1e-12,
                                      op0=ALU.mult, op1=ALU.max),
            lambda e: e.reciprocal(out=rows[0:1, 2:3], in_=rows[0:1, 1:2]),
        ], "VE")
        S.wait("tensor", "VE", ve_sx)
        pe_sxb = S.put("tensor", lambda e: e.matmul(
            ps_b[:, 0:1], ones1[:, :], rows[0:1, 2:3], start=True, stop=True), "PE")
        S.wait("vector", "PE", pe_sxb)
        def _avec(e):
            for s in range(NSEQ):
                for d in range(NCH):
                    e.tensor_scalar(out=avec[:, 2 * d + s:2 * d + s + 1],
                                    in0=mod_ap(4, d, s),
                                    scalar1=vecb[:, 0:1], scalar2=None, op0=ALU.mult)
                    ins = e.tensor_scalar(out=bvec[:, 2 * d + s:2 * d + s + 1],
                                          in0=mod_ap(3, d, s),
                                          scalar1=vecb[:, 0:1], scalar2=None, op0=ALU.mult)
            return ins
        ve_avec = S.chain("vector", [
            lambda e: e.tensor_copy(out=vecb[:, 0:1], in_=ps_b[:, 0:1]),
            _avec,
        ], "VE")
        if DEBUG_OUT:
            S.wait("sync", "VE", ve_avec)
            S.dma("sync", scald[0:1, 0:8], z8[:, :], "ST")
            S.dma("sync", moddbg2[:, :], modT[:, :], "ST")

        S.flush(block)
        att.close()

        if PHASES == "attn":
            S.wait("sync", "ST", S.cnt["ST"])
            for s in range(NSEQ):
                S.dma("sync", yTd[s], x2d[s], "ST")
            S.wait("sync", "LD", S.cnt["LD"])
            S.wait("sync", "ST", S.cnt["ST"])
            S.wait("gpsimd", "GD", S.cnt["GD"])
            S.flush(block)
            S.check()
            return nc


        ffn = ExitStack()
        sbf = lambda n, sh, dt=F32: ffn.enter_context(nc.sbuf_tensor(n, sh, dt))
        xga = [sbf(f"xga{i}", [128, G]) for i in range(2)]
        wr = [sbf(f"wr{i}", [128, 1024]) for i in range(2)]
        wdq = bufD
        wh_n = sbf("wh_n", [128, 1024], BF16)
        wl_n = sbf("wl_n", [128, 1024], BF16)
        lhsTh = sbf("lhsTh", [128, 2 * 1024], BF16)
        lhsTl = sbf("lhsTl", [128, 2 * 1024], BF16)
        pstg = [sbf(f"pstg{i}", [128, G]) for i in range(2)]
        p4_t = [sbf(f"p4_t{i}", [128, G]) for i in range(2)]
        roh = [sbf(f"roh{i}", [128, G], BF16) for i in range(2)]
        rol = [sbf(f"rol{i}", [128, G], BF16) for i in range(2)]
        sblk_g = sbf("sblk_g", [128, 96])
        rblk_g = sbf("rblk_g", [128, 96])
        rows2 = sbf("rows2", [2, 1])
        psF = ffn.enter_context(nc.psum_tensor("psF", [128, 4096], F32))
        f2s = ExitStack()
        rxh = f2s.enter_context(nc.sbuf_tensor("rxh", [128, NCH * TOK], BF16))
        rxl = f2s.enter_context(nc.sbuf_tensor("rxl", [128, NCH * TOK], BF16))

        st_attn = S.cnt["ST"]

        ve_f2a = 0
        ve_xpar = [0, 0]
        S.wait("sync", "ST", st_attn)
        for g in range(NGT):
            s_, gg = divmod(g, NG)
            t0 = gg * G
            for d in range(NCH):
                par = (g * NCH + d) % 2
                S.wait("sync", "VE", ve_xpar[par])
                S.dma("sync", xga[par][:, :],
                      xhd[s_, 128 * d:128 * (d + 1), t0:t0 + G], f"LDW{par}")
                S.wait("vector", f"LDW{par}")
                if g == 0 and d == 0:
                    S.wait("vector", "VE", ve_avec)
                def _rx(e, s_=s_, g=g, d=d, par=par):
                    col = d * TOK + s_ * T + (g % NG) * G
                    t = xga[par][:, :]
                    e.tensor_scalar(out=t, in0=t,
                                    scalar1=avec[:, 2 * d + s_:2 * d + s_ + 1],
                                    scalar2=bvec[:, 2 * d + s_:2 * d + s_ + 1],
                                    op0=ALU.mult, op1=ALU.add)
                    e.tensor_scalar(out=t, in0=t, scalar1=MAGIC, scalar2=MAGIC,
                                    op0=ALU.add, op1=ALU.subtract)
                    e.tensor_copy(out=rxh[:, col:col + G], in_=t)
                    ins = e.tensor_tensor(out=rxl[:, col:col + G], in0=t,
                                          in1=rxh[:, col:col + G], op=ALU.subtract)
                    return ins
                ve_xpar[par] = S.put("vector", _rx, "VE")
        ve_f2a = S.cnt["VE"]

        wsrc = [wup, wgate]
        pdst = [pud, pgd]
        mcol = [2, 3]
        deq_ve = [0, 0]
        tr_pe = 0
        ev_se = 0
        mm_pe = {0: 0, 1: 0}
        ps_dma = [0, 0]

        def dequant_1024(par_buf, ve_sem_tag="VE"):
            n_blk = 32
            S.put("vector", lambda e: e.tensor_reduce(
                out=sblk[:, 0:n_blk],
                in_=par_buf.rearrange("p (b c) -> p b c", b=n_blk),
                axis=AX.X, op=ALU.max, apply_absolute_value=True), "VE")
            S.chain("vector", [
                lambda e: e.tensor_scalar(out=sblk[:, 0:n_blk], in0=sblk[:, 0:n_blk],
                                          scalar1=cst(6), scalar2=1e-12,
                                          op0=ALU.mult, op1=ALU.max),
                lambda e: e.reciprocal(out=rblk[:, 0:n_blk], in_=sblk[:, 0:n_blk]),
                lambda e: e.tensor_tensor(
                    out=wdq[:, 0:1024].rearrange("p (b c) -> p b c", b=n_blk),
                    in0=par_buf.rearrange("p (b c) -> p b c", b=n_blk),
                    in1=rblk[:, 0:n_blk].unsqueeze(2).broadcast_to([128, n_blk, 32]),
                    op=ALU.mult),
            ], "VE")
            def _fin(e):
                e.tensor_scalar(out=wdq[:, 0:1024], in0=wdq[:, 0:1024], scalar1=MAGIC,
                                scalar2=MAGIC, op0=ALU.add, op1=ALU.subtract)
                e.tensor_tensor(
                    out=wdq[:, 0:1024].rearrange("p (b c) -> p b c", b=n_blk),
                    in0=wdq[:, 0:1024].rearrange("p (b c) -> p b c", b=n_blk),
                    in1=sblk[:, 0:n_blk].unsqueeze(2).broadcast_to([128, n_blk, 32]),
                    op=ALU.mult)
                e.tensor_copy(out=wh_n[:, :], in_=wdq[:, 0:1024])
                return e.tensor_tensor(out=wl_n[:, :], in0=wdq[:, 0:1024],
                                       in1=wh_n[:, :], op=ALU.subtract)
            return S.put("vector", _fin, "VE")

        sblk = sblk_g
        rblk = rblk_g

        first_mm = True
        for wi in range(2):
            for oc in range(NOCH):
                it = wi * NOCH + oc
                par = it % 2
                S.wait("sync", "VE", deq_ve[par])
                S.dma("sync", wr[par][:, :], wsrc[wi][128 * oc:128 * (oc + 1), :], f"LW{par}")
                S.wait("vector", f"LW{par}")
                S.wait("vector", "PE", tr_pe)
                deq_ve[par] = dequant_1024(wr[par][:, :])
                S.wait("tensor", "VE", deq_ve[par])
                S.wait("tensor", "SE", ev_se)
                if not first_mm:
                    S.wait("tensor", "PE", mm_pe[0])
                def _tr(e):
                    for ih in range(NCH):
                        e.transpose(psF[:, 64 * ih:64 * ih + 64].bitcast(BF16),
                                    wh_n[:, 128 * ih:128 * (ih + 1)], identb_t[:, :])
                        ins = e.transpose(psF[:, 512 + 64 * ih:512 + 64 * ih + 64].bitcast(BF16),
                                          wl_n[:, 128 * ih:128 * (ih + 1)], identb_t[:, :])
                    return ins
                tr_pe = S.put("tensor", _tr, "PE")
                S.wait("scalar", "PE", tr_pe)
                S.wait("scalar", "PE", mm_pe[1])
                def _trev(e, par=par):
                    for ih in range(NCH):
                        e.copy(out=lhsTh[:, 1024 * par + 128 * ih:1024 * par + 128 * (ih + 1)],
                               in_=psF[:, 64 * ih:64 * ih + 64].bitcast(BF16))
                        ins = e.copy(out=lhsTl[:, 1024 * par + 128 * ih:1024 * par + 128 * (ih + 1)],
                                     in_=psF[:, 512 + 64 * ih:512 + 64 * ih + 64].bitcast(BF16))
                    return ins
                trev_se = S.put("scalar", _trev, "SE")

                for half in range(2):
                    S.wait("tensor", "SE", trev_se)
                    S.wait("tensor", "SE", ev_se)
                    if first_mm:
                        S.wait("tensor", "VE", ve_f2a)
                    def _mm(e, par=par, half=half):
                        for pi, (lt, rx) in enumerate(((lhsTh, rxh), (lhsTh, rxl), (lhsTl, rxh))):
                            for ih in range(NCH):
                                step = pi * NCH + ih
                                for tt in range(4):
                                    ins = e.matmul(
                                        psF[:, 2048 * half + 512 * tt:2048 * half + 512 * (tt + 1)],
                                        lt[:, 1024 * par + 128 * ih:1024 * par + 128 * (ih + 1)],
                                        rx[:, ih * TOK + 2048 * half + 512 * tt:
                                           ih * TOK + 2048 * half + 512 * (tt + 1)],
                                        start=(step == 0), stop=(step == 23))
                        return ins
                    mm_pe[half] = S.put("tensor", _mm, "PE")
                    first_mm = False
                    for tt in range(4):
                        sp = tt % 2
                        S.wait("scalar", "PE", mm_pe[half])
                        S.wait("scalar", f"PS{sp}", S.cnt[f"PS{sp}"])
                        S.wait("scalar", "VE")
                        ev_se = S.put("scalar", lambda e, half=half, tt=tt, sp=sp: e.copy(
                            out=pstg[sp][:, :],
                            in_=psF[:, 2048 * half + 512 * tt:2048 * half + 512 * (tt + 1)]), "SE")
                        S.wait("vector", "SE", ev_se)
                        S.put("vector", lambda e, sp=sp: e.tensor_reduce(
                            out=maxacc[:, 1:2], in_=pstg[sp][:, :], axis=AX.X,
                            op=ALU.max, apply_absolute_value=True), "VE")
                        S.chain("vector", [
                            lambda e, wi=wi: e.tensor_tensor(
                                out=maxacc[:, mcol[wi]:mcol[wi] + 1],
                                in0=maxacc[:, mcol[wi]:mcol[wi] + 1],
                                in1=maxacc[:, 1:2], op=ALU.max)], "VE")
                        S.wait("gpsimd", "SE", ev_se)
                        ps_dma[sp] = S.dma(
                            "gpsimd",
                            pdst[wi][oc, :, 2048 * half + 512 * tt:2048 * half + 512 * (tt + 1)],
                            pstg[sp][:, :], f"PS{sp}")

        S.flush(block)
        f2s.close()
        S.wait("tensor", "VE")
        S.wait("tensor", "SE")
        pe_tr2 = S.put("tensor", lambda e: e.transpose(
            psF[0:2, 0:128], maxacc[:, 2:4], identf_t[:, :]), "PE")
        S.wait("vector", "PE", pe_tr2)
        ve_m2 = S.put("vector", lambda e: e.tensor_reduce(
            out=rows2[0:2, 0:1], in_=psF[0:2, 0:128], axis=AX.X, op=ALU.max), "VE")
        S.wait("gpsimd", "VE", ve_m2)
        S.wait("gpsimd", "PS0", S.cnt["PS0"])
        S.wait("gpsimd", "PS1", S.cnt["PS1"])
        S.wait("gpsimd", "VE", ve_consts)
        S.dma("gpsimd", ccin[1][0:1, 2:8], z8[0:1, 2:8], "GD")
        S.dma("gpsimd", ccin[1][0:1, 0:2], rows2[0:2, 0:1], "GD")
        S.wait("gpsimd", "GD")
        cc2 = S.put("gpsimd", lambda e: e.collective_compute(
            "AllReduce", ALU.max, replica_groups=[list(range(8))],
            ins=[ccin[1]], outs=[ccout[1]]), "CC")
        S.wait("gpsimd", "CC", cc2)
        S.dma("gpsimd", z8[:, :], ccout[1][:, :], "GD")
        S.wait("vector", "GD")
        ve_ar2 = S.chain("vector", [
            lambda e: e.tensor_scalar(out=rows[0:1, 3:4], in0=z8[0:1, 0:1],
                                      scalar1=rows[0:1, 1:2], scalar2=None, op0=ALU.mult),
            lambda e: e.tensor_scalar(out=rows[0:1, 4:5], in0=z8[0:1, 1:2],
                                      scalar1=rows[0:1, 1:2], scalar2=None, op0=ALU.mult),
            lambda e: e.tensor_scalar(out=rows[0:1, 5:6], in0=rows[0:1, 3:4],
                                      scalar1=cst(7)[0:1, :], scalar2=1e-12,
                                      op0=ALU.mult, op1=ALU.max),
            lambda e: e.tensor_scalar(out=rows[0:1, 8:9], in0=rows[0:1, 4:5],
                                      scalar1=cst(7)[0:1, :], scalar2=1e-12,
                                      op0=ALU.mult, op1=ALU.max),
            lambda e: e.reciprocal(out=rows[0:1, 6:7], in_=rows[0:1, 5:6]),
            lambda e: e.tensor_scalar(out=rows[0:1, 7:8], in0=rows[0:1, 6:7],
                                      scalar1=rows[0:1, 1:2], scalar2=None, op0=ALU.mult),
            lambda e: e.reciprocal(out=rows[0:1, 9:10], in_=rows[0:1, 8:9]),
            lambda e: e.tensor_scalar(out=rows[0:1, 10:11], in0=rows[0:1, 9:10],
                                      scalar1=rows[0:1, 1:2], scalar2=None, op0=ALU.mult),
        ], "VE")
        last_ve = ve_ar2
        for col, slot in ((1, 7), (2, 10), (3, 8)):
            S.wait("tensor", "VE", last_ve)
            pe_b = S.put("tensor", lambda e, slot=slot: e.matmul(
                psF[:, 0:1], ones1[:, :], rows[0:1, slot:slot + 1],
                start=True, stop=True), "PE")
            S.wait("vector", "PE", pe_b)
            last_ve = S.put("vector", lambda e, col=col: e.tensor_copy(
                out=vecb[:, col:col + 1], in_=psF[:, 0:1]), "VE")
        ve_vecb2 = last_ve

        f34 = ExitStack()
        wdres = f34.enter_context(nc.sbuf_tensor("wdres", [128, NOCH * 2048], BF16))
        f34b = ExitStack()
        BIGW = 2048
        bigA = [f34b.enter_context(nc.sbuf_tensor(f"bigA{i}", [128, BIGW], F32))
                for i in range(2)]
        bigB = [f34b.enter_context(nc.sbuf_tensor(f"bigB{i}", [128, BIGW], F32))
                for i in range(2)]
        bigC = [f34b.enter_context(nc.sbuf_tensor(f"bigC{i}", [128, BIGW], F32))
                for i in range(2)]
        cons3 = [0, 0]
        wd_tile = 0
        pe_tw = 0
        se_tw = 0
        deq_mark = 0
        for it in range(NOCH * 2):
            oc, hh = divmod(it, 2)
            par = it % 2
            c0 = BIGW * hh
            S.wait("sync", "VE", cons3[par])
            S.wait("sync", f"PS{par}", S.cnt[f"PS{par}"])
            S.dma("sync", bigA[par][:, :], pgd[oc, :, c0:c0 + BIGW], f"PL{par}")
            S.wait("vector", f"PL{par}")
            if it == 0:
                S.wait("vector", "VE", ve_vecb2)
            def _rg(e, par=par):
                e.tensor_scalar(out=bigA[par][:, :], in0=bigA[par][:, :],
                                scalar1=vecb[:, 2:3], scalar2=MAGIC,
                                op0=ALU.mult, op1=ALU.add)
                return e.tensor_scalar(out=bigA[par][:, :], in0=bigA[par][:, :],
                                       scalar1=MAGIC, scalar2=None, op0=ALU.subtract)
            ve_rg = S.put("vector", _rg, "VE")
            S.wait("scalar", "VE", ve_rg)
            se_sg2 = S.put("scalar", lambda e, par=par: e.activation(
                out=bigB[par][:, :], in_=bigA[par][:, :], func=AF.Sigmoid,
                scale=vecb[:, 3:4]), "SE")
            S.wait("vector", "SE", se_sg2)
            S.wait("vector", f"PS{par}", S.cnt[f"PS{par}"])
            def _s16p3(e, par=par):
                e.tensor_scalar(out=bigB[par][:, :], in0=bigB[par][:, :],
                                scalar1=65536.0, scalar2=MAGIC,
                                op0=ALU.mult, op1=ALU.add)
                e.tensor_scalar(out=bigB[par][:, :], in0=bigB[par][:, :],
                                scalar1=MAGIC, scalar2=65535.0,
                                op0=ALU.subtract, op1=ALU.min)
                e.tensor_tensor(out=bigC[par][:, :], in0=bigA[par][:, :],
                                in1=bigB[par][:, :], op=ALU.mult)
                return e.tensor_reduce(out=maxacc[:, 1:2], in_=bigC[par][:, :],
                                       axis=AX.X, op=ALU.max, apply_absolute_value=True)
            ve_p3 = S.put("vector", _s16p3, "VE")
            cons3[par] = S.chain("vector", [
                lambda e: e.tensor_tensor(out=maxacc[:, 4:5], in0=maxacc[:, 4:5],
                                          in1=maxacc[:, 1:2], op=ALU.max)], "VE")
            S.wait("gpsimd", "VE", cons3[par])
            S.dma("gpsimd", p3d[oc, :, c0:c0 + BIGW], bigC[par][:, :], f"PS{par}")

            if hh == 0 and wd_tile < NCH * 3:
                dt_, third = divmod(wd_tile, 3)
                wd_tile += 1
                S.wait("gpsimd", "VE", deq_mark)
                S.dma("gpsimd", wr[0][:, :],
                      wdown[128 * dt_:128 * (dt_ + 1), 1024 * third:1024 * (third + 1)], "WS")
                S.wait("vector", "WS")
                S.wait("vector", "PE", pe_tw)
                deq_mark = dequant_1024(wr[0][:, :])
                S.wait("tensor", "VE", deq_mark)
                S.wait("tensor", "SE", se_tw)
                def _trw8(e):
                    for j in range(8):
                        e.transpose(psF[:, 3072 + 64 * j:3072 + 64 * j + 64].bitcast(BF16),
                                    wh_n[:, 128 * j:128 * (j + 1)], identb_t[:, :])
                        ins = e.transpose(psF[:, 3584 + 64 * j:3584 + 64 * j + 64].bitcast(BF16),
                                          wl_n[:, 128 * j:128 * (j + 1)], identb_t[:, :])
                    return ins
                pe_tw = S.put("tensor", _trw8, "PE")
                S.wait("scalar", "PE", pe_tw)
                def _twev(e, dt_=dt_, third=third):
                    for j in range(8):
                        ihh = third * 8 + j
                        e.copy(out=wdres[:, 2048 * ihh + 128 * dt_:2048 * ihh + 128 * (dt_ + 1)],
                               in_=psF[:, 3072 + 64 * j:3072 + 64 * j + 64].bitcast(BF16))
                        ins = e.copy(
                            out=wdres[:, 2048 * ihh + 1024 + 128 * dt_:
                                      2048 * ihh + 1024 + 128 * (dt_ + 1)],
                            in_=psF[:, 3584 + 64 * j:3584 + 64 * j + 64].bitcast(BF16))
                    return ins
                se_tw = S.put("scalar", _twev, "SE")

        S.wait("tensor", "VE")
        pe_tr3 = S.put("tensor", lambda e: e.transpose(
            psF[0:1, 0:128], maxacc[:, 4:5], identf_t[:, :]), "PE")
        S.wait("vector", "PE", pe_tr3)
        ve_m3 = S.put("vector", lambda e: e.tensor_reduce(
            out=z8[0:1, 2:3], in_=psF[0:1, 0:128], axis=AX.X, op=ALU.max), "VE")
        ve_m3b = S.chain("vector", [
            lambda e: e.tensor_scalar(out=rows[0:1, 11:12], in0=rows[0:1, 8:9],
                                      scalar1=1.0 / 65536.0, scalar2=None, op0=ALU.mult),
            lambda e: e.tensor_scalar(out=z8[0:1, 0:1], in0=z8[0:1, 2:3],
                                      scalar1=rows[0:1, 11:12], scalar2=None, op0=ALU.mult),
        ], "VE")
        S.wait("gpsimd", "VE", ve_m3b)
        S.wait("gpsimd", "PS0", S.cnt["PS0"])
        S.wait("gpsimd", "PS1", S.cnt["PS1"])
        S.dma("gpsimd", ccin[2][:, :], z8[:, :], "GD")
        S.wait("gpsimd", "GD")
        cc3 = S.put("gpsimd", lambda e: e.collective_compute(
            "AllReduce", ALU.max, replica_groups=[list(range(8))],
            ins=[ccin[2]], outs=[ccout[2]]), "CC")
        S.wait("gpsimd", "CC", cc3)
        S.dma("gpsimd", z8[:, :], ccout[2][:, :], "GD")
        S.wait("vector", "GD")
        ve_ar3 = S.chain("vector", [
            lambda e: e.tensor_scalar(out=rows[0:1, 12:13], in0=z8[0:1, 0:1],
                                      scalar1=cst(7)[0:1, :], scalar2=1e-12,
                                      op0=ALU.mult, op1=ALU.max),
            lambda e: e.reciprocal(out=rows[0:1, 13:14], in_=rows[0:1, 12:13]),
            lambda e: e.tensor_scalar(out=rows[0:1, 14:15], in0=rows[0:1, 13:14],
                                      scalar1=rows[0:1, 11:12], scalar2=None, op0=ALU.mult),
        ], "VE")
        S.wait("tensor", "VE", ve_ar3)
        pe_b3 = S.put("tensor", lambda e: e.matmul(
            psF[:, 0:1], ones1[:, :], rows[0:1, 14:15], start=True, stop=True), "PE")
        S.wait("vector", "PE", pe_b3)
        ve_vecb4 = S.put("vector", lambda e: e.tensor_copy(
            out=vecb[:, 4:5], in_=psF[:, 0:1]), "VE")

        cons4 = [cons3[0], cons3[1]]
        for it in range(NOCH * 2):
            oc, hh = divmod(it, 2)
            par = it % 2
            c0 = BIGW * hh
            S.wait("sync", "VE", cons4[par])
            S.dma("sync", bigA[par][:, :], pud[oc, :, c0:c0 + BIGW], f"PL{par}")
            S.dma("sync", bigB[par][:, :], p3d[oc, :, c0:c0 + BIGW], f"PL{par}")
            S.wait("vector", f"PL{par}")
            if it == 0:
                S.wait("vector", "VE", ve_vecb4)
            S.wait("vector", f"PS{par}", S.cnt[f"PS{par}"])
            def _ruf(e, par=par):
                e.tensor_scalar(out=bigA[par][:, :], in0=bigA[par][:, :],
                                scalar1=vecb[:, 1:2], scalar2=MAGIC,
                                op0=ALU.mult, op1=ALU.add)
                e.tensor_scalar(out=bigA[par][:, :], in0=bigA[par][:, :],
                                scalar1=MAGIC, scalar2=None, op0=ALU.subtract)
                e.tensor_scalar(out=bigB[par][:, :], in0=bigB[par][:, :],
                                scalar1=vecb[:, 4:5], scalar2=MAGIC,
                                op0=ALU.mult, op1=ALU.add)
                e.tensor_scalar(out=bigB[par][:, :], in0=bigB[par][:, :],
                                scalar1=MAGIC, scalar2=None, op0=ALU.subtract)
                e.tensor_tensor(out=bigC[par][:, :], in0=bigA[par][:, :],
                                in1=bigB[par][:, :], op=ALU.mult)
                return e.tensor_reduce(out=maxacc[:, 1:2], in_=bigC[par][:, :],
                                       axis=AX.X, op=ALU.max, apply_absolute_value=True)
            ve_p4 = S.put("vector", _ruf, "VE")
            cons4[par] = S.chain("vector", [
                lambda e: e.tensor_tensor(out=maxacc[:, 5:6], in0=maxacc[:, 5:6],
                                          in1=maxacc[:, 1:2], op=ALU.max)], "VE")
            S.wait("gpsimd", "VE", cons4[par])
            S.dma("gpsimd", p4d[oc, :, c0:c0 + BIGW], bigC[par][:, :], f"PS{par}")

        S.wait("tensor", "VE")
        pe_tr4 = S.put("tensor", lambda e: e.transpose(
            psF[0:1, 0:128], maxacc[:, 5:6], identf_t[:, :]), "PE")
        S.wait("vector", "PE", pe_tr4)
        ve_m4 = S.put("vector", lambda e: e.tensor_reduce(
            out=z8[0:1, 2:3], in_=psF[0:1, 0:128], axis=AX.X, op=ALU.max), "VE")
        ve_m4b = S.chain("vector", [
            lambda e: e.tensor_scalar(out=rows[0:1, 15:16], in0=rows[0:1, 12:13],
                                      scalar1=rows[0:1, 5:6], scalar2=None, op0=ALU.mult),
            lambda e: e.tensor_scalar(out=z8[0:1, 0:1], in0=z8[0:1, 2:3],
                                      scalar1=rows[0:1, 15:16], scalar2=None, op0=ALU.mult),
        ], "VE")
        S.wait("gpsimd", "VE", ve_m4b)
        S.wait("gpsimd", "PS0", S.cnt["PS0"])
        S.wait("gpsimd", "PS1", S.cnt["PS1"])
        S.dma("gpsimd", ccin[3][:, :], z8[:, :], "GD")
        S.wait("gpsimd", "GD")
        cc4 = S.put("gpsimd", lambda e: e.collective_compute(
            "AllReduce", ALU.max, replica_groups=[list(range(8))],
            ins=[ccin[3]], outs=[ccout[3]]), "CC")
        S.wait("gpsimd", "CC", cc4)
        S.dma("gpsimd", z8[:, :], ccout[3][:, :], "GD")
        S.wait("vector", "GD")
        ve_ar4 = S.chain("vector", [
            lambda e: e.tensor_scalar(out=rows[0:1, 0:1], in0=z8[0:1, 0:1],
                                      scalar1=cst(7)[0:1, :], scalar2=1e-12,
                                      op0=ALU.mult, op1=ALU.max),
            lambda e: e.reciprocal(out=rows[0:1, 2:3], in_=rows[0:1, 0:1]),
            lambda e: e.tensor_scalar(out=rows[0:1, 3:4], in0=rows[0:1, 2:3],
                                      scalar1=rows[0:1, 15:16], scalar2=None, op0=ALU.mult),
        ], "VE")
        last_ve = ve_ar4
        for col, slot in ((5, 3), (6, 0)):
            S.wait("tensor", "VE", last_ve)
            pe_b = S.put("tensor", lambda e, slot=slot: e.matmul(
                psF[:, 0:1], ones1[:, :], rows[0:1, slot:slot + 1],
                start=True, stop=True), "PE")
            S.wait("vector", "PE", pe_b)
            last_ve = S.put("vector", lambda e, col=col: e.tensor_copy(
                out=vecb[:, col:col + 1], in_=psF[:, 0:1]), "VE")
        def _g2(e):
            for s2 in range(NSEQ):
                for d in range(NCH):
                    ins = e.tensor_scalar(out=g2vec[:, 2 * d + s2:2 * d + s2 + 1],
                                          in0=mod_ap(5, d, s2),
                                          scalar1=vecb[:, 6:7], scalar2=None, op0=ALU.mult)
            return ins
        ve_g2 = S.chain("vector", [lambda e: e.engine_nop(), _g2], "VE")

        f34b.close()
        pe_f5 = 0
        ve_ro = [0, 0]
        ve_x2used = 0
        first_f5mm = True
        for g in range(NGT):
            s_, gg = divmod(g, NG)
            t0 = gg * G
            gcol = s_ * T + t0
            for oc in range(NOCH):
                par = oc % 2
                S.wait("gpsimd", "VE", ve_ro[par])
                S.dma("gpsimd", p4_t[par][:, :], p4d[oc, :, gcol:gcol + G], f"WD{par}")
                S.wait("vector", f"WD{par}")
                S.wait("vector", "VE", ve_g2)
                S.wait("vector", "PE", pe_f5)
                def _ro(e, par=par):
                    e.tensor_scalar(out=bufD[:, 0:G], in0=p4_t[par][:, :],
                                    scalar1=vecb[:, 5:6], scalar2=MAGIC,
                                    op0=ALU.mult, op1=ALU.add)
                    e.tensor_scalar(out=bufD[:, 0:G], in0=bufD[:, 0:G],
                                    scalar1=MAGIC, scalar2=None, op0=ALU.subtract)
                    e.tensor_copy(out=roh[par][:, :], in_=bufD[:, 0:G])
                    return e.tensor_tensor(out=rol[par][:, :], in0=bufD[:, 0:G],
                                           in1=roh[par][:, :], op=ALU.subtract)
                ve_ro[par] = S.put("vector", _ro, "VE")
                S.wait("tensor", "VE", ve_ro[par])
                if first_f5mm:
                    S.wait("tensor", "SE", S.cnt["SE"])
                    first_f5mm = False
                if oc < 2 and g > 0:
                    S.wait("tensor", "VE", ve_x2used)
                def _dmm(e, par=par, oc=oc):
                    for pi, (lofs, rx) in enumerate(((0, roh), (0, rol), (1024, roh))):
                        for d in range(NCH):
                            ins = e.matmul(
                                psF[:, 512 * d:512 * (d + 1)],
                                wdres[:, 2048 * oc + lofs + 128 * d:
                                      2048 * oc + lofs + 128 * (d + 1)],
                                rx[par][:, :],
                                start=(oc == 0 and pi == 0), stop=(oc == NOCH - 1 and pi == 2))
                    return ins
                pe_f5 = S.put("tensor", _dmm, "PE")
            for d in range(NCH):
                par = d % 2
                S.wait("sync", "VE", ve_xpar[par])
                S.wait("sync", "ST", S.cnt["ST"])
                S.dma("sync", xga[par][:, :],
                      x2d[s_, 128 * d:128 * (d + 1), t0:t0 + G], f"LDW{par}")
                S.wait("vector", "PE", pe_f5)
                S.wait("vector", f"LDW{par}")
                def _yev5(e, s_=s_, d=d, par=par):
                    return e.scalar_tensor_tensor(
                        out=xga[par][:, :], in0=psF[:, 512 * d:512 * (d + 1)],
                        scalar=g2vec[:, 2 * d + s_:2 * d + s_ + 1],
                        in1=xga[par][:, :], op0=ALU.mult, op1=ALU.add)
                ve_xpar[par] = S.put("vector", _yev5, "VE")
                S.wait("sync", "VE", ve_xpar[par])
                S.dma("sync", yTd[s_, 128 * d:128 * (d + 1), t0:t0 + G],
                      xga[par][:, :], "ST")
            ve_x2used = S.cnt["VE"]

        if DEBUG_OUT:
            S.wait("sync", "VE", S.cnt["VE"])
            S.dma("sync", scald[0:1, 0:16], rows[0:1, 0:16], "ST")
        S.wait("sync", "LD", S.cnt["LD"])
        S.wait("sync", "ST", S.cnt["ST"])
        S.wait("sync", "WS", S.cnt["WS"])
        S.wait("gpsimd", "GD", S.cnt["GD"])
        S.wait("gpsimd", "PS0", S.cnt["PS0"])
        S.wait("gpsimd", "PS1", S.cnt["PS1"])
        S.flush(block)
        f34.close()
        ffn.close()
        S.check()
    return nc



def prep_core(inputs, c):
    import ml_dtypes
    f32 = np.float32
    x = np.asarray(inputs["L0_attn_x"])[2 * c:2 * c + 2]
    ind2 = np.zeros((128, 2), f32)
    ind2[0:64, 0] = 1.0
    ind2[64:128, 1] = 1.0
    indq = np.zeros((16, 1024), f32)
    for ch in range(8):
        for mm_ in range(128):
            indq[2 * ch + mm_ // 64, ch * 128 + mm_] = 1.0
    ind16 = np.zeros((128, 128), f32)
    for ch in range(8):
        for p in range(128):
            ind16[p, ch * 16 + 2 * ch + p // 64] = 1.0
    perm = PERM
    colperm = np.concatenate([np.arange(64 * h, 64 * h + 64) for h in perm])
    m = {
        "xt": x.transpose(0, 2, 1),
        "wqT": np.asarray(inputs["L0_attn_wq"]).T[:, colperm],
        "wkT": np.asarray(inputs["L0_attn_wk"]).T,
        "wvT": np.asarray(inputs["L0_attn_wv"]).T,
        "woT": np.asarray(inputs["L0_attn_wo"]).T[colperm, :],
        "wup": np.asarray(inputs["L0_ffn_w_up"]),
        "wgate": np.asarray(inputs["L0_ffn_w_gate"]),
        "wdown": np.asarray(inputs["L0_ffn_w_down"]),
        "wmod": np.asarray(inputs["L0_wmod"]),
        "cTd": np.asarray(inputs["L0_c"])[2 * c:2 * c + 2].T,
        "qnwd": np.tile(np.asarray(inputs["L0_attn_q_norm_w"]).reshape(-1), 2)[:, None],
        "knwd": np.tile(np.asarray(inputs["L0_attn_k_norm_w"]).reshape(-1), 2)[:, None],
        "identfd": np.eye(128, dtype=f32),
        "identbd": np.eye(128, dtype=ml_dtypes.bfloat16),
        "ind2d": ind2,
        "p2d": ind2.T,
        "indqd": indq,
        "ind16d": ind16,
    }
    return {k: np.ascontiguousarray(np.asarray(v, dtype=v.dtype if hasattr(v, "dtype") else f32))
            for k, v in m.items()}


_cache = {}


def kernel(**inputs):
    key = (PHASES, DEBUG_OUT)
    if key not in _cache:
        _cache[key] = build_program()
    nc = _cache[key]
    in_maps = [prep_core(inputs, c) for c in range(8)]
    res = run_bass_kernel_spmd(nc, in_maps, core_ids=list(range(8)))
    outs = [res.results[c]["yTd"].transpose(0, 2, 1) for c in range(8)]
    out = np.concatenate(outs, axis=0).astype(np.float32)
    kernel.dbg = res.results
    return out


# version 03
# speedup vs baseline: 1.4408x; 1.0287x over previous
import os
import numpy as np
from contextlib import ExitStack

import concourse.bass as bass
import concourse.mybir as mybir
from concourse.bass_utils import run_bass_kernel_spmd

F32 = mybir.dt.float32
BF16 = mybir.dt.bfloat16
I16 = mybir.dt.int16
ALU = mybir.AluOpType
AF = mybir.ActivationFunctionType
AX = mybir.AxisListType

MAGIC = float(np.float32(1.5 * 2 ** 23))
DEBUG_OUT = False
PHASES = "full"
KSTOP = ""

DM, NCH, T, NSEQ, NG, G, NQT = 1024, 8, 2048, 2, 4, 512, 4
H, KVH, HD, DFF, NOCH = 16, 4, 64, 3072, 24
TOK = NSEQ * T
NGT = NSEQ * NG
PERM = [0, 4, 1, 5, 2, 6, 3, 7, 8, 12, 9, 13, 10, 14, 11, 15]
ENGS = ("sync", "tensor", "vector", "scalar", "gpsimd")
SEM = {}


class Sched:
    def __init__(self):
        self.streams = {k: [] for k in ENGS}
        self.meta = {k: [] for k in ENGS}
        self.cnt = {}

    def newsem(self, name):
        self.cnt[name] = 0

    def wait(self, eng, sem, val=None):
        v = self.cnt[sem] if val is None else val
        if v > 0:
            self.streams[eng].append(lambda e, s=sem, v=v: e.wait_ge(SEM[s], v))
            self.meta[eng].append(("wait", sem, v, eng))
        return v

    def put(self, eng, build, sem=None, inc=1):
        if sem is None:
            self.streams[eng].append(lambda e, b=build: b(e))
            self.meta[eng].append(("op", None, 0, eng))
            return None
        self.cnt[sem] += inc
        v = self.cnt[sem]
        self.streams[eng].append(lambda e, b=build, s=sem, i=inc: b(e).then_inc(SEM[s], i))
        self.meta[eng].append(("inc", sem, v, eng))
        return v

    def chain(self, eng, builds, sem):
        last = None
        for b in builds:
            if last is not None:
                self.wait(eng, sem, last)
            last = self.put(eng, b, sem)
        return last

    def dma(self, eng, out, in_, sem):
        return self.put(eng, lambda e, o=out, i=in_: e.dma_start(out=o, in_=i), sem, 16)

    def flush(self, block):
        for eng in ENGS:
            st = self.streams[eng]
            if st:
                getattr(block, eng)(lambda e, st=list(st): [f(e) for f in st])
                self.streams[eng] = []

    def check(self):
        pos = {k: 0 for k in ENGS}
        cur = {s: 0 for s in self.cnt}
        progress = True
        while progress:
            progress = False
            for eng in ENGS:
                while pos[eng] < len(self.meta[eng]):
                    kind, sem, v, _ = self.meta[eng][pos[eng]]
                    if kind == "wait" and cur[sem] < v:
                        break
                    if kind == "inc":
                        cur[sem] = max(cur[sem], v)
                    pos[eng] += 1
                    progress = True
        stuck = {k: (pos[k], len(self.meta[k])) for k in ENGS if pos[k] < len(self.meta[k])}
        if stuck:
            detail = {k: self.meta[k][pos[k]] for k in stuck}
            raise RuntimeError(f"deadlock: {stuck}; blocked on {detail}; sems {cur}")


def build_program():
    nc = bass.Bass("TRN2", target_bir_lowering=False, detect_race_conditions=False)
    S = Sched()
    for s in ("LD", "LDC", "LDK", "LDW0", "LDW1", "LW0", "LW1", "PS0", "PS1",
              "PL0", "PL1", "WD0", "WD1", "WS", "ST", "GD", "PE", "VE", "SE", "GE", "CC"):
        S.newsem(s)

    ei = lambda n, sh, dt=F32: nc.dram_tensor(n, sh, dt, kind="ExternalInput").ap()
    xt = ei("xt", [NSEQ, DM, T])
    wqT, wkT = ei("wqT", [DM, DM]), ei("wkT", [DM, KVH * HD])
    wvT, woT = ei("wvT", [DM, KVH * HD]), ei("woT", [DM, DM])
    wup, wgate = ei("wup", [DFF, DM]), ei("wgate", [DFF, DM])
    wdown, wmod = ei("wdown", [DM, DFF]), ei("wmod", [DM, 6 * DM])
    cTd = ei("cTd", [DM, NSEQ])
    qnwd, knwd = ei("qnwd", [128, 1]), ei("knwd", [128, 1])
    identfd, identbd = ei("identfd", [128, 128]), ei("identbd", [128, 128], BF16)
    ind2d, p2d, indqd = ei("ind2d", [128, 2]), ei("p2d", [2, 128]), ei("indqd", [16, 1024])
    ind16d = ei("ind16d", [128, 128])

    yTd = nc.dram_tensor("yTd", [NSEQ, DM, T], F32, kind="ExternalOutput").ap()
    dbg = "ExternalOutput" if DEBUG_OUT else "Internal"
    x2d = nc.dram_tensor("x2d", [NSEQ, DM, T], F32, kind=dbg).ap()
    xhd = nc.dram_tensor("xhd", [NSEQ, DM, T], F32, kind=dbg).ap()
    scald = nc.dram_tensor("scald", [1, 16], F32, kind=dbg).ap()
    pud = nc.dram_tensor("pud", [NOCH, 128, TOK], F32, kind=dbg).ap()
    pgd = nc.dram_tensor("pgd", [NOCH, 128, TOK], F32, kind=dbg).ap()
    rud = nc.dram_tensor("rud", [NOCH, 128, TOK], I16, kind="Internal").ap()
    p3d = nc.dram_tensor("p3d", [NOCH, 128, TOK], F32, kind=dbg).ap()
    p4d = nc.dram_tensor("p4d", [NOCH, 128, TOK], F32, kind=dbg).ap()
    wdThd = nc.dram_tensor("wdThd", [NOCH, 128, DM], BF16, kind="Internal").ap()
    wdTld = nc.dram_tensor("wdTld", [NOCH, 128, DM], BF16, kind="Internal").ap()
    mrst_d = nc.dram_tensor("mrst_d", [NSEQ, 2, T], F32, kind="Internal").ap()
    srd = nc.dram_tensor("srd", [16, 128], F32, kind="Internal").ap()
    qdbg = nc.dram_tensor("qdbg", [NCH, 128, G], F32, kind=dbg).ap()
    kdbg = nc.dram_tensor("kdbg", [2, 128, T], F32, kind=dbg).ap()
    ydbg = nc.dram_tensor("ydbg", [128, NCH * G], F32, kind=dbg).ap()
    qsdbg = nc.dram_tensor("qsdbg", [16, G], F32, kind=dbg).ap()
    wodbg = nc.dram_tensor("wodbg", [128, 2048], F32, kind=dbg).ap()
    aodbg = nc.dram_tensor("aodbg", [NCH, 128, G], F32, kind=dbg).ap()
    moddbg1 = nc.dram_tensor("moddbg1", [128, 96], F32, kind=dbg).ap()
    moddbg2 = nc.dram_tensor("moddbg2", [128, 96], F32, kind=dbg).ap()
    ydbg2 = nc.dram_tensor("ydbg2", [128, NCH * G], F32, kind=dbg).ap()
    ccin = [nc.dram_tensor(f"cc{i}in", [1, 8], F32).ap() for i in range(4)]
    ccout = [nc.dram_tensor(f"cc{i}out", [1, 8], F32).ap() for i in range(4)]

    es = ExitStack()
    with es:
        for name in S.cnt:
            SEM[name] = es.enter_context(nc.semaphore(name))
        block = es.enter_context(nc.Block())
        sbuf = lambda n, sh, dt=F32: es.enter_context(nc.sbuf_tensor(n, sh, dt))

        ones128 = sbuf("ones128", [128, 1])
        ones1 = sbuf("ones1", [1, 128])
        csts = sbuf("csts", [128, 8])
        qnw_t, knw_t = sbuf("qnw_t", [128, 1]), sbuf("knw_t", [128, 1])
        identf_t = sbuf("identf_t", [128, 128])
        identb_t = sbuf("identb_t", [128, 128], BF16)
        ind2_t, p2_t = sbuf("ind2_t", [128, 2]), sbuf("p2_t", [2, 128])
        modT = sbuf("modT", [128, 96])
        siluc, ctile = sbuf("siluc", [128, 16]), sbuf("ctile", [128, 16])
        rows = sbuf("rows", [1, 16])
        z8 = sbuf("z8", [1, 8])
        vecb = sbuf("vecb", [128, 8])
        avec, bvec = sbuf("avec", [128, 16]), sbuf("bvec", [128, 16])
        g2vec = sbuf("g2vec", [128, 16])
        maxacc = sbuf("maxacc", [128, 8])
        bufD = sbuf("bufD", [128, 2 * G])
        att = ExitStack()
        sba = lambda n, sh, dt=F32: att.enter_context(nc.sbuf_tensor(n, sh, dt))
        indq_t = sba("indq_t", [16, 1024])
        ind16_t = sba("ind16_t", [128, 128])
        hl_lo, hl_hi = sba("hl_lo", [1, 128]), sba("hl_hi", [1, 128])
        mask_diag, mask_far = sba("mask_diag", [128, 128]), sba("mask_far", [128, 128])
        mrow, vrow, rrow = sba("mrow", [1, G]), sba("vrow", [1, G]), sba("rrow", [1, G])

        cst = lambda i: csts[:, i:i + 1]

        for dst, src in ((qnw_t, qnwd), (knw_t, knwd), (identf_t, identfd),
                         (identb_t, identbd), (ind2_t, ind2d), (p2_t, p2d),
                         (indq_t, indqd), (ind16_t, ind16d)):
            S.dma("sync", dst[:, :], src[:, :], "LDK")
        S.dma("sync", ctile[:, :].rearrange("p (a s) -> p a s", a=NCH),
              cTd.rearrange("(a p) s -> p a s", p=128), "LDK")
        ld_consts = S.cnt["LDK"]

        def _consts(e):
            e.memset(ones128[:, :], 1.0)
            e.memset(ones1[:, :], 1.0)
            for i, v in enumerate((0.125, 1e-5, 1e-6, 1.0 / 64.0, -0.5, 1.0 / 1024.0,
                                   float(np.float32(1.0 / 7.0)),
                                   float(np.float32(1.0 / 32767.0)))):
                e.memset(csts[:, i:i + 1], v)
            e.memset(mask_diag[:, :], 1.0)
            e.memset(mask_far[:, :], 1.0)
            e.memset(z8[:, :], 0.0)
            e.memset(maxacc[:, :], 0.0)
            e.memset(hl_lo[0:1, 0:64], 1.0)
            e.memset(hl_lo[0:1, 64:128], 0.0)
            e.memset(hl_hi[0:1, 0:64], 0.0)
            e.memset(hl_hi[0:1, 64:128], 1.0)
            return e.engine_nop()
        ve_consts = S.put("vector", _consts, "VE")

        S.wait("gpsimd", "VE", ve_consts)
        S.put("gpsimd", lambda e: e.affine_select(
            mask_diag[:, :], mask_diag[:, :], pattern=[[1, 128]],
            compare_op=ALU.is_ge, fill=0.0, channel_multiplier=-1), "GE")
        ge_masks = S.put("gpsimd", lambda e: e.affine_select(
            mask_far[:, :], mask_far[:, :], pattern=[[-1, 128]],
            compare_op=ALU.is_ge, fill=0.0, channel_multiplier=1, base=-1), "GE")

        S.wait("scalar", "LDK", ld_consts)
        se_sig = S.put("scalar", lambda e: e.activation(
            out=siluc[:, :], in_=ctile[:, :], func=AF.Sigmoid), "SE")
        S.wait("vector", "SE", se_sig)
        S.wait("vector", "LDK", ld_consts)
        ve_silu = S.put("vector", lambda e: e.tensor_mul(
            siluc[:, :], siluc[:, :], ctile[:, :]), "VE")

        ph0 = ExitStack()
        wmt = ph0.enter_context(nc.sbuf_tensor("wmt", [128, NCH * 128], F32))
        ps_mod = ph0.enter_context(nc.psum_tensor("ps_mod", [128, 2], F32))
        for oc in range(48):
            if oc > 0:
                S.wait("sync", "PE", S.cnt["PE"])
            for d in range(NCH):
                S.dma("sync", wmt[:, 128 * d:128 * (d + 1)],
                      wmod[128 * d:128 * (d + 1), 128 * oc:128 * (oc + 1)], "LDC")
            S.wait("tensor", "LDC")
            S.wait("tensor", "VE", ve_silu)
            if oc > 0:
                S.wait("tensor", "VE")
            def _mm(e, oc=oc):
                for d in range(NCH):
                    ins = e.matmul(ps_mod[:, :], wmt[:, 128 * d:128 * (d + 1)],
                                   siluc[:, 2 * d:2 * d + 2],
                                   start=(d == 0), stop=(d == NCH - 1))
                return ins
            S.put("tensor", _mm, "PE")
            S.wait("vector", "PE")
            S.put("vector", lambda e, oc=oc: e.tensor_copy(
                out=modT[:, 2 * oc:2 * oc + 2], in_=ps_mod[:, :]), "VE")

        ops = [lambda e: e.tensor_scalar(out=modT[:, 16:32], in0=modT[:, 16:32],
                                         scalar1=1.0, scalar2=None, op0=ALU.add),
               lambda e: e.tensor_scalar(out=modT[:, 64:80], in0=modT[:, 64:80],
                                         scalar1=1.0, scalar2=None, op0=ALU.add)]
        for c0 in (32, 80):
            sl = modT[:, c0:c0 + 16]
            t, m, m2 = g2vec[:, 0:16], avec[:, 0:16], bvec[:, 0:16]
            ops += [
                lambda e, sl=sl, t=t: e.tensor_scalar(out=t, in0=sl, scalar1=-1.0,
                                                      scalar2=None, op0=ALU.mult),
                lambda e, sl=sl, t=t: e.tensor_tensor(out=t, in0=sl, in1=t, op=ALU.max),
                lambda e, t=t: e.tensor_scalar(out=t, in0=t, scalar1=0.25,
                                               scalar2=None, op0=ALU.max),
                lambda e, sl=sl, m=m: e.tensor_scalar(out=m, in0=sl, scalar1=0.0,
                                                      scalar2=None, op0=ALU.is_ge),
                lambda e, m=m, m2=m2: e.tensor_scalar(out=m2, in0=m, scalar1=2.0,
                                                      scalar2=-1.0, op0=ALU.mult, op1=ALU.add),
                lambda e, sl=sl, t=t, m2=m2: e.tensor_mul(sl, t, m2),
            ]
        ve_mod = S.chain("vector", ops, "VE")
        if DEBUG_OUT:
            S.wait("sync", "VE", ve_mod)
            S.dma("sync", moddbg1[:, :], modT[:, :], "ST")
        S.flush(block)
        ph0.close()
        if KSTOP == "mod":
            S.wait("sync", "LDK", S.cnt["LDK"])
            S.wait("sync", "LDC", S.cnt["LDC"])
            S.wait("sync", "VE", S.cnt["VE"])
            S.flush(block)
            S.check()
            return nc

        def mod_ap(p, ch, s):
            c = (8 * p + ch) * 2 + s
            return modT[:, c:c + 1]

        bufA = sba("bufA", [128, NCH * G])
        bufB = sba("bufB", [128, NCH * G])
        bufC = sba("bufC", [128, NCH * G])
        ysb = sba("ysb", [128, NCH * G])
        kvp = [sba(f"kvp{i}", [128, T]) for i in range(2)]
        qp = [sba(f"qp{i}", [128, G]) for i in range(NCH)]
        vext = sba("vext", [128, 16 * 260])
        attnT = sba("attnT", [128, 640])
        qssr = sba("qssr", [16, G])
        srow16 = sba("srow16", [16, 128])
        recip16 = sba("recip16", [16, 128])
        sr65 = sba("sr65", [65, 2048])
        wk_s = sba("wk_s", [128, NCH * 256])
        wv_s = sba("wv_s", [128, NCH * 256])
        wq_s = sba("wq_s", [128, NCH * 1024])
        wo_g = sba("wo_g", [128, 2 * 1024])
        ps_sc = att.enter_context(nc.psum_tensor("ps_sc", [128, 640], F32))
        ps_y = att.enter_context(nc.psum_tensor("ps_y", [128, 2048], F32))
        ps_a = att.enter_context(nc.psum_tensor("ps_a", [128, 512], F32))
        ps_b = att.enter_context(nc.psum_tensor("ps_b", [128, 512], F32))

        Atile = lambda d: bufA[:, G * d:G * (d + 1)]
        Btile = lambda d: bufB[:, G * d:G * (d + 1)]
        Ctile = lambda d: bufC[:, G * d:G * (d + 1)]

        S.wait("sync", "PE", S.cnt["PE"])
        for d in range(NCH):
            S.dma("sync", wk_s[:, 256 * d:256 * (d + 1)], wkT[128 * d:128 * (d + 1), :], "LDC")
            S.dma("sync", wv_s[:, 256 * d:256 * (d + 1)], wvT[128 * d:128 * (d + 1), :], "LDC")
            S.dma("sync", wq_s[:, 1024 * d:1024 * (d + 1)], wqT[128 * d:128 * (d + 1), :], "LDC")
        ld_attw = S.cnt["LDC"]
        S.wait("vector", "VE", ve_mod)
        ve_vext1 = S.put("vector", lambda e: e.memset(
            vext[:, :].rearrange("p (t k c) -> p t k c", t=16, k=4)[:, :, :, 64:65], 1.0), "VE")

        def ln_stats(src_tiles, wait_ve=None):
            pe_mm = 0
            for d in range(NCH):
                half = bufD[:, G * (d % 2):G * (d % 2) + G]
                if wait_ve is not None and d == 0:
                    S.wait("scalar", "VE", wait_ve)
                if d >= 2:
                    S.wait("scalar", "PE", pe_mm_prev)
                se_sq = S.put("scalar", lambda e, d=d, half=half: e.activation(
                    out=half, in_=src_tiles[d], func=AF.Square), "SE")
                S.wait("tensor", "SE", se_sq)
                def _mm(e, d=d, half=half):
                    e.matmul(ps_sc[0:1, 0:G], ones128[:, :], src_tiles[d],
                             start=(d == 0), stop=(d == NCH - 1))
                    return e.matmul(ps_y[0:1, 0:G], ones128[:, :], half,
                                    start=(d == 0), stop=(d == NCH - 1))
                pe_mm_prev = pe_mm
                pe_mm = S.put("tensor", _mm, "PE")
            S.wait("vector", "PE", pe_mm)
            S.wait("vector", "GD")
            def _rows(e):
                e.tensor_scalar(out=mrow[:, :], in0=ps_sc[0:1, 0:G],
                                scalar1=cst(5)[0:1, :], scalar2=None, op0=ALU.mult)
                e.tensor_scalar(out=vrow[:, :], in0=ps_y[0:1, 0:G],
                                scalar1=cst(5)[0:1, :], scalar2=None, op0=ALU.mult)
                e.tensor_mul(rrow[:, :], mrow[:, :], mrow[:, :])
                return e.tensor_sub(vrow[:, :], vrow[:, :], rrow[:, :])
            ve_rows = S.put("vector", _rows, "VE")
            S.wait("scalar", "VE", ve_rows)
            def _rst(e):
                e.activation(out=rrow[:, :], in_=vrow[:, :], func=AF.Ln,
                             scale=1.0, bias=cst(1)[0:1, :])
                return e.activation(out=rrow[:, :], in_=rrow[:, :], func=AF.Exp,
                                    scale=cst(4)[0:1, :])
            return S.put("scalar", _rst, "SE"), ve_rows

        def bcast_mr(m_ap, r_ap, wait_se=None, wait_ve=None):
            if wait_se is not None:
                S.wait("tensor", "SE", wait_se)
            if wait_ve is not None:
                S.wait("tensor", "VE", wait_ve)
            def _bc(e):
                e.matmul(ps_sc[:, 0:G], ones1[:, :], m_ap, start=True, stop=True)
                return e.matmul(ps_y[:, 0:G], ones1[:, :], r_ap, start=True, stop=True)
            return S.put("tensor", _bc, "PE")

        def h1_build(x_tiles, out_tiles, s, pe_bc):
            S.wait("vector", "PE", pe_bc)
            def _h1(e, s=s):
                for d in range(NCH):
                    e.tensor_sub(out_tiles[d], x_tiles[d], ps_sc[:, 0:G])
                    e.tensor_mul(out_tiles[d], out_tiles[d], ps_y[:, 0:G])
                    ins = e.tensor_scalar(out=out_tiles[d], in0=out_tiles[d],
                                          scalar1=mod_ap(1, d, s), scalar2=mod_ap(0, d, s),
                                          op0=ALU.mult, op1=ALU.add)
                return ins
            return S.put("vector", _h1, "VE")

        prev = {"VE": 0, "PE": 0}

        A2STOPS = ('a2g1', 'a2q', 'a2att', 'a2sc', 'a2av', 'a2sr')
        seqs = [0] if KSTOP in ('a1', 'a2') or KSTOP in A2STOPS else list(range(NSEQ))
        for s in seqs:
            for g in range(NG):
                t0 = g * G
                S.wait("sync", "VE", prev["VE"])
                S.wait("sync", "PE", prev["PE"])
                for d in range(NCH):
                    S.dma("sync", Atile(d), xt[s, 128 * d:128 * (d + 1), t0:t0 + G], "LD")
                S.wait("scalar", "LD")
                S.wait("tensor", "LDC", ld_attw)
                S.wait("tensor", "LD")
                se_r, ve_rows = ln_stats([Atile(d) for d in range(NCH)],
                                         wait_ve=prev["VE"])
                S.wait("gpsimd", "SE", se_r)
                S.wait("gpsimd", "VE", ve_rows)
                S.dma("gpsimd", mrst_d[s, 0:1, t0:t0 + G], mrow[:, :], "GD")
                S.dma("gpsimd", mrst_d[s, 1:2, t0:t0 + G], rrow[:, :], "GD")
                pe_bc = bcast_mr(mrow[:, :], rrow[:, :], wait_se=se_r, wait_ve=ve_rows)
                ve_h1 = h1_build([Atile(d) for d in range(NCH)],
                                 [Btile(d) for d in range(NCH)], s, pe_bc)

                for kv in range(2):
                    S.wait("tensor", "VE", ve_h1)
                    if kv == 1:
                        S.wait("tensor", "VE")
                    def _kmm(e, kv=kv):
                        for d in range(NCH):
                            ins = e.matmul(ps_a[:, 0:G],
                                           wk_s[:, 256 * d + 128 * kv:256 * d + 128 * (kv + 1)],
                                           Btile(d), start=(d == 0), stop=(d == NCH - 1))
                        return ins
                    pe_k = S.put("tensor", _kmm, "PE")
                    S.wait("vector", "PE", pe_k)
                    ve_kev = S.put("vector", lambda e: e.tensor_copy(
                        out=bufD[:, 0:G], in_=ps_a[:, 0:G]), "VE")
                    S.wait("scalar", "VE", ve_kev)
                    se_ksq = S.put("scalar", lambda e: e.activation(
                        out=bufD[:, G:2 * G], in_=bufD[:, 0:G], func=AF.Square), "SE")
                    S.wait("tensor", "SE", se_ksq)
                    pe_kss = S.put("tensor", lambda e: e.matmul(
                        ps_b[0:2, 0:G], ind2_t[:, :], bufD[:, G:2 * G],
                        start=True, stop=True), "PE")
                    S.wait("vector", "PE", pe_kss)
                    ve_kssr = S.put("vector", lambda e: e.tensor_copy(
                        out=qssr[0:2, :], in_=ps_b[0:2, 0:G]), "VE")
                    S.wait("scalar", "VE", ve_kssr)
                    def _krst(e):
                        e.activation(out=qssr[0:2, :], in_=qssr[0:2, :], func=AF.Ln,
                                     scale=cst(3)[0:2, :], bias=cst(2)[0:2, :])
                        return e.activation(out=qssr[0:2, :], in_=qssr[0:2, :],
                                            func=AF.Exp, scale=cst(4)[0:2, :])
                    se_krst = S.put("scalar", _krst, "SE")
                    S.wait("tensor", "SE", se_krst)
                    pe_kbc = S.put("tensor", lambda e: e.matmul(
                        ps_b[:, 0:G], p2_t[:, :], qssr[0:2, :], start=True, stop=True), "PE")
                    S.wait("vector", "PE", pe_kbc)
                    def _knrm(e, kv=kv, t0=t0):
                        e.tensor_mul(bufD[:, 0:G], bufD[:, 0:G], ps_b[:, 0:G])
                        return e.tensor_scalar(
                            out=kvp[kv][:, t0:t0 + G], in0=bufD[:, 0:G],
                            scalar1=knw_t[:, :], scalar2=None, op0=ALU.mult)
                    S.put("vector", _knrm, "VE")

                for ts_ in range(4):
                    ti = g * 4 + ts_
                    S.wait("tensor", "VE")
                    def _vmm(e, ts_=ts_):
                        for d in range(NCH):
                            ins = e.matmul(ps_a[:, 0:256],
                                           Btile(d)[:, 128 * ts_:128 * (ts_ + 1)],
                                           wv_s[:, 256 * d:256 * (d + 1)],
                                           start=(d == 0), stop=(d == NCH - 1))
                        return ins
                    pe_v = S.put("tensor", _vmm, "PE")
                    S.wait("vector", "PE", pe_v)
                    S.wait("vector", "VE", ve_vext1)
                    def _vev(e, ti=ti):
                        dst = vext[:, 260 * ti:260 * (ti + 1)].rearrange(
                            "p (k c) -> p k c", k=4)[:, :, 0:64]
                        src = ps_a[:, 0:256].rearrange("p (k c) -> p k c", k=4)
                        return e.tensor_copy(out=dst, in_=src)
                    S.put("vector", _vev, "VE")
                prev = {"VE": S.cnt["VE"], "PE": S.cnt["PE"]}

            if KSTOP == "a1":
                break
            for g in [0] if KSTOP in A2STOPS else range(NG):
                t0 = g * G
                S.wait("sync", "VE", prev["VE"])
                S.wait("sync", "PE", prev["PE"])
                for d in range(NCH):
                    S.dma("sync", Atile(d), xt[s, 128 * d:128 * (d + 1), t0:t0 + G], "LD")
                S.wait("sync", "GD")
                S.wait("sync", "PE")
                S.dma("sync", mrow[:, :], mrst_d[s, 0:1, t0:t0 + G], "LD")
                S.dma("sync", rrow[:, :], mrst_d[s, 1:2, t0:t0 + G], "LD")
                for yc in range(NCH):
                    S.dma("sync", wo_g[:, 128 * yc:128 * (yc + 1)],
                          woT[128 * yc:128 * (yc + 1), 0:128], "LDW0")
                S.wait("tensor", "LD")
                pe_bc = bcast_mr(mrow[:, :], rrow[:, :], wait_ve=prev["VE"])
                ve_h1 = h1_build([Atile(d) for d in range(NCH)],
                                 [Btile(d) for d in range(NCH)], s, pe_bc)

                for qc in range(NCH):
                    S.wait("tensor", "VE", ve_h1)
                    if qc > 0:
                        S.wait("tensor", "VE")
                    def _qmm(e, qc=qc):
                        for d in range(NCH):
                            ins = e.matmul(ps_a[:, 0:G],
                                           wq_s[:, 1024 * d + 128 * qc:1024 * d + 128 * (qc + 1)],
                                           Btile(d), start=(d == 0), stop=(d == NCH - 1))
                        return ins
                    pe_q = S.put("tensor", _qmm, "PE")
                    S.wait("vector", "PE", pe_q)
                    ve_qev = S.put("vector", lambda e, qc=qc: e.tensor_copy(
                        out=Ctile(qc), in_=ps_a[:, 0:G]), "VE")
                    S.wait("scalar", "VE", ve_qev)
                    se_qsq = S.put("scalar", lambda e, qc=qc: e.activation(
                        out=bufD[:, 0:G], in_=Ctile(qc), func=AF.Square), "SE")
                    S.wait("tensor", "SE", se_qsq)
                    pe_qss = S.put("tensor", lambda e, qc=qc: e.matmul(
                        ps_b[0:16, 0:G], ind16_t[:, 16 * qc:16 * (qc + 1)], bufD[:, 0:G],
                        start=(qc == 0), stop=(qc == NCH - 1)), "PE")
                S.wait("vector", "PE", pe_qss)
                ve_qssr = S.put("vector", lambda e: e.tensor_copy(
                    out=qssr[:, :], in_=ps_b[0:16, 0:G]), "VE")
                S.wait("scalar", "VE")
                def _qrst(e):
                    e.activation(out=qssr[:, :], in_=qssr[:, :], func=AF.Ln,
                                 scale=cst(3)[0:16, :], bias=cst(2)[0:16, :])
                    return e.activation(out=qssr[:, :], in_=qssr[:, :],
                                        func=AF.Exp, scale=cst(4)[0:16, :])
                se_qrst = S.put("scalar", _qrst, "SE")
                for qc in range(NCH):
                    S.wait("tensor", "SE", se_qrst)
                    S.wait("tensor", "VE")
                    pe_qbc = S.put("tensor", lambda e, qc=qc: e.matmul(
                        ps_b[:, 0:G], indq_t[:, 128 * qc:128 * (qc + 1)],
                        qssr[:, :], start=True, stop=True), "PE")
                    S.wait("vector", "PE", pe_qbc)
                    def _qn(e, qc=qc):
                        e.tensor_mul(bufD[:, 0:G], Ctile(qc)[:, :], ps_b[:, 0:G])
                        return e.tensor_scalar(
                            out=qp[qc][:, :], in0=bufD[:, 0:G],
                            scalar1=qnw_t[:, :], scalar2=None, op0=ALU.mult)
                    S.put("vector", _qn, "VE")
                ve_qn = S.cnt["VE"]
                S.wait("vector", "GE", ge_masks)
                if KSTOP == "a2q":
                    continue

                for qt in range(NQT):
                    qi = g * NQT + qt
                    kts = list(range(max(0, qi - 4), qi + 1))
                    n = len(kts)
                    se_exp_prev = None
                    ve_mask_prev = None
                    pe_av_last = None
                    for hi in range(H + 1):
                        if hi < H:
                            r = hi
                            h = PERM[r]
                            S.wait("tensor", "VE", ve_qn)
                            if se_exp_prev is not None:
                                S.wait("tensor", "SE", se_exp_prev)
                            if hi == 0:
                                S.wait("tensor", "VE")
                                S.wait("tensor", "SE")
                            def _sc(e, h=h, r=r, qt=qt, kts=kts):
                                ko = 64 * ((h // 4) % 2)
                                for j, kt in enumerate(kts):
                                    ins = e.matmul(
                                        ps_sc[:, 128 * j:128 * (j + 1)],
                                        kvp[(h // 4) // 2][ko:ko + 64, 128 * kt:128 * (kt + 1)],
                                        qp[r // 2][64 * (r % 2):64 * (r % 2) + 64,
                                                   128 * qt:128 * (qt + 1)],
                                        start=True, stop=True)
                                return ins
                            pe_sc = S.put("tensor", _sc, "PE")
                        if hi > 0 and KSTOP != "a2sc":
                            rp = hi - 1
                            hpv = PERM[rp]
                            S.wait("tensor", "VE", ve_mask_prev)
                            def _av(e, rp=rp, hpv=hpv, kts=kts, n=n):
                                for j, kt in enumerate(kts):
                                    v0 = 260 * kt + 65 * (hpv // 4)
                                    ins = e.matmul(
                                        ps_y[0:65, 128 * rp:128 * (rp + 1)],
                                        vext[:, v0:v0 + 65],
                                        attnT[:, 128 * j:128 * (j + 1)],
                                        start=(j == 0), stop=(j == n - 1))
                                return ins
                            pe_av_last = S.put("tensor", _av, "PE")
                        if hi < H:
                            S.wait("scalar", "PE", pe_av_last if pe_av_last is not None else pe_sc)
                            se_exp_prev = S.put("scalar", lambda e, n=n: e.activation(
                                out=attnT[:, 0:128 * n], in_=ps_sc[:, 0:128 * n],
                                func=AF.Exp, scale=cst(0)), "SE")
                            S.wait("vector", "SE", se_exp_prev)
                            def _msk(e, n=n, qi=qi):
                                ins = e.tensor_mul(attnT[:, 128 * (n - 1):128 * n],
                                                   attnT[:, 128 * (n - 1):128 * n],
                                                   mask_diag[:, :])
                                if qi >= 4:
                                    ins = e.tensor_mul(attnT[:, 0:128],
                                                       attnT[:, 0:128], mask_far[:, :])
                                return ins
                            ve_mask_prev = S.put("vector", _msk, "VE")

                    if KSTOP in ("a2sc", "a2av"):
                        continue
                    S.wait("scalar", "PE", pe_av_last)
                    S.wait("scalar", "GD")
                    se_sr = S.put("scalar", lambda e: e.copy(
                        out=sr65[:, :], in_=ps_y[0:65, 0:2048]), "SE")
                    S.wait("gpsimd", "SE", se_sr)
                    S.dma("gpsimd", srd.rearrange("r q -> (r q)").rearrange("(p f) -> p f", p=1),
                          sr65[64:65, 0:2048], "GD")
                    S.wait("gpsimd", "GD")
                    S.dma("gpsimd", srow16[:, :], srd[:, :], "GD")
                    if KSTOP == "a2sr":
                        continue
                    def _yev(e, qt=qt):
                        for r in range(0, H, 2):
                            dst = ysb[0:64, 512 * (r // 2) + 128 * qt:512 * (r // 2) + 128 * (qt + 1)]
                            ins = e.tensor_copy(out=dst, in_=sr65[0:64, 128 * r:128 * (r + 1)])
                        return ins
                    S.wait("vector", "SE", se_sr)
                    se_yev = S.put("vector", _yev, "VE")
                    for r in range(1, H, 2):
                        S.dma("gpsimd",
                              ysb[64:128, 512 * (r // 2) + 128 * qt:512 * (r // 2) + 128 * (qt + 1)],
                              sr65[0:64, 128 * r:128 * (r + 1)], "GD")
                    gd_yev = S.cnt["GD"]
                    S.wait("vector", "GD", gd_yev)
                    ve_rec = S.put("vector", lambda e: e.reciprocal(
                        out=recip16[:, :], in_=srow16[:, :]), "VE")
                    for half in range(2):
                        S.wait("tensor", "VE", ve_rec)
                        if half == 0:
                            S.wait("tensor", "VE", se_yev)
                        else:
                            S.wait("tensor", "VE")
                        def _rb(e, half=half):
                            for qc in range(4 * half, 4 * half + 4):
                                ins = e.matmul(ps_b[:, 128 * (qc % 4):128 * (qc % 4 + 1)],
                                               indq_t[:, 128 * qc:128 * (qc + 1)],
                                               recip16[:, :], start=True, stop=True)
                            return ins
                        pe_rb = S.put("tensor", _rb, "PE")
                        S.wait("vector", "PE", pe_rb)
                        if half == 0:
                            S.wait("vector", "GD", gd_yev)
                        def _nrm(e, half=half, qt=qt):
                            for qc in range(4 * half, 4 * half + 4):
                                sl = ysb[:, 512 * qc + 128 * qt:512 * qc + 128 * (qt + 1)]
                                ins = e.tensor_mul(sl, sl, ps_b[:, 128 * (qc % 4):128 * (qc % 4 + 1)])
                            return ins
                        S.put("vector", _nrm, "VE")

                if KSTOP in ("a2att", "a2sc", "a2av", "a2sr"):
                    continue
                if DEBUG_OUT and s == 0 and g == 0:
                    S.wait("sync", "VE", S.cnt["VE"])
                    S.dma("sync", qsdbg[:, :], qssr[:, :], "ST")
                    for qc in range(NCH):
                        S.dma("sync", qdbg[qc], qp[qc][:, :], "ST")
                    for i in range(2):
                        S.dma("sync", kdbg[i], kvp[i][:, :], "ST")
                    S.dma("sync", ydbg[:, :], ysb[:, :], "ST")
                ve_norm_all = S.cnt["VE"]
                pe_o = None
                for dd in range(NCH):
                    if dd > 0:
                        S.wait("sync", "PE", pe_o)
                        for yc in range(NCH):
                            S.dma("sync", wo_g[:, 128 * yc:128 * (yc + 1)],
                                  woT[128 * yc:128 * (yc + 1), 128 * dd:128 * (dd + 1)], "LDW0")
                    S.wait("tensor", "VE", ve_norm_all)
                    if dd > 0:
                        S.wait("tensor", "VE")
                    S.wait("tensor", "LDW0", S.cnt["LDW0"])
                    def _omm(e, dd=dd):
                        for yc in range(NCH):
                            ins = e.matmul(ps_a[:, 0:G],
                                           wo_g[:, 128 * yc:128 * (yc + 1)],
                                           ysb[:, 512 * yc:512 * (yc + 1)],
                                           start=(yc == 0), stop=(yc == NCH - 1))
                        return ins
                    pe_o = S.put("tensor", _omm, "PE")
                    S.wait("vector", "PE", pe_o)
                    def _x2(e, d=dd, s=s):
                        return e.scalar_tensor_tensor(
                            out=Btile(d), in0=ps_a[:, 0:G], scalar=mod_ap(2, d, s),
                            in1=Atile(d), op0=ALU.mult, op1=ALU.add)
                    ve_x2 = S.put("vector", _x2, "VE")
                    S.wait("sync", "VE", ve_x2)
                    S.dma("sync", x2d[s, 128 * dd:128 * (dd + 1), t0:t0 + G], Btile(dd), "ST")

                se_r2, _ = ln_stats([Btile(d) for d in range(NCH)], wait_ve=S.cnt["VE"])
                pe_bc2 = bcast_mr(mrow[:, :], rrow[:, :], wait_se=se_r2)
                S.wait("vector", "PE", pe_bc2)
                def _xh(e, s=s):
                    for d in range(NCH):
                        e.tensor_sub(Ctile(d), Btile(d), ps_sc[:, 0:G])
                        e.tensor_mul(Ctile(d), Ctile(d), ps_y[:, 0:G])
                        e.tensor_scalar(out=bufD[:, 0:G], in0=Ctile(d),
                                        scalar1=mod_ap(4, d, s), scalar2=mod_ap(3, d, s),
                                        op0=ALU.mult, op1=ALU.add)
                        e.tensor_reduce(out=maxacc[:, 1:2], in_=bufD[:, 0:G],
                                        axis=AX.X, op=ALU.max, apply_absolute_value=True)
                        ins = e.tensor_tensor(out=maxacc[:, 0:1], in0=maxacc[:, 0:1],
                                              in1=maxacc[:, 1:2], op=ALU.max)
                    return ins
                ve_xh = S.put("vector", _xh, "VE")
                S.wait("sync", "VE", ve_xh)
                for d in range(NCH):
                    S.dma("sync", xhd[s, 128 * d:128 * (d + 1), t0:t0 + G], Ctile(d), "ST")
                prev = {"VE": S.cnt["VE"], "PE": S.cnt["PE"]}

        if KSTOP in ("a1", "a2", "a2q") or KSTOP in A2STOPS:
            S.wait("sync", "LD", S.cnt["LD"])
            S.wait("sync", "LDC", S.cnt["LDC"])
            S.wait("sync", "ST", S.cnt["ST"])
            S.wait("gpsimd", "GD", S.cnt["GD"])
            S.wait("sync", "VE", S.cnt["VE"])
            S.wait("sync", "PE", S.cnt["PE"])
            S.wait("sync", "SE", S.cnt["SE"])
            S.flush(block)
            att.close()
            S.check()
            return nc
        S.wait("tensor", "VE")
        pe_tr = S.put("tensor", lambda e: e.transpose(
            ps_b[0:1, 0:128], maxacc[:, 0:1], identf_t[:, :]), "PE")
        S.wait("vector", "PE", pe_tr)
        ve_mh = S.put("vector", lambda e: e.tensor_reduce(
            out=z8[0:1, 0:1], in_=ps_b[0:1, 0:128], axis=AX.X, op=ALU.max), "VE")
        S.wait("gpsimd", "VE", ve_mh)
        S.dma("gpsimd", ccin[0][:, :], z8[:, :], "GD")
        S.wait("gpsimd", "GD")
        cc1 = S.put("gpsimd", lambda e: e.collective_compute(
            "AllReduce", ALU.max, replica_groups=[list(range(8))],
            ins=[ccin[0]], outs=[ccout[0]]), "CC")
        S.wait("gpsimd", "CC", cc1)
        S.dma("gpsimd", z8[:, :], ccout[0][:, :], "GD")
        S.wait("vector", "GD")
        ve_sx = S.chain("vector", [
            lambda e: e.tensor_scalar(out=rows[0:1, 1:2], in0=z8[0:1, 0:1],
                                      scalar1=cst(7)[0:1, :], scalar2=1e-12,
                                      op0=ALU.mult, op1=ALU.max),
            lambda e: e.reciprocal(out=rows[0:1, 2:3], in_=rows[0:1, 1:2]),
        ], "VE")
        S.wait("tensor", "VE", ve_sx)
        pe_sxb = S.put("tensor", lambda e: e.matmul(
            ps_b[:, 0:1], ones1[:, :], rows[0:1, 2:3], start=True, stop=True), "PE")
        S.wait("vector", "PE", pe_sxb)
        def _avec(e):
            for s in range(NSEQ):
                for d in range(NCH):
                    e.tensor_scalar(out=avec[:, 2 * d + s:2 * d + s + 1],
                                    in0=mod_ap(4, d, s),
                                    scalar1=vecb[:, 0:1], scalar2=None, op0=ALU.mult)
                    ins = e.tensor_scalar(out=bvec[:, 2 * d + s:2 * d + s + 1],
                                          in0=mod_ap(3, d, s),
                                          scalar1=vecb[:, 0:1], scalar2=None, op0=ALU.mult)
            return ins
        ve_avec = S.chain("vector", [
            lambda e: e.tensor_copy(out=vecb[:, 0:1], in_=ps_b[:, 0:1]),
            _avec,
        ], "VE")
        if DEBUG_OUT:
            S.wait("sync", "VE", ve_avec)
            S.dma("sync", scald[0:1, 0:8], z8[:, :], "ST")
            S.dma("sync", moddbg2[:, :], modT[:, :], "ST")

        S.flush(block)
        att.close()

        if PHASES == "attn":
            S.wait("sync", "ST", S.cnt["ST"])
            for s in range(NSEQ):
                S.dma("sync", yTd[s], x2d[s], "ST")
            S.wait("sync", "LD", S.cnt["LD"])
            S.wait("sync", "ST", S.cnt["ST"])
            S.wait("gpsimd", "GD", S.cnt["GD"])
            S.flush(block)
            S.check()
            return nc


        ffn = ExitStack()
        sbf = lambda n, sh, dt=F32: ffn.enter_context(nc.sbuf_tensor(n, sh, dt))
        xga = [sbf(f"xga{i}", [128, G]) for i in range(2)]
        wr = [sbf(f"wr{i}", [128, 1024]) for i in range(2)]
        wdq = bufD
        wh_n = sbf("wh_n", [128, 1024], BF16)
        wl_n = sbf("wl_n", [128, 1024], BF16)
        lhsTh = sbf("lhsTh", [128, 2 * 1024], BF16)
        lhsTl = sbf("lhsTl", [128, 2 * 1024], BF16)
        pstg = [sbf(f"pstg{i}", [128, G]) for i in range(2)]
        p4_t = [sbf(f"p4_t{i}", [128, G]) for i in range(2)]
        roh = [sbf(f"roh{i}", [128, G], BF16) for i in range(2)]
        rol = [sbf(f"rol{i}", [128, G], BF16) for i in range(2)]
        sblk_g = sbf("sblk_g", [128, 96])
        rblk_g = sbf("rblk_g", [128, 96])
        rows2 = sbf("rows2", [2, 1])
        psF = ffn.enter_context(nc.psum_tensor("psF", [128, 4096], F32))
        f2s = ExitStack()
        rxh = f2s.enter_context(nc.sbuf_tensor("rxh", [128, NCH * TOK], BF16))
        rxl = f2s.enter_context(nc.sbuf_tensor("rxl", [128, NCH * TOK], BF16))

        st_attn = S.cnt["ST"]

        ve_f2a = 0
        ve_xpar = [0, 0]
        S.wait("sync", "ST", st_attn)
        for g in range(NGT):
            s_, gg = divmod(g, NG)
            t0 = gg * G
            for d in range(NCH):
                par = (g * NCH + d) % 2
                S.wait("sync", "VE", ve_xpar[par])
                S.dma("sync", xga[par][:, :],
                      xhd[s_, 128 * d:128 * (d + 1), t0:t0 + G], f"LDW{par}")
                S.wait("vector", f"LDW{par}")
                if g == 0 and d == 0:
                    S.wait("vector", "VE", ve_avec)
                def _rx(e, s_=s_, g=g, d=d, par=par):
                    col = d * TOK + s_ * T + (g % NG) * G
                    t = xga[par][:, :]
                    e.tensor_scalar(out=t, in0=t,
                                    scalar1=avec[:, 2 * d + s_:2 * d + s_ + 1],
                                    scalar2=bvec[:, 2 * d + s_:2 * d + s_ + 1],
                                    op0=ALU.mult, op1=ALU.add)
                    e.tensor_scalar(out=t, in0=t, scalar1=MAGIC, scalar2=MAGIC,
                                    op0=ALU.add, op1=ALU.subtract)
                    e.tensor_copy(out=rxh[:, col:col + G], in_=t)
                    ins = e.tensor_tensor(out=rxl[:, col:col + G], in0=t,
                                          in1=rxh[:, col:col + G], op=ALU.subtract)
                    return ins
                ve_xpar[par] = S.put("vector", _rx, "VE")
        ve_f2a = S.cnt["VE"]

        wsrc = [wup, wgate]
        pdst = [pud, pgd]
        mcol = [2, 3]
        deq_ve = [0, 0]
        tr_pe = 0
        ev_se = 0
        mm_pe = {0: 0, 1: 0}
        ps_dma = [0, 0]

        def dequant_1024(par_buf, ve_sem_tag="VE"):
            n_blk = 32
            S.put("vector", lambda e: e.tensor_reduce(
                out=sblk[:, 0:n_blk],
                in_=par_buf.rearrange("p (b c) -> p b c", b=n_blk),
                axis=AX.X, op=ALU.max, apply_absolute_value=True), "VE")
            S.chain("vector", [
                lambda e: e.tensor_scalar(out=sblk[:, 0:n_blk], in0=sblk[:, 0:n_blk],
                                          scalar1=cst(6), scalar2=1e-12,
                                          op0=ALU.mult, op1=ALU.max),
                lambda e: e.reciprocal(out=rblk[:, 0:n_blk], in_=sblk[:, 0:n_blk]),
                lambda e: e.tensor_tensor(
                    out=wdq[:, 0:1024].rearrange("p (b c) -> p b c", b=n_blk),
                    in0=par_buf.rearrange("p (b c) -> p b c", b=n_blk),
                    in1=rblk[:, 0:n_blk].unsqueeze(2).broadcast_to([128, n_blk, 32]),
                    op=ALU.mult),
            ], "VE")
            def _fin(e):
                e.tensor_scalar(out=wdq[:, 0:1024], in0=wdq[:, 0:1024], scalar1=MAGIC,
                                scalar2=MAGIC, op0=ALU.add, op1=ALU.subtract)
                e.tensor_tensor(
                    out=wdq[:, 0:1024].rearrange("p (b c) -> p b c", b=n_blk),
                    in0=wdq[:, 0:1024].rearrange("p (b c) -> p b c", b=n_blk),
                    in1=sblk[:, 0:n_blk].unsqueeze(2).broadcast_to([128, n_blk, 32]),
                    op=ALU.mult)
                e.tensor_copy(out=wh_n[:, :], in_=wdq[:, 0:1024])
                return e.tensor_tensor(out=wl_n[:, :], in0=wdq[:, 0:1024],
                                       in1=wh_n[:, :], op=ALU.subtract)
            return S.put("vector", _fin, "VE")

        sblk = sblk_g
        rblk = rblk_g

        first_mm = True
        for wi in range(2):
            for oc in range(NOCH):
                it = wi * NOCH + oc
                par = it % 2
                S.wait("sync", "VE", deq_ve[par])
                S.dma("sync", wr[par][:, :], wsrc[wi][128 * oc:128 * (oc + 1), :], f"LW{par}")
                S.wait("vector", f"LW{par}")
                S.wait("vector", "PE", tr_pe)
                deq_ve[par] = dequant_1024(wr[par][:, :])
                S.wait("tensor", "VE", deq_ve[par])
                S.wait("tensor", "SE", ev_se)
                if not first_mm:
                    S.wait("tensor", "PE", mm_pe[0])
                def _tr(e):
                    for ih in range(NCH):
                        e.transpose(psF[:, 64 * ih:64 * ih + 64].bitcast(BF16),
                                    wh_n[:, 128 * ih:128 * (ih + 1)], identb_t[:, :])
                        ins = e.transpose(psF[:, 512 + 64 * ih:512 + 64 * ih + 64].bitcast(BF16),
                                          wl_n[:, 128 * ih:128 * (ih + 1)], identb_t[:, :])
                    return ins
                tr_pe = S.put("tensor", _tr, "PE")
                S.wait("scalar", "PE", tr_pe)
                S.wait("scalar", "PE", mm_pe[1])
                def _trev(e, par=par):
                    for ih in range(NCH):
                        e.copy(out=lhsTh[:, 1024 * par + 128 * ih:1024 * par + 128 * (ih + 1)],
                               in_=psF[:, 64 * ih:64 * ih + 64].bitcast(BF16))
                        ins = e.copy(out=lhsTl[:, 1024 * par + 128 * ih:1024 * par + 128 * (ih + 1)],
                                     in_=psF[:, 512 + 64 * ih:512 + 64 * ih + 64].bitcast(BF16))
                    return ins
                trev_se = S.put("scalar", _trev, "SE")

                for half in range(2):
                    S.wait("tensor", "SE", trev_se)
                    S.wait("tensor", "SE", ev_se)
                    if first_mm:
                        S.wait("tensor", "VE", ve_f2a)
                    def _mm(e, par=par, half=half):
                        for pi, (lt, rx) in enumerate(((lhsTh, rxh), (lhsTh, rxl), (lhsTl, rxh))):
                            for ih in range(NCH):
                                step = pi * NCH + ih
                                for tt in range(4):
                                    ins = e.matmul(
                                        psF[:, 2048 * half + 512 * tt:2048 * half + 512 * (tt + 1)],
                                        lt[:, 1024 * par + 128 * ih:1024 * par + 128 * (ih + 1)],
                                        rx[:, ih * TOK + 2048 * half + 512 * tt:
                                           ih * TOK + 2048 * half + 512 * (tt + 1)],
                                        start=(step == 0), stop=(step == 23))
                        return ins
                    mm_pe[half] = S.put("tensor", _mm, "PE")
                    first_mm = False
                    for tt in range(4):
                        sp = tt % 2
                        S.wait("scalar", "PE", mm_pe[half])
                        S.wait("scalar", f"PS{sp}", S.cnt[f"PS{sp}"])
                        S.wait("scalar", "VE")
                        ev_se = S.put("scalar", lambda e, half=half, tt=tt, sp=sp: e.copy(
                            out=pstg[sp][:, :],
                            in_=psF[:, 2048 * half + 512 * tt:2048 * half + 512 * (tt + 1)]), "SE")
                        S.wait("vector", "SE", ev_se)
                        ve_red = S.put("vector", lambda e, sp=sp: e.tensor_reduce(
                            out=maxacc[:, 1:2], in_=pstg[sp][:, :], axis=AX.X,
                            op=ALU.max, apply_absolute_value=True), "VE")
                        S.wait("vector", "VE", ve_red)
                        S.chain("vector", [
                            lambda e, wi=wi: e.tensor_tensor(
                                out=maxacc[:, mcol[wi]:mcol[wi] + 1],
                                in0=maxacc[:, mcol[wi]:mcol[wi] + 1],
                                in1=maxacc[:, 1:2], op=ALU.max)], "VE")
                        S.wait("gpsimd", "SE", ev_se)
                        ps_dma[sp] = S.dma(
                            "gpsimd",
                            pdst[wi][oc, :, 2048 * half + 512 * tt:2048 * half + 512 * (tt + 1)],
                            pstg[sp][:, :], f"PS{sp}")

        S.flush(block)
        f2s.close()
        S.wait("tensor", "VE")
        S.wait("tensor", "SE")
        pe_tr2 = S.put("tensor", lambda e: e.transpose(
            psF[0:2, 0:128], maxacc[:, 2:4], identf_t[:, :]), "PE")
        S.wait("vector", "PE", pe_tr2)
        ve_m2 = S.put("vector", lambda e: e.tensor_reduce(
            out=rows2[0:2, 0:1], in_=psF[0:2, 0:128], axis=AX.X, op=ALU.max), "VE")
        S.wait("gpsimd", "VE", ve_m2)
        S.wait("gpsimd", "PS0", S.cnt["PS0"])
        S.wait("gpsimd", "PS1", S.cnt["PS1"])
        S.wait("gpsimd", "VE", ve_consts)
        S.dma("gpsimd", ccin[1][0:1, 2:8], z8[0:1, 2:8], "GD")
        S.dma("gpsimd", ccin[1][0:1, 0:2], rows2[0:2, 0:1], "GD")
        S.wait("gpsimd", "GD")
        cc2 = S.put("gpsimd", lambda e: e.collective_compute(
            "AllReduce", ALU.max, replica_groups=[list(range(8))],
            ins=[ccin[1]], outs=[ccout[1]]), "CC")
        S.wait("gpsimd", "CC", cc2)
        S.dma("gpsimd", z8[:, :], ccout[1][:, :], "GD")
        S.wait("vector", "GD")
        ve_ar2 = S.chain("vector", [
            lambda e: e.tensor_scalar(out=rows[0:1, 3:4], in0=z8[0:1, 0:1],
                                      scalar1=rows[0:1, 1:2], scalar2=None, op0=ALU.mult),
            lambda e: e.tensor_scalar(out=rows[0:1, 4:5], in0=z8[0:1, 1:2],
                                      scalar1=rows[0:1, 1:2], scalar2=None, op0=ALU.mult),
            lambda e: e.tensor_scalar(out=rows[0:1, 5:6], in0=rows[0:1, 3:4],
                                      scalar1=cst(7)[0:1, :], scalar2=1e-12,
                                      op0=ALU.mult, op1=ALU.max),
            lambda e: e.tensor_scalar(out=rows[0:1, 8:9], in0=rows[0:1, 4:5],
                                      scalar1=cst(7)[0:1, :], scalar2=1e-12,
                                      op0=ALU.mult, op1=ALU.max),
            lambda e: e.reciprocal(out=rows[0:1, 6:7], in_=rows[0:1, 5:6]),
            lambda e: e.tensor_scalar(out=rows[0:1, 7:8], in0=rows[0:1, 6:7],
                                      scalar1=rows[0:1, 1:2], scalar2=None, op0=ALU.mult),
            lambda e: e.reciprocal(out=rows[0:1, 9:10], in_=rows[0:1, 8:9]),
            lambda e: e.tensor_scalar(out=rows[0:1, 10:11], in0=rows[0:1, 9:10],
                                      scalar1=rows[0:1, 1:2], scalar2=None, op0=ALU.mult),
        ], "VE")
        last_ve = ve_ar2
        for col, slot in ((1, 7), (2, 10), (3, 8)):
            S.wait("tensor", "VE", last_ve)
            pe_b = S.put("tensor", lambda e, slot=slot: e.matmul(
                psF[:, 0:1], ones1[:, :], rows[0:1, slot:slot + 1],
                start=True, stop=True), "PE")
            S.wait("vector", "PE", pe_b)
            last_ve = S.put("vector", lambda e, col=col: e.tensor_copy(
                out=vecb[:, col:col + 1], in_=psF[:, 0:1]), "VE")
        ve_vecb2 = last_ve

        f34 = ExitStack()
        wdres = f34.enter_context(nc.sbuf_tensor("wdres", [128, NOCH * 2048], BF16))
        f34b = ExitStack()
        BIGW = 2048
        bigA = [f34b.enter_context(nc.sbuf_tensor(f"bigA{i}", [128, BIGW], F32))
                for i in range(2)]
        bigB = [f34b.enter_context(nc.sbuf_tensor(f"bigB{i}", [128, BIGW], F32))
                for i in range(2)]
        bigC = [f34b.enter_context(nc.sbuf_tensor(f"bigC{i}", [128, BIGW], F32))
                for i in range(2)]
        cons3 = [0, 0]
        wd_tile = 0
        pe_tw = 0
        se_tw = 0
        deq_mark = 0
        for it in range(NOCH * 2):
            oc, hh = divmod(it, 2)
            par = it % 2
            c0 = BIGW * hh
            S.wait("sync", "VE", cons3[par])
            S.wait("sync", f"PS{par}", S.cnt[f"PS{par}"])
            S.dma("sync", bigA[par][:, :], pgd[oc, :, c0:c0 + BIGW], f"PL{par}")
            S.wait("vector", f"PL{par}")
            if it == 0:
                S.wait("vector", "VE", ve_vecb2)
            def _rg(e, par=par):
                e.tensor_scalar(out=bigA[par][:, :], in0=bigA[par][:, :],
                                scalar1=vecb[:, 2:3], scalar2=MAGIC,
                                op0=ALU.mult, op1=ALU.add)
                return e.tensor_scalar(out=bigA[par][:, :], in0=bigA[par][:, :],
                                       scalar1=MAGIC, scalar2=None, op0=ALU.subtract)
            ve_rg = S.put("vector", _rg, "VE")
            S.wait("scalar", "VE", ve_rg)
            se_sg2 = S.put("scalar", lambda e, par=par: e.activation(
                out=bigB[par][:, :], in_=bigA[par][:, :], func=AF.Sigmoid,
                scale=vecb[:, 3:4]), "SE")
            S.wait("vector", "SE", se_sg2)
            S.wait("vector", f"PS{par}", S.cnt[f"PS{par}"])
            def _s16p3(e, par=par):
                e.tensor_scalar(out=bigB[par][:, :], in0=bigB[par][:, :],
                                scalar1=65536.0, scalar2=MAGIC,
                                op0=ALU.mult, op1=ALU.add)
                e.tensor_scalar(out=bigB[par][:, :], in0=bigB[par][:, :],
                                scalar1=MAGIC, scalar2=65535.0,
                                op0=ALU.subtract, op1=ALU.min)
                e.tensor_tensor(out=bigC[par][:, :], in0=bigA[par][:, :],
                                in1=bigB[par][:, :], op=ALU.mult)
                return e.tensor_reduce(out=maxacc[:, 1:2], in_=bigC[par][:, :],
                                       axis=AX.X, op=ALU.max, apply_absolute_value=True)
            ve_p3 = S.put("vector", _s16p3, "VE")
            S.wait("vector", "VE", ve_p3)
            cons3[par] = S.chain("vector", [
                lambda e: e.tensor_tensor(out=maxacc[:, 4:5], in0=maxacc[:, 4:5],
                                          in1=maxacc[:, 1:2], op=ALU.max)], "VE")
            S.wait("gpsimd", "VE", cons3[par])
            S.dma("gpsimd", p3d[oc, :, c0:c0 + BIGW], bigC[par][:, :], f"PS{par}")

            if hh == 0 and wd_tile < NCH * 3:
                dt_, third = divmod(wd_tile, 3)
                wd_tile += 1
                S.wait("gpsimd", "VE", deq_mark)
                S.dma("gpsimd", wr[0][:, :],
                      wdown[128 * dt_:128 * (dt_ + 1), 1024 * third:1024 * (third + 1)], "WS")
                S.wait("vector", "WS")
                S.wait("vector", "PE", pe_tw)
                deq_mark = dequant_1024(wr[0][:, :])
                S.wait("tensor", "VE", deq_mark)
                S.wait("tensor", "SE", se_tw)
                def _trw8(e):
                    for j in range(8):
                        e.transpose(psF[:, 3072 + 64 * j:3072 + 64 * j + 64].bitcast(BF16),
                                    wh_n[:, 128 * j:128 * (j + 1)], identb_t[:, :])
                        ins = e.transpose(psF[:, 3584 + 64 * j:3584 + 64 * j + 64].bitcast(BF16),
                                          wl_n[:, 128 * j:128 * (j + 1)], identb_t[:, :])
                    return ins
                pe_tw = S.put("tensor", _trw8, "PE")
                S.wait("scalar", "PE", pe_tw)
                def _twev(e, dt_=dt_, third=third):
                    for j in range(8):
                        ihh = third * 8 + j
                        e.copy(out=wdres[:, 2048 * ihh + 128 * dt_:2048 * ihh + 128 * (dt_ + 1)],
                               in_=psF[:, 3072 + 64 * j:3072 + 64 * j + 64].bitcast(BF16))
                        ins = e.copy(
                            out=wdres[:, 2048 * ihh + 1024 + 128 * dt_:
                                      2048 * ihh + 1024 + 128 * (dt_ + 1)],
                            in_=psF[:, 3584 + 64 * j:3584 + 64 * j + 64].bitcast(BF16))
                    return ins
                se_tw = S.put("scalar", _twev, "SE")

        S.wait("tensor", "VE")
        pe_tr3 = S.put("tensor", lambda e: e.transpose(
            psF[0:1, 0:128], maxacc[:, 4:5], identf_t[:, :]), "PE")
        S.wait("vector", "PE", pe_tr3)
        ve_m3 = S.put("vector", lambda e: e.tensor_reduce(
            out=z8[0:1, 2:3], in_=psF[0:1, 0:128], axis=AX.X, op=ALU.max), "VE")
        ve_m3b = S.chain("vector", [
            lambda e: e.tensor_scalar(out=rows[0:1, 11:12], in0=rows[0:1, 8:9],
                                      scalar1=1.0 / 65536.0, scalar2=None, op0=ALU.mult),
            lambda e: e.tensor_scalar(out=z8[0:1, 0:1], in0=z8[0:1, 2:3],
                                      scalar1=rows[0:1, 11:12], scalar2=None, op0=ALU.mult),
        ], "VE")
        S.wait("gpsimd", "VE", ve_m3b)
        S.wait("gpsimd", "PS0", S.cnt["PS0"])
        S.wait("gpsimd", "PS1", S.cnt["PS1"])
        S.dma("gpsimd", ccin[2][:, :], z8[:, :], "GD")
        S.wait("gpsimd", "GD")
        cc3 = S.put("gpsimd", lambda e: e.collective_compute(
            "AllReduce", ALU.max, replica_groups=[list(range(8))],
            ins=[ccin[2]], outs=[ccout[2]]), "CC")
        S.wait("gpsimd", "CC", cc3)
        S.dma("gpsimd", z8[:, :], ccout[2][:, :], "GD")
        S.wait("vector", "GD")
        ve_ar3 = S.chain("vector", [
            lambda e: e.tensor_scalar(out=rows[0:1, 12:13], in0=z8[0:1, 0:1],
                                      scalar1=cst(7)[0:1, :], scalar2=1e-12,
                                      op0=ALU.mult, op1=ALU.max),
            lambda e: e.reciprocal(out=rows[0:1, 13:14], in_=rows[0:1, 12:13]),
            lambda e: e.tensor_scalar(out=rows[0:1, 14:15], in0=rows[0:1, 13:14],
                                      scalar1=rows[0:1, 11:12], scalar2=None, op0=ALU.mult),
        ], "VE")
        S.wait("tensor", "VE", ve_ar3)
        pe_b3 = S.put("tensor", lambda e: e.matmul(
            psF[:, 0:1], ones1[:, :], rows[0:1, 14:15], start=True, stop=True), "PE")
        S.wait("vector", "PE", pe_b3)
        ve_vecb4 = S.put("vector", lambda e: e.tensor_copy(
            out=vecb[:, 4:5], in_=psF[:, 0:1]), "VE")

        cons4 = [cons3[0], cons3[1]]
        for it in range(NOCH * 2):
            oc, hh = divmod(it, 2)
            par = it % 2
            c0 = BIGW * hh
            S.wait("sync", "VE", cons4[par])
            S.dma("sync", bigA[par][:, :], pud[oc, :, c0:c0 + BIGW], f"PL{par}")
            S.dma("sync", bigB[par][:, :], p3d[oc, :, c0:c0 + BIGW], f"PL{par}")
            S.wait("vector", f"PL{par}")
            if it == 0:
                S.wait("vector", "VE", ve_vecb4)
            S.wait("vector", f"PS{par}", S.cnt[f"PS{par}"])
            def _ruf(e, par=par):
                e.tensor_scalar(out=bigA[par][:, :], in0=bigA[par][:, :],
                                scalar1=vecb[:, 1:2], scalar2=MAGIC,
                                op0=ALU.mult, op1=ALU.add)
                e.tensor_scalar(out=bigA[par][:, :], in0=bigA[par][:, :],
                                scalar1=MAGIC, scalar2=None, op0=ALU.subtract)
                e.tensor_scalar(out=bigB[par][:, :], in0=bigB[par][:, :],
                                scalar1=vecb[:, 4:5], scalar2=MAGIC,
                                op0=ALU.mult, op1=ALU.add)
                e.tensor_scalar(out=bigB[par][:, :], in0=bigB[par][:, :],
                                scalar1=MAGIC, scalar2=None, op0=ALU.subtract)
                e.tensor_tensor(out=bigC[par][:, :], in0=bigA[par][:, :],
                                in1=bigB[par][:, :], op=ALU.mult)
                return e.tensor_reduce(out=maxacc[:, 1:2], in_=bigC[par][:, :],
                                       axis=AX.X, op=ALU.max, apply_absolute_value=True)
            ve_p4 = S.put("vector", _ruf, "VE")
            S.wait("vector", "VE", ve_p4)
            cons4[par] = S.chain("vector", [
                lambda e: e.tensor_tensor(out=maxacc[:, 5:6], in0=maxacc[:, 5:6],
                                          in1=maxacc[:, 1:2], op=ALU.max)], "VE")
            S.wait("gpsimd", "VE", cons4[par])
            S.dma("gpsimd", p4d[oc, :, c0:c0 + BIGW], bigC[par][:, :], f"PS{par}")

        S.wait("tensor", "VE")
        pe_tr4 = S.put("tensor", lambda e: e.transpose(
            psF[0:1, 0:128], maxacc[:, 5:6], identf_t[:, :]), "PE")
        S.wait("vector", "PE", pe_tr4)
        ve_m4 = S.put("vector", lambda e: e.tensor_reduce(
            out=z8[0:1, 2:3], in_=psF[0:1, 0:128], axis=AX.X, op=ALU.max), "VE")
        ve_m4b = S.chain("vector", [
            lambda e: e.tensor_scalar(out=rows[0:1, 15:16], in0=rows[0:1, 12:13],
                                      scalar1=rows[0:1, 5:6], scalar2=None, op0=ALU.mult),
            lambda e: e.tensor_scalar(out=z8[0:1, 0:1], in0=z8[0:1, 2:3],
                                      scalar1=rows[0:1, 15:16], scalar2=None, op0=ALU.mult),
        ], "VE")
        S.wait("gpsimd", "VE", ve_m4b)
        S.wait("gpsimd", "PS0", S.cnt["PS0"])
        S.wait("gpsimd", "PS1", S.cnt["PS1"])
        S.dma("gpsimd", ccin[3][:, :], z8[:, :], "GD")
        S.wait("gpsimd", "GD")
        cc4 = S.put("gpsimd", lambda e: e.collective_compute(
            "AllReduce", ALU.max, replica_groups=[list(range(8))],
            ins=[ccin[3]], outs=[ccout[3]]), "CC")
        S.wait("gpsimd", "CC", cc4)
        S.dma("gpsimd", z8[:, :], ccout[3][:, :], "GD")
        S.wait("vector", "GD")
        ve_ar4 = S.chain("vector", [
            lambda e: e.tensor_scalar(out=rows[0:1, 0:1], in0=z8[0:1, 0:1],
                                      scalar1=cst(7)[0:1, :], scalar2=1e-12,
                                      op0=ALU.mult, op1=ALU.max),
            lambda e: e.reciprocal(out=rows[0:1, 2:3], in_=rows[0:1, 0:1]),
            lambda e: e.tensor_scalar(out=rows[0:1, 3:4], in0=rows[0:1, 2:3],
                                      scalar1=rows[0:1, 15:16], scalar2=None, op0=ALU.mult),
        ], "VE")
        last_ve = ve_ar4
        for col, slot in ((5, 3), (6, 0)):
            S.wait("tensor", "VE", last_ve)
            pe_b = S.put("tensor", lambda e, slot=slot: e.matmul(
                psF[:, 0:1], ones1[:, :], rows[0:1, slot:slot + 1],
                start=True, stop=True), "PE")
            S.wait("vector", "PE", pe_b)
            last_ve = S.put("vector", lambda e, col=col: e.tensor_copy(
                out=vecb[:, col:col + 1], in_=psF[:, 0:1]), "VE")
        def _g2(e):
            for s2 in range(NSEQ):
                for d in range(NCH):
                    ins = e.tensor_scalar(out=g2vec[:, 2 * d + s2:2 * d + s2 + 1],
                                          in0=mod_ap(5, d, s2),
                                          scalar1=vecb[:, 6:7], scalar2=None, op0=ALU.mult)
            return ins
        ve_g2 = S.chain("vector", [lambda e: e.engine_nop(), _g2], "VE")

        f34b.close()
        pe_f5 = 0
        ve_ro = [0, 0]
        ve_x2used = 0
        first_f5mm = True
        for g in range(NGT):
            s_, gg = divmod(g, NG)
            t0 = gg * G
            gcol = s_ * T + t0
            for oc in range(NOCH):
                par = oc % 2
                S.wait("gpsimd", "VE", ve_ro[par])
                S.dma("gpsimd", p4_t[par][:, :], p4d[oc, :, gcol:gcol + G], f"WD{par}")
                S.wait("vector", f"WD{par}")
                S.wait("vector", "VE", ve_g2)
                S.wait("vector", "PE", pe_f5)
                def _ro(e, par=par):
                    e.tensor_scalar(out=bufD[:, 0:G], in0=p4_t[par][:, :],
                                    scalar1=vecb[:, 5:6], scalar2=MAGIC,
                                    op0=ALU.mult, op1=ALU.add)
                    e.tensor_scalar(out=bufD[:, 0:G], in0=bufD[:, 0:G],
                                    scalar1=MAGIC, scalar2=None, op0=ALU.subtract)
                    e.tensor_copy(out=roh[par][:, :], in_=bufD[:, 0:G])
                    return e.tensor_tensor(out=rol[par][:, :], in0=bufD[:, 0:G],
                                           in1=roh[par][:, :], op=ALU.subtract)
                ve_ro[par] = S.put("vector", _ro, "VE")
                S.wait("tensor", "VE", ve_ro[par])
                if first_f5mm:
                    S.wait("tensor", "SE", S.cnt["SE"])
                    first_f5mm = False
                if oc < 2 and g > 0:
                    S.wait("tensor", "VE", ve_x2used)
                def _dmm(e, par=par, oc=oc):
                    for pi, (lofs, rx) in enumerate(((0, roh), (0, rol), (1024, roh))):
                        for d in range(NCH):
                            ins = e.matmul(
                                psF[:, 512 * d:512 * (d + 1)],
                                wdres[:, 2048 * oc + lofs + 128 * d:
                                      2048 * oc + lofs + 128 * (d + 1)],
                                rx[par][:, :],
                                start=(oc == 0 and pi == 0), stop=(oc == NOCH - 1 and pi == 2))
                    return ins
                pe_f5 = S.put("tensor", _dmm, "PE")
            for d in range(NCH):
                par = d % 2
                S.wait("sync", "VE", ve_xpar[par])
                S.wait("sync", "ST", S.cnt["ST"])
                S.dma("sync", xga[par][:, :],
                      x2d[s_, 128 * d:128 * (d + 1), t0:t0 + G], f"LDW{par}")
                S.wait("vector", "PE", pe_f5)
                S.wait("vector", f"LDW{par}")
                def _yev5(e, s_=s_, d=d, par=par):
                    return e.scalar_tensor_tensor(
                        out=xga[par][:, :], in0=psF[:, 512 * d:512 * (d + 1)],
                        scalar=g2vec[:, 2 * d + s_:2 * d + s_ + 1],
                        in1=xga[par][:, :], op0=ALU.mult, op1=ALU.add)
                ve_xpar[par] = S.put("vector", _yev5, "VE")
                S.wait("sync", "VE", ve_xpar[par])
                S.dma("sync", yTd[s_, 128 * d:128 * (d + 1), t0:t0 + G],
                      xga[par][:, :], "ST")
            ve_x2used = S.cnt["VE"]

        if DEBUG_OUT:
            S.wait("sync", "VE", S.cnt["VE"])
            S.dma("sync", scald[0:1, 0:16], rows[0:1, 0:16], "ST")
        S.wait("sync", "LD", S.cnt["LD"])
        S.wait("sync", "ST", S.cnt["ST"])
        S.wait("sync", "WS", S.cnt["WS"])
        S.wait("gpsimd", "GD", S.cnt["GD"])
        S.wait("gpsimd", "PS0", S.cnt["PS0"])
        S.wait("gpsimd", "PS1", S.cnt["PS1"])
        S.flush(block)
        f34.close()
        ffn.close()
        S.check()
    return nc



def prep_core(inputs, c):
    import ml_dtypes
    f32 = np.float32
    x = np.asarray(inputs["L0_attn_x"])[2 * c:2 * c + 2]
    ind2 = np.zeros((128, 2), f32)
    ind2[0:64, 0] = 1.0
    ind2[64:128, 1] = 1.0
    indq = np.zeros((16, 1024), f32)
    for ch in range(8):
        for mm_ in range(128):
            indq[2 * ch + mm_ // 64, ch * 128 + mm_] = 1.0
    ind16 = np.zeros((128, 128), f32)
    for ch in range(8):
        for p in range(128):
            ind16[p, ch * 16 + 2 * ch + p // 64] = 1.0
    perm = PERM
    colperm = np.concatenate([np.arange(64 * h, 64 * h + 64) for h in perm])
    m = {
        "xt": x.transpose(0, 2, 1),
        "wqT": np.asarray(inputs["L0_attn_wq"]).T[:, colperm],
        "wkT": np.asarray(inputs["L0_attn_wk"]).T,
        "wvT": np.asarray(inputs["L0_attn_wv"]).T,
        "woT": np.asarray(inputs["L0_attn_wo"]).T[colperm, :],
        "wup": np.asarray(inputs["L0_ffn_w_up"]),
        "wgate": np.asarray(inputs["L0_ffn_w_gate"]),
        "wdown": np.asarray(inputs["L0_ffn_w_down"]),
        "wmod": np.asarray(inputs["L0_wmod"]),
        "cTd": np.asarray(inputs["L0_c"])[2 * c:2 * c + 2].T,
        "qnwd": np.tile(np.asarray(inputs["L0_attn_q_norm_w"]).reshape(-1), 2)[:, None],
        "knwd": np.tile(np.asarray(inputs["L0_attn_k_norm_w"]).reshape(-1), 2)[:, None],
        "identfd": np.eye(128, dtype=f32),
        "identbd": np.eye(128, dtype=ml_dtypes.bfloat16),
        "ind2d": ind2,
        "p2d": ind2.T,
        "indqd": indq,
        "ind16d": ind16,
    }
    return {k: np.ascontiguousarray(np.asarray(v, dtype=v.dtype if hasattr(v, "dtype") else f32))
            for k, v in m.items()}


_cache = {}


def kernel(**inputs):
    key = (PHASES, DEBUG_OUT)
    if key not in _cache:
        _cache[key] = build_program()
    nc = _cache[key]
    in_maps = [prep_core(inputs, c) for c in range(8)]
    res = run_bass_kernel_spmd(nc, in_maps, core_ids=list(range(8)))
    outs = [res.results[c]["yTd"].transpose(0, 2, 1) for c in range(8)]
    out = np.concatenate(outs, axis=0).astype(np.float32)
    kernel.dbg = res.results
    return out


# version 05
# speedup vs baseline: 1.5190x; 1.0543x over previous
import os
import numpy as np
from contextlib import ExitStack

import concourse.bass as bass
import concourse.mybir as mybir
from concourse.bass_utils import run_bass_kernel_spmd

F32 = mybir.dt.float32
BF16 = mybir.dt.bfloat16
I16 = mybir.dt.int16
ALU = mybir.AluOpType
AF = mybir.ActivationFunctionType
AX = mybir.AxisListType

MAGIC = float(np.float32(1.5 * 2 ** 23))
DEBUG_OUT = False
PHASES = "full"
KSTOP = ""

DM, NCH, T, NSEQ, NG, G, NQT = 1024, 8, 2048, 2, 4, 512, 4
H, KVH, HD, DFF, NOCH = 16, 4, 64, 3072, 24
TOK = NSEQ * T
NGT = NSEQ * NG
PERM = [0, 4, 1, 5, 2, 6, 3, 7, 8, 12, 9, 13, 10, 14, 11, 15]
ENGS = ("sync", "tensor", "vector", "scalar", "gpsimd")
SEM = {}


class Sched:
    def __init__(self):
        self.streams = {k: [] for k in ENGS}
        self.meta = {k: [] for k in ENGS}
        self.cnt = {}

    def newsem(self, name):
        self.cnt[name] = 0

    def wait(self, eng, sem, val=None):
        v = self.cnt[sem] if val is None else val
        if v > 0:
            self.streams[eng].append(lambda e, s=sem, v=v: e.wait_ge(SEM[s], v))
            self.meta[eng].append(("wait", sem, v, eng))
        return v

    def put(self, eng, build, sem=None, inc=1):
        if sem is None:
            self.streams[eng].append(lambda e, b=build: b(e))
            self.meta[eng].append(("op", None, 0, eng))
            return None
        self.cnt[sem] += inc
        v = self.cnt[sem]
        self.streams[eng].append(lambda e, b=build, s=sem, i=inc: b(e).then_inc(SEM[s], i))
        self.meta[eng].append(("inc", sem, v, eng))
        return v

    def chain(self, eng, builds, sem):
        last = None
        for b in builds:
            if last is not None:
                self.wait(eng, sem, last)
            last = self.put(eng, b, sem)
        return last

    def dma(self, eng, out, in_, sem):
        return self.put(eng, lambda e, o=out, i=in_: e.dma_start(out=o, in_=i), sem, 16)

    def flush(self, block):
        for eng in ENGS:
            st = self.streams[eng]
            if st:
                getattr(block, eng)(lambda e, st=list(st): [f(e) for f in st])
                self.streams[eng] = []

    def check(self):
        pos = {k: 0 for k in ENGS}
        cur = {s: 0 for s in self.cnt}
        progress = True
        while progress:
            progress = False
            for eng in ENGS:
                while pos[eng] < len(self.meta[eng]):
                    kind, sem, v, _ = self.meta[eng][pos[eng]]
                    if kind == "wait" and cur[sem] < v:
                        break
                    if kind == "inc":
                        cur[sem] = max(cur[sem], v)
                    pos[eng] += 1
                    progress = True
        stuck = {k: (pos[k], len(self.meta[k])) for k in ENGS if pos[k] < len(self.meta[k])}
        if stuck:
            detail = {k: self.meta[k][pos[k]] for k in stuck}
            raise RuntimeError(f"deadlock: {stuck}; blocked on {detail}; sems {cur}")


def build_program():
    nc = bass.Bass("TRN2", target_bir_lowering=False, detect_race_conditions=False)
    S = Sched()
    for s in ("LD", "LDC", "LDK", "LDW0", "LDW1", "LW0", "LW1", "PS0", "PS1",
              "PL0", "PL1", "WD0", "WD1", "WS", "ST", "GD", "XL", "YS",
              "PE", "VE", "SE", "GE", "CC"):
        S.newsem(s)

    ei = lambda n, sh, dt=F32: nc.dram_tensor(n, sh, dt, kind="ExternalInput").ap()
    xt = ei("xt", [NSEQ, DM, T])
    wqT, wkT = ei("wqT", [DM, DM]), ei("wkT", [DM, KVH * HD])
    wvT, woT = ei("wvT", [DM, KVH * HD]), ei("woT", [DM, DM])
    wup, wgate = ei("wup", [DFF, DM]), ei("wgate", [DFF, DM])
    wdown, wmod = ei("wdown", [DM, DFF]), ei("wmod", [DM, 6 * DM])
    cTd = ei("cTd", [DM, NSEQ])
    qnwd, knwd = ei("qnwd", [128, 1]), ei("knwd", [128, 1])
    identfd, identbd = ei("identfd", [128, 128]), ei("identbd", [128, 128], BF16)
    ind2d, p2d, indqd = ei("ind2d", [128, 2]), ei("p2d", [2, 128]), ei("indqd", [16, 1024])
    ind16d = ei("ind16d", [128, 128])

    yTd = nc.dram_tensor("yTd", [NSEQ, DM, T], F32, kind="ExternalOutput").ap()
    dbg = "ExternalOutput" if DEBUG_OUT else "Internal"
    x2d = nc.dram_tensor("x2d", [NSEQ, DM, T], F32, kind=dbg).ap()
    xhd = nc.dram_tensor("xhd", [NSEQ, DM, T], F32, kind=dbg).ap()
    scald = nc.dram_tensor("scald", [1, 16], F32, kind=dbg).ap()
    pud = nc.dram_tensor("pud", [NOCH, 128, TOK], F32, kind=dbg).ap()
    pgd = nc.dram_tensor("pgd", [NOCH, 128, TOK], F32, kind=dbg).ap()
    rud = nc.dram_tensor("rud", [NOCH, 128, TOK], I16, kind="Internal").ap()
    p3d = nc.dram_tensor("p3d", [NOCH, 128, TOK], F32, kind=dbg).ap()
    p4d = nc.dram_tensor("p4d", [NOCH, 128, TOK], F32, kind=dbg).ap()
    wdThd = nc.dram_tensor("wdThd", [NOCH, 128, DM], BF16, kind="Internal").ap()
    wdTld = nc.dram_tensor("wdTld", [NOCH, 128, DM], BF16, kind="Internal").ap()
    mrst_d = nc.dram_tensor("mrst_d", [NSEQ, 2, T], F32, kind="Internal").ap()
    srd = nc.dram_tensor("srd", [16, 128], F32, kind="Internal").ap()
    qdbg = nc.dram_tensor("qdbg", [NCH, 128, G], F32, kind=dbg).ap()
    kdbg = nc.dram_tensor("kdbg", [2, 128, T], F32, kind=dbg).ap()
    ydbg = nc.dram_tensor("ydbg", [128, NCH * G], F32, kind=dbg).ap()
    qsdbg = nc.dram_tensor("qsdbg", [16, G], F32, kind=dbg).ap()
    wodbg = nc.dram_tensor("wodbg", [128, 2048], F32, kind=dbg).ap()
    aodbg = nc.dram_tensor("aodbg", [NCH, 128, G], F32, kind=dbg).ap()
    moddbg1 = nc.dram_tensor("moddbg1", [128, 96], F32, kind=dbg).ap()
    moddbg2 = nc.dram_tensor("moddbg2", [128, 96], F32, kind=dbg).ap()
    ydbg2 = nc.dram_tensor("ydbg2", [128, NCH * G], F32, kind=dbg).ap()
    ccin = [nc.dram_tensor(f"cc{i}in", [1, 8], F32).ap() for i in range(4)]
    ccout = [nc.dram_tensor(f"cc{i}out", [1, 8], F32).ap() for i in range(4)]

    es = ExitStack()
    with es:
        for name in S.cnt:
            SEM[name] = es.enter_context(nc.semaphore(name))
        block = es.enter_context(nc.Block())
        sbuf = lambda n, sh, dt=F32: es.enter_context(nc.sbuf_tensor(n, sh, dt))

        ones128 = sbuf("ones128", [128, 1])
        ones1 = sbuf("ones1", [1, 128])
        csts = sbuf("csts", [128, 8])
        qnw_t, knw_t = sbuf("qnw_t", [128, 1]), sbuf("knw_t", [128, 1])
        identf_t = sbuf("identf_t", [128, 128])
        identb_t = sbuf("identb_t", [128, 128], BF16)
        ind2_t, p2_t = sbuf("ind2_t", [128, 2]), sbuf("p2_t", [2, 128])
        modT = sbuf("modT", [128, 96])
        siluc, ctile = sbuf("siluc", [128, 16]), sbuf("ctile", [128, 16])
        rows = sbuf("rows", [1, 16])
        z8 = sbuf("z8", [1, 8])
        vecb = sbuf("vecb", [128, 8])
        avec, bvec = sbuf("avec", [128, 16]), sbuf("bvec", [128, 16])
        g2vec = sbuf("g2vec", [128, 16])
        maxacc = sbuf("maxacc", [128, 8])
        bufD = sbuf("bufD", [128, 2 * G])
        att = ExitStack()
        sba = lambda n, sh, dt=F32: att.enter_context(nc.sbuf_tensor(n, sh, dt))
        indq_t = sba("indq_t", [16, 1024])
        ind16_t = sba("ind16_t", [128, 128])
        hl_lo, hl_hi = sba("hl_lo", [1, 128]), sba("hl_hi", [1, 128])
        mask_diag, mask_far = sba("mask_diag", [128, 128]), sba("mask_far", [128, 128])
        mrow, vrow, rrow = sba("mrow", [1, G]), sba("vrow", [1, G]), sba("rrow", [1, G])

        cst = lambda i: csts[:, i:i + 1]

        for dst, src in ((qnw_t, qnwd), (knw_t, knwd), (identf_t, identfd),
                         (identb_t, identbd), (ind2_t, ind2d), (p2_t, p2d),
                         (indq_t, indqd), (ind16_t, ind16d)):
            S.dma("sync", dst[:, :], src[:, :], "LDK")
        S.dma("sync", ctile[:, :].rearrange("p (a s) -> p a s", a=NCH),
              cTd.rearrange("(a p) s -> p a s", p=128), "LDK")
        ld_consts = S.cnt["LDK"]

        def _consts(e):
            e.memset(ones128[:, :], 1.0)
            e.memset(ones1[:, :], 1.0)
            for i, v in enumerate((0.125, 1e-5, 1e-6, 1.0 / 64.0, -0.5, 1.0 / 1024.0,
                                   float(np.float32(1.0 / 7.0)),
                                   float(np.float32(1.0 / 32767.0)))):
                e.memset(csts[:, i:i + 1], v)
            e.memset(mask_diag[:, :], 1.0)
            e.memset(mask_far[:, :], 1.0)
            e.memset(z8[:, :], 0.0)
            e.memset(maxacc[:, :], 0.0)
            e.memset(hl_lo[0:1, 0:64], 1.0)
            e.memset(hl_lo[0:1, 64:128], 0.0)
            e.memset(hl_hi[0:1, 0:64], 0.0)
            e.memset(hl_hi[0:1, 64:128], 1.0)
            return e.engine_nop()
        ve_consts = S.put("vector", _consts, "VE")

        S.wait("gpsimd", "VE", ve_consts)
        S.put("gpsimd", lambda e: e.affine_select(
            mask_diag[:, :], mask_diag[:, :], pattern=[[1, 128]],
            compare_op=ALU.is_ge, fill=0.0, channel_multiplier=-1), "GE")
        ge_masks = S.put("gpsimd", lambda e: e.affine_select(
            mask_far[:, :], mask_far[:, :], pattern=[[-1, 128]],
            compare_op=ALU.is_ge, fill=0.0, channel_multiplier=1, base=-1), "GE")

        S.wait("scalar", "LDK", ld_consts)
        se_sig = S.put("scalar", lambda e: e.activation(
            out=siluc[:, :], in_=ctile[:, :], func=AF.Sigmoid), "SE")
        S.wait("vector", "SE", se_sig)
        S.wait("vector", "LDK", ld_consts)
        ve_silu = S.put("vector", lambda e: e.tensor_mul(
            siluc[:, :], siluc[:, :], ctile[:, :]), "VE")

        ph0 = ExitStack()
        wmt = ph0.enter_context(nc.sbuf_tensor("wmt", [128, NCH * 128], F32))
        ps_mod = ph0.enter_context(nc.psum_tensor("ps_mod", [128, 2], F32))
        for oc in range(48):
            if oc > 0:
                S.wait("sync", "PE", S.cnt["PE"])
            for d in range(NCH):
                S.dma("sync", wmt[:, 128 * d:128 * (d + 1)],
                      wmod[128 * d:128 * (d + 1), 128 * oc:128 * (oc + 1)], "LDC")
            S.wait("tensor", "LDC")
            S.wait("tensor", "VE", ve_silu)
            if oc > 0:
                S.wait("tensor", "VE")
            def _mm(e, oc=oc):
                for d in range(NCH):
                    ins = e.matmul(ps_mod[:, :], wmt[:, 128 * d:128 * (d + 1)],
                                   siluc[:, 2 * d:2 * d + 2],
                                   start=(d == 0), stop=(d == NCH - 1))
                return ins
            S.put("tensor", _mm, "PE")
            S.wait("vector", "PE")
            S.put("vector", lambda e, oc=oc: e.tensor_copy(
                out=modT[:, 2 * oc:2 * oc + 2], in_=ps_mod[:, :]), "VE")

        ops = [lambda e: e.tensor_scalar(out=modT[:, 16:32], in0=modT[:, 16:32],
                                         scalar1=1.0, scalar2=None, op0=ALU.add),
               lambda e: e.tensor_scalar(out=modT[:, 64:80], in0=modT[:, 64:80],
                                         scalar1=1.0, scalar2=None, op0=ALU.add)]
        for c0 in (32, 80):
            sl = modT[:, c0:c0 + 16]
            t, m, m2 = g2vec[:, 0:16], avec[:, 0:16], bvec[:, 0:16]
            ops += [
                lambda e, sl=sl, t=t: e.tensor_scalar(out=t, in0=sl, scalar1=-1.0,
                                                      scalar2=None, op0=ALU.mult),
                lambda e, sl=sl, t=t: e.tensor_tensor(out=t, in0=sl, in1=t, op=ALU.max),
                lambda e, t=t: e.tensor_scalar(out=t, in0=t, scalar1=0.25,
                                               scalar2=None, op0=ALU.max),
                lambda e, sl=sl, m=m: e.tensor_scalar(out=m, in0=sl, scalar1=0.0,
                                                      scalar2=None, op0=ALU.is_ge),
                lambda e, m=m, m2=m2: e.tensor_scalar(out=m2, in0=m, scalar1=2.0,
                                                      scalar2=-1.0, op0=ALU.mult, op1=ALU.add),
                lambda e, sl=sl, t=t, m2=m2: e.tensor_mul(sl, t, m2),
            ]
        ve_mod = S.chain("vector", ops, "VE")
        if DEBUG_OUT:
            S.wait("sync", "VE", ve_mod)
            S.dma("sync", moddbg1[:, :], modT[:, :], "ST")
        S.flush(block)
        ph0.close()
        if KSTOP == "mod":
            S.wait("sync", "LDK", S.cnt["LDK"])
            S.wait("sync", "LDC", S.cnt["LDC"])
            S.wait("sync", "VE", S.cnt["VE"])
            S.flush(block)
            S.check()
            return nc

        def mod_ap(p, ch, s):
            c = (8 * p + ch) * 2 + s
            return modT[:, c:c + 1]

        bufA = sba("bufA", [128, NCH * G])
        bufB = sba("bufB", [128, NCH * G])
        bufC = sba("bufC", [128, NCH * G])
        ysb = sba("ysb", [128, NCH * G])
        kvp = [sba(f"kvp{i}", [128, T]) for i in range(2)]
        qp = [sba(f"qp{i}", [128, G]) for i in range(NCH)]
        vext = sba("vext", [128, 16 * 260])
        attnT = sba("attnT", [128, 640])
        qssr = sba("qssr", [16, G])
        srow16 = sba("srow16", [16, 128])
        recip16 = sba("recip16", [16, 128])
        sr65 = sba("sr65", [65, 2048])
        wk_s = sba("wk_s", [128, NCH * 256])
        wv_s = sba("wv_s", [128, NCH * 256])
        wq_s = sba("wq_s", [128, NCH * 1024])
        wo_g = sba("wo_g", [128, 2 * 1024])
        ps_sc = att.enter_context(nc.psum_tensor("ps_sc", [128, 640], F32))
        ps_y = att.enter_context(nc.psum_tensor("ps_y", [128, 2048], F32))
        ps_a = att.enter_context(nc.psum_tensor("ps_a", [128, 512], F32))
        ps_b = att.enter_context(nc.psum_tensor("ps_b", [128, 512], F32))

        Atile = lambda d: bufA[:, G * d:G * (d + 1)]
        Btile = lambda d: bufB[:, G * d:G * (d + 1)]
        Ctile = lambda d: bufC[:, G * d:G * (d + 1)]

        S.wait("sync", "PE", S.cnt["PE"])
        for d in range(NCH):
            S.dma("sync", wk_s[:, 256 * d:256 * (d + 1)], wkT[128 * d:128 * (d + 1), :], "LDC")
            S.dma("sync", wv_s[:, 256 * d:256 * (d + 1)], wvT[128 * d:128 * (d + 1), :], "LDC")
            S.dma("sync", wq_s[:, 1024 * d:1024 * (d + 1)], wqT[128 * d:128 * (d + 1), :], "LDC")
        ld_attw = S.cnt["LDC"]
        S.wait("vector", "VE", ve_mod)
        ve_vext1 = S.put("vector", lambda e: e.memset(
            vext[:, :].rearrange("p (t k c) -> p t k c", t=16, k=4)[:, :, :, 64:65], 1.0), "VE")

        def ln_stats(src_tiles, wait_ve=None):
            pe_mm = 0
            for d in range(NCH):
                half = bufD[:, G * (d % 2):G * (d % 2) + G]
                if wait_ve is not None and d == 0:
                    S.wait("scalar", "VE", wait_ve)
                if d >= 2:
                    S.wait("scalar", "PE", pe_mm_prev)
                se_sq = S.put("scalar", lambda e, d=d, half=half: e.activation(
                    out=half, in_=src_tiles[d], func=AF.Square), "SE")
                S.wait("tensor", "SE", se_sq)
                def _mm(e, d=d, half=half):
                    e.matmul(ps_sc[0:1, 0:G], ones128[:, :], src_tiles[d],
                             start=(d == 0), stop=(d == NCH - 1))
                    return e.matmul(ps_y[0:1, 0:G], ones128[:, :], half,
                                    start=(d == 0), stop=(d == NCH - 1))
                pe_mm_prev = pe_mm
                pe_mm = S.put("tensor", _mm, "PE")
            S.wait("vector", "PE", pe_mm)
            S.wait("vector", "GD")
            def _rows(e):
                e.tensor_scalar(out=mrow[:, :], in0=ps_sc[0:1, 0:G],
                                scalar1=cst(5)[0:1, :], scalar2=None, op0=ALU.mult)
                e.tensor_scalar(out=vrow[:, :], in0=ps_y[0:1, 0:G],
                                scalar1=cst(5)[0:1, :], scalar2=None, op0=ALU.mult)
                e.tensor_mul(rrow[:, :], mrow[:, :], mrow[:, :])
                return e.tensor_sub(vrow[:, :], vrow[:, :], rrow[:, :])
            ve_rows = S.put("vector", _rows, "VE")
            S.wait("scalar", "VE", ve_rows)
            def _rst(e):
                e.activation(out=rrow[:, :], in_=vrow[:, :], func=AF.Ln,
                             scale=1.0, bias=cst(1)[0:1, :])
                return e.activation(out=rrow[:, :], in_=rrow[:, :], func=AF.Exp,
                                    scale=cst(4)[0:1, :])
            return S.put("scalar", _rst, "SE"), ve_rows

        def bcast_mr(m_ap, r_ap, wait_se=None, wait_ve=None):
            if wait_se is not None:
                S.wait("tensor", "SE", wait_se)
            if wait_ve is not None:
                S.wait("tensor", "VE", wait_ve)
            def _bc(e):
                e.matmul(ps_sc[:, 0:G], ones1[:, :], m_ap, start=True, stop=True)
                return e.matmul(ps_y[:, 0:G], ones1[:, :], r_ap, start=True, stop=True)
            return S.put("tensor", _bc, "PE")

        def h1_build(x_tiles, out_tiles, s, pe_bc):
            S.wait("vector", "PE", pe_bc)
            def _h1(e, s=s):
                for d in range(NCH):
                    e.tensor_sub(out_tiles[d], x_tiles[d], ps_sc[:, 0:G])
                    e.tensor_mul(out_tiles[d], out_tiles[d], ps_y[:, 0:G])
                    ins = e.tensor_scalar(out=out_tiles[d], in0=out_tiles[d],
                                          scalar1=mod_ap(1, d, s), scalar2=mod_ap(0, d, s),
                                          op0=ALU.mult, op1=ALU.add)
                return ins
            return S.put("vector", _h1, "VE")

        prev = {"VE": 0, "PE": 0}

        A2STOPS = ('a2g1', 'a2q', 'a2att', 'a2sc', 'a2av', 'a2sr')
        seqs = [0] if KSTOP in ('a1', 'a2') or KSTOP in A2STOPS else list(range(NSEQ))
        for s in seqs:
            for g in range(NG):
                t0 = g * G
                S.wait("sync", "VE", prev["VE"])
                S.wait("sync", "PE", prev["PE"])
                for d in range(NCH):
                    S.dma("sync", Atile(d), xt[s, 128 * d:128 * (d + 1), t0:t0 + G], "LD")
                S.wait("scalar", "LD")
                S.wait("tensor", "LDC", ld_attw)
                S.wait("tensor", "LD")
                se_r, ve_rows = ln_stats([Atile(d) for d in range(NCH)],
                                         wait_ve=prev["VE"])
                S.wait("gpsimd", "SE", se_r)
                S.wait("gpsimd", "VE", ve_rows)
                S.dma("gpsimd", mrst_d[s, 0:1, t0:t0 + G], mrow[:, :], "GD")
                S.dma("gpsimd", mrst_d[s, 1:2, t0:t0 + G], rrow[:, :], "GD")
                pe_bc = bcast_mr(mrow[:, :], rrow[:, :], wait_se=se_r, wait_ve=ve_rows)
                ve_h1 = h1_build([Atile(d) for d in range(NCH)],
                                 [Btile(d) for d in range(NCH)], s, pe_bc)

                for kv in range(2):
                    S.wait("tensor", "VE", ve_h1)
                    if kv == 1:
                        S.wait("tensor", "VE")
                    def _kmm(e, kv=kv):
                        for d in range(NCH):
                            ins = e.matmul(ps_a[:, 0:G],
                                           wk_s[:, 256 * d + 128 * kv:256 * d + 128 * (kv + 1)],
                                           Btile(d), start=(d == 0), stop=(d == NCH - 1))
                        return ins
                    pe_k = S.put("tensor", _kmm, "PE")
                    S.wait("vector", "PE", pe_k)
                    ve_kev = S.put("vector", lambda e: e.tensor_copy(
                        out=bufD[:, 0:G], in_=ps_a[:, 0:G]), "VE")
                    S.wait("scalar", "VE", ve_kev)
                    se_ksq = S.put("scalar", lambda e: e.activation(
                        out=bufD[:, G:2 * G], in_=bufD[:, 0:G], func=AF.Square), "SE")
                    S.wait("tensor", "SE", se_ksq)
                    pe_kss = S.put("tensor", lambda e: e.matmul(
                        ps_b[0:2, 0:G], ind2_t[:, :], bufD[:, G:2 * G],
                        start=True, stop=True), "PE")
                    S.wait("vector", "PE", pe_kss)
                    ve_kssr = S.put("vector", lambda e: e.tensor_copy(
                        out=qssr[0:2, :], in_=ps_b[0:2, 0:G]), "VE")
                    S.wait("scalar", "VE", ve_kssr)
                    def _krst(e):
                        e.activation(out=qssr[0:2, :], in_=qssr[0:2, :], func=AF.Ln,
                                     scale=cst(3)[0:2, :], bias=cst(2)[0:2, :])
                        return e.activation(out=qssr[0:2, :], in_=qssr[0:2, :],
                                            func=AF.Exp, scale=cst(4)[0:2, :])
                    se_krst = S.put("scalar", _krst, "SE")
                    S.wait("tensor", "SE", se_krst)
                    pe_kbc = S.put("tensor", lambda e: e.matmul(
                        ps_b[:, 0:G], p2_t[:, :], qssr[0:2, :], start=True, stop=True), "PE")
                    S.wait("vector", "PE", pe_kbc)
                    def _knrm(e, kv=kv, t0=t0):
                        e.tensor_mul(bufD[:, 0:G], bufD[:, 0:G], ps_b[:, 0:G])
                        return e.tensor_scalar(
                            out=kvp[kv][:, t0:t0 + G], in0=bufD[:, 0:G],
                            scalar1=knw_t[:, :], scalar2=None, op0=ALU.mult)
                    S.put("vector", _knrm, "VE")

                for ts_ in range(4):
                    ti = g * 4 + ts_
                    S.wait("tensor", "VE")
                    def _vmm(e, ts_=ts_):
                        for d in range(NCH):
                            ins = e.matmul(ps_a[:, 0:256],
                                           Btile(d)[:, 128 * ts_:128 * (ts_ + 1)],
                                           wv_s[:, 256 * d:256 * (d + 1)],
                                           start=(d == 0), stop=(d == NCH - 1))
                        return ins
                    pe_v = S.put("tensor", _vmm, "PE")
                    S.wait("vector", "PE", pe_v)
                    S.wait("vector", "VE", ve_vext1)
                    def _vev(e, ti=ti):
                        dst = vext[:, 260 * ti:260 * (ti + 1)].rearrange(
                            "p (k c) -> p k c", k=4)[:, :, 0:64]
                        src = ps_a[:, 0:256].rearrange("p (k c) -> p k c", k=4)
                        return e.tensor_copy(out=dst, in_=src)
                    S.put("vector", _vev, "VE")
                prev = {"VE": S.cnt["VE"], "PE": S.cnt["PE"]}

            if KSTOP == "a1":
                break
            for g in [0] if KSTOP in A2STOPS else range(NG):
                t0 = g * G
                S.wait("sync", "VE", prev["VE"])
                S.wait("sync", "PE", prev["PE"])
                for d in range(NCH):
                    S.dma("sync", Atile(d), xt[s, 128 * d:128 * (d + 1), t0:t0 + G], "LD")
                S.wait("sync", "GD")
                S.wait("sync", "PE")
                S.dma("sync", mrow[:, :], mrst_d[s, 0:1, t0:t0 + G], "LD")
                S.dma("sync", rrow[:, :], mrst_d[s, 1:2, t0:t0 + G], "LD")
                for yc in range(NCH):
                    S.dma("sync", wo_g[:, 128 * yc:128 * (yc + 1)],
                          woT[128 * yc:128 * (yc + 1), 0:128], "LDW0")
                S.wait("tensor", "LD")
                pe_bc = bcast_mr(mrow[:, :], rrow[:, :], wait_ve=prev["VE"])
                ve_h1 = h1_build([Atile(d) for d in range(NCH)],
                                 [Btile(d) for d in range(NCH)], s, pe_bc)

                for qc in range(NCH):
                    S.wait("tensor", "VE", ve_h1)
                    if qc > 0:
                        S.wait("tensor", "VE")
                    def _qmm(e, qc=qc):
                        for d in range(NCH):
                            ins = e.matmul(ps_a[:, 0:G],
                                           wq_s[:, 1024 * d + 128 * qc:1024 * d + 128 * (qc + 1)],
                                           Btile(d), start=(d == 0), stop=(d == NCH - 1))
                        return ins
                    pe_q = S.put("tensor", _qmm, "PE")
                    S.wait("vector", "PE", pe_q)
                    ve_qev = S.put("vector", lambda e, qc=qc: e.tensor_copy(
                        out=Ctile(qc), in_=ps_a[:, 0:G]), "VE")
                    S.wait("scalar", "VE", ve_qev)
                    se_qsq = S.put("scalar", lambda e, qc=qc: e.activation(
                        out=bufD[:, 0:G], in_=Ctile(qc), func=AF.Square), "SE")
                    S.wait("tensor", "SE", se_qsq)
                    pe_qss = S.put("tensor", lambda e, qc=qc: e.matmul(
                        ps_b[0:16, 0:G], ind16_t[:, 16 * qc:16 * (qc + 1)], bufD[:, 0:G],
                        start=(qc == 0), stop=(qc == NCH - 1)), "PE")
                S.wait("vector", "PE", pe_qss)
                ve_qssr = S.put("vector", lambda e: e.tensor_copy(
                    out=qssr[:, :], in_=ps_b[0:16, 0:G]), "VE")
                S.wait("scalar", "VE")
                def _qrst(e):
                    e.activation(out=qssr[:, :], in_=qssr[:, :], func=AF.Ln,
                                 scale=cst(3)[0:16, :], bias=cst(2)[0:16, :])
                    return e.activation(out=qssr[:, :], in_=qssr[:, :],
                                        func=AF.Exp, scale=cst(4)[0:16, :])
                se_qrst = S.put("scalar", _qrst, "SE")
                for qc in range(NCH):
                    S.wait("tensor", "SE", se_qrst)
                    S.wait("tensor", "VE")
                    pe_qbc = S.put("tensor", lambda e, qc=qc: e.matmul(
                        ps_b[:, 0:G], indq_t[:, 128 * qc:128 * (qc + 1)],
                        qssr[:, :], start=True, stop=True), "PE")
                    S.wait("vector", "PE", pe_qbc)
                    def _qn(e, qc=qc):
                        e.tensor_mul(bufD[:, 0:G], Ctile(qc)[:, :], ps_b[:, 0:G])
                        return e.tensor_scalar(
                            out=qp[qc][:, :], in0=bufD[:, 0:G],
                            scalar1=qnw_t[:, :], scalar2=None, op0=ALU.mult)
                    S.put("vector", _qn, "VE")
                ve_qn = S.cnt["VE"]
                S.wait("vector", "GE", ge_masks)
                if KSTOP == "a2q":
                    continue

                for qt in range(NQT):
                    qi = g * NQT + qt
                    kts = list(range(max(0, qi - 4), qi + 1))
                    n = len(kts)
                    se_exp_prev = None
                    ve_mask_prev = None
                    pe_av_last = None
                    for hi in range(H + 1):
                        if hi < H:
                            r = hi
                            h = PERM[r]
                            S.wait("tensor", "VE", ve_qn)
                            if se_exp_prev is not None:
                                S.wait("tensor", "SE", se_exp_prev)
                            if hi == 0:
                                S.wait("tensor", "VE")
                                S.wait("tensor", "SE")
                            def _sc(e, h=h, r=r, qt=qt, kts=kts):
                                ko = 64 * ((h // 4) % 2)
                                for j, kt in enumerate(kts):
                                    ins = e.matmul(
                                        ps_sc[:, 128 * j:128 * (j + 1)],
                                        kvp[(h // 4) // 2][ko:ko + 64, 128 * kt:128 * (kt + 1)],
                                        qp[r // 2][64 * (r % 2):64 * (r % 2) + 64,
                                                   128 * qt:128 * (qt + 1)],
                                        start=True, stop=True)
                                return ins
                            pe_sc = S.put("tensor", _sc, "PE")
                        if hi > 0 and KSTOP != "a2sc":
                            rp = hi - 1
                            hpv = PERM[rp]
                            S.wait("tensor", "VE", ve_mask_prev)
                            def _av(e, rp=rp, hpv=hpv, kts=kts, n=n):
                                for j, kt in enumerate(kts):
                                    v0 = 260 * kt + 65 * (hpv // 4)
                                    ins = e.matmul(
                                        ps_y[0:65, 128 * rp:128 * (rp + 1)],
                                        vext[:, v0:v0 + 65],
                                        attnT[:, 128 * j:128 * (j + 1)],
                                        start=(j == 0), stop=(j == n - 1))
                                return ins
                            pe_av_last = S.put("tensor", _av, "PE")
                        if hi < H:
                            S.wait("scalar", "PE", pe_av_last if pe_av_last is not None else pe_sc)
                            se_exp_prev = S.put("scalar", lambda e, n=n: e.activation(
                                out=attnT[:, 0:128 * n], in_=ps_sc[:, 0:128 * n],
                                func=AF.Exp, scale=cst(0)), "SE")
                            S.wait("vector", "SE", se_exp_prev)
                            def _msk(e, n=n, qi=qi):
                                ins = e.tensor_mul(attnT[:, 128 * (n - 1):128 * n],
                                                   attnT[:, 128 * (n - 1):128 * n],
                                                   mask_diag[:, :])
                                if qi >= 4:
                                    ins = e.tensor_mul(attnT[:, 0:128],
                                                       attnT[:, 0:128], mask_far[:, :])
                                return ins
                            ve_mask_prev = S.put("vector", _msk, "VE")

                    if KSTOP in ("a2sc", "a2av"):
                        continue
                    S.wait("scalar", "PE", pe_av_last)
                    S.wait("scalar", "GD")
                    se_sr = S.put("scalar", lambda e: e.copy(
                        out=sr65[:, :], in_=ps_y[0:65, 0:2048]), "SE")
                    S.wait("gpsimd", "SE", se_sr)
                    S.dma("gpsimd", srd.rearrange("r q -> (r q)").rearrange("(p f) -> p f", p=1),
                          sr65[64:65, 0:2048], "GD")
                    S.wait("gpsimd", "GD")
                    S.dma("gpsimd", srow16[:, :], srd[:, :], "GD")
                    if KSTOP == "a2sr":
                        continue
                    def _yev(e, qt=qt):
                        for r in range(0, H, 2):
                            dst = ysb[0:64, 512 * (r // 2) + 128 * qt:512 * (r // 2) + 128 * (qt + 1)]
                            ins = e.tensor_copy(out=dst, in_=sr65[0:64, 128 * r:128 * (r + 1)])
                        return ins
                    S.wait("vector", "SE", se_sr)
                    se_yev = S.put("vector", _yev, "VE")
                    for r in range(1, H, 2):
                        S.dma("gpsimd",
                              ysb[64:128, 512 * (r // 2) + 128 * qt:512 * (r // 2) + 128 * (qt + 1)],
                              sr65[0:64, 128 * r:128 * (r + 1)], "GD")
                    gd_yev = S.cnt["GD"]
                    S.wait("vector", "GD", gd_yev)
                    ve_rec = S.put("vector", lambda e: e.reciprocal(
                        out=recip16[:, :], in_=srow16[:, :]), "VE")
                    for half in range(2):
                        S.wait("tensor", "VE", ve_rec)
                        if half == 0:
                            S.wait("tensor", "VE", se_yev)
                        else:
                            S.wait("tensor", "VE")
                        def _rb(e, half=half):
                            for qc in range(4 * half, 4 * half + 4):
                                ins = e.matmul(ps_b[:, 128 * (qc % 4):128 * (qc % 4 + 1)],
                                               indq_t[:, 128 * qc:128 * (qc + 1)],
                                               recip16[:, :], start=True, stop=True)
                            return ins
                        pe_rb = S.put("tensor", _rb, "PE")
                        S.wait("vector", "PE", pe_rb)
                        if half == 0:
                            S.wait("vector", "GD", gd_yev)
                        def _nrm(e, half=half, qt=qt):
                            for qc in range(4 * half, 4 * half + 4):
                                sl = ysb[:, 512 * qc + 128 * qt:512 * qc + 128 * (qt + 1)]
                                ins = e.tensor_mul(sl, sl, ps_b[:, 128 * (qc % 4):128 * (qc % 4 + 1)])
                            return ins
                        S.put("vector", _nrm, "VE")

                if KSTOP in ("a2att", "a2sc", "a2av", "a2sr"):
                    continue
                if DEBUG_OUT and s == 0 and g == 0:
                    S.wait("sync", "VE", S.cnt["VE"])
                    S.dma("sync", qsdbg[:, :], qssr[:, :], "ST")
                    for qc in range(NCH):
                        S.dma("sync", qdbg[qc], qp[qc][:, :], "ST")
                    for i in range(2):
                        S.dma("sync", kdbg[i], kvp[i][:, :], "ST")
                    S.dma("sync", ydbg[:, :], ysb[:, :], "ST")
                ve_norm_all = S.cnt["VE"]
                pe_o = None
                for dd in range(NCH):
                    if dd > 0:
                        S.wait("sync", "PE", pe_o)
                        for yc in range(NCH):
                            S.dma("sync", wo_g[:, 128 * yc:128 * (yc + 1)],
                                  woT[128 * yc:128 * (yc + 1), 128 * dd:128 * (dd + 1)], "LDW0")
                    S.wait("tensor", "VE", ve_norm_all)
                    if dd > 0:
                        S.wait("tensor", "VE")
                    S.wait("tensor", "LDW0", S.cnt["LDW0"])
                    def _omm(e, dd=dd):
                        for yc in range(NCH):
                            ins = e.matmul(ps_a[:, 0:G],
                                           wo_g[:, 128 * yc:128 * (yc + 1)],
                                           ysb[:, 512 * yc:512 * (yc + 1)],
                                           start=(yc == 0), stop=(yc == NCH - 1))
                        return ins
                    pe_o = S.put("tensor", _omm, "PE")
                    S.wait("vector", "PE", pe_o)
                    def _x2(e, d=dd, s=s):
                        return e.scalar_tensor_tensor(
                            out=Btile(d), in0=ps_a[:, 0:G], scalar=mod_ap(2, d, s),
                            in1=Atile(d), op0=ALU.mult, op1=ALU.add)
                    ve_x2 = S.put("vector", _x2, "VE")
                    S.wait("sync", "VE", ve_x2)
                    S.dma("sync", x2d[s, 128 * dd:128 * (dd + 1), t0:t0 + G], Btile(dd), "ST")

                se_r2, _ = ln_stats([Btile(d) for d in range(NCH)], wait_ve=S.cnt["VE"])
                pe_bc2 = bcast_mr(mrow[:, :], rrow[:, :], wait_se=se_r2)
                S.wait("vector", "PE", pe_bc2)
                def _xh(e, s=s):
                    for d in range(NCH):
                        e.tensor_sub(Ctile(d), Btile(d), ps_sc[:, 0:G])
                        e.tensor_mul(Ctile(d), Ctile(d), ps_y[:, 0:G])
                        e.tensor_scalar(out=bufD[:, 0:G], in0=Ctile(d),
                                        scalar1=mod_ap(4, d, s), scalar2=mod_ap(3, d, s),
                                        op0=ALU.mult, op1=ALU.add)
                        e.tensor_reduce(out=maxacc[:, 1:2], in_=bufD[:, 0:G],
                                        axis=AX.X, op=ALU.max, apply_absolute_value=True)
                        ins = e.tensor_tensor(out=maxacc[:, 0:1], in0=maxacc[:, 0:1],
                                              in1=maxacc[:, 1:2], op=ALU.max)
                    return ins
                ve_xh = S.put("vector", _xh, "VE")
                S.wait("sync", "VE", ve_xh)
                for d in range(NCH):
                    S.dma("sync", xhd[s, 128 * d:128 * (d + 1), t0:t0 + G], Ctile(d), "ST")
                prev = {"VE": S.cnt["VE"], "PE": S.cnt["PE"]}

        if KSTOP in ("a1", "a2", "a2q") or KSTOP in A2STOPS:
            S.wait("sync", "LD", S.cnt["LD"])
            S.wait("sync", "LDC", S.cnt["LDC"])
            S.wait("sync", "ST", S.cnt["ST"])
            S.wait("gpsimd", "GD", S.cnt["GD"])
            S.wait("sync", "VE", S.cnt["VE"])
            S.wait("sync", "PE", S.cnt["PE"])
            S.wait("sync", "SE", S.cnt["SE"])
            S.flush(block)
            att.close()
            S.check()
            return nc
        S.wait("tensor", "VE")
        pe_tr = S.put("tensor", lambda e: e.transpose(
            ps_b[0:1, 0:128], maxacc[:, 0:1], identf_t[:, :]), "PE")
        S.wait("vector", "PE", pe_tr)
        ve_mh = S.put("vector", lambda e: e.tensor_reduce(
            out=z8[0:1, 0:1], in_=ps_b[0:1, 0:128], axis=AX.X, op=ALU.max), "VE")
        S.wait("gpsimd", "VE", ve_mh)
        S.dma("gpsimd", ccin[0][:, :], z8[:, :], "GD")
        S.wait("gpsimd", "GD")
        cc1 = S.put("gpsimd", lambda e: e.collective_compute(
            "AllReduce", ALU.max, replica_groups=[list(range(8))],
            ins=[ccin[0]], outs=[ccout[0]]), "CC")
        S.wait("gpsimd", "CC", cc1)
        S.dma("gpsimd", z8[:, :], ccout[0][:, :], "GD")
        S.wait("vector", "GD")
        ve_sx = S.chain("vector", [
            lambda e: e.tensor_scalar(out=rows[0:1, 1:2], in0=z8[0:1, 0:1],
                                      scalar1=cst(7)[0:1, :], scalar2=1e-12,
                                      op0=ALU.mult, op1=ALU.max),
            lambda e: e.reciprocal(out=rows[0:1, 2:3], in_=rows[0:1, 1:2]),
        ], "VE")
        S.wait("tensor", "VE", ve_sx)
        pe_sxb = S.put("tensor", lambda e: e.matmul(
            ps_b[:, 0:1], ones1[:, :], rows[0:1, 2:3], start=True, stop=True), "PE")
        S.wait("vector", "PE", pe_sxb)
        def _avec(e):
            for s in range(NSEQ):
                for d in range(NCH):
                    e.tensor_scalar(out=avec[:, 2 * d + s:2 * d + s + 1],
                                    in0=mod_ap(4, d, s),
                                    scalar1=vecb[:, 0:1], scalar2=None, op0=ALU.mult)
                    ins = e.tensor_scalar(out=bvec[:, 2 * d + s:2 * d + s + 1],
                                          in0=mod_ap(3, d, s),
                                          scalar1=vecb[:, 0:1], scalar2=None, op0=ALU.mult)
            return ins
        ve_avec = S.chain("vector", [
            lambda e: e.tensor_copy(out=vecb[:, 0:1], in_=ps_b[:, 0:1]),
            _avec,
        ], "VE")
        if DEBUG_OUT:
            S.wait("sync", "VE", ve_avec)
            S.dma("sync", scald[0:1, 0:8], z8[:, :], "ST")
            S.dma("sync", moddbg2[:, :], modT[:, :], "ST")

        S.flush(block)
        att.close()

        if PHASES == "attn":
            S.wait("sync", "ST", S.cnt["ST"])
            for s in range(NSEQ):
                S.dma("sync", yTd[s], x2d[s], "ST")
            S.wait("sync", "LD", S.cnt["LD"])
            S.wait("sync", "ST", S.cnt["ST"])
            S.wait("gpsimd", "GD", S.cnt["GD"])
            S.flush(block)
            S.check()
            return nc


        ffn = ExitStack()
        sbf = lambda n, sh, dt=F32: ffn.enter_context(nc.sbuf_tensor(n, sh, dt))
        xga = [sbf(f"xga{i}", [128, G]) for i in range(2)]
        wr = [sbf(f"wr{i}", [128, 1024]) for i in range(2)]
        wdq = bufD
        wh_n = sbf("wh_n", [128, 1024], BF16)
        wl_n = sbf("wl_n", [128, 1024], BF16)
        lhsTh = sbf("lhsTh", [128, 2 * 1024], BF16)
        lhsTl = sbf("lhsTl", [128, 2 * 1024], BF16)
        xg = sbf("xg", [128, TOK])
        p4_t = [sbf(f"p4_t{i}", [128, G]) for i in range(2)]
        roh = [sbf(f"roh{i}", [128, G], BF16) for i in range(2)]
        rol = [sbf(f"rol{i}", [128, G], BF16) for i in range(2)]
        sblk_g = sbf("sblk_g", [128, 96])
        rblk_g = sbf("rblk_g", [128, 96])
        rows2 = sbf("rows2", [2, 1])
        psF = ffn.enter_context(nc.psum_tensor("psF", [128, 4096], F32))
        f2s = ExitStack()
        rxh = f2s.enter_context(nc.sbuf_tensor("rxh", [128, NCH * TOK], BF16))
        rxl = f2s.enter_context(nc.sbuf_tensor("rxl", [128, NCH * TOK], BF16))
        pbig = [f2s.enter_context(nc.sbuf_tensor(f"pbig{i}", [128, 2048], F32))
                for i in range(2)]

        st_attn = S.cnt["ST"]

        ve_f2a = 0
        ve_xpar = [0, 0]
        S.wait("sync", "ST", st_attn)
        for g in range(NGT):
            s_, gg = divmod(g, NG)
            t0 = gg * G
            for d in range(NCH):
                par = (g * NCH + d) % 2
                S.wait("sync", "VE", ve_xpar[par])
                S.dma("sync", xga[par][:, :],
                      xhd[s_, 128 * d:128 * (d + 1), t0:t0 + G], f"LDW{par}")
                S.wait("vector", f"LDW{par}")
                if g == 0 and d == 0:
                    S.wait("vector", "VE", ve_avec)
                def _rx(e, s_=s_, g=g, d=d, par=par):
                    col = d * TOK + s_ * T + (g % NG) * G
                    t = xga[par][:, :]
                    e.tensor_scalar(out=t, in0=t,
                                    scalar1=avec[:, 2 * d + s_:2 * d + s_ + 1],
                                    scalar2=bvec[:, 2 * d + s_:2 * d + s_ + 1],
                                    op0=ALU.mult, op1=ALU.add)
                    e.tensor_scalar(out=t, in0=t, scalar1=MAGIC, scalar2=MAGIC,
                                    op0=ALU.add, op1=ALU.subtract)
                    e.tensor_copy(out=rxh[:, col:col + G], in_=t)
                    ins = e.tensor_tensor(out=rxl[:, col:col + G], in0=t,
                                          in1=rxh[:, col:col + G], op=ALU.subtract)
                    return ins
                ve_xpar[par] = S.put("vector", _rx, "VE")
        ve_f2a = S.cnt["VE"]

        wsrc = [wup, wgate]
        pdst = [pud, pgd]
        mcol = [2, 3]
        deq_ve = [0, 0]
        tr_pe = 0
        ev_se2 = [0, 0]
        red_mark = [0, 0]
        mm_pe = {0: 0, 1: 0}
        ps_dma = [0, 0]

        def dequant_1024(par_buf, ve_sem_tag="VE"):
            n_blk = 32
            S.put("vector", lambda e: e.tensor_reduce(
                out=sblk[:, 0:n_blk],
                in_=par_buf.rearrange("p (b c) -> p b c", b=n_blk),
                axis=AX.X, op=ALU.max, apply_absolute_value=True), "VE")
            S.chain("vector", [
                lambda e: e.tensor_scalar(out=sblk[:, 0:n_blk], in0=sblk[:, 0:n_blk],
                                          scalar1=cst(6), scalar2=1e-12,
                                          op0=ALU.mult, op1=ALU.max),
                lambda e: e.reciprocal(out=rblk[:, 0:n_blk], in_=sblk[:, 0:n_blk]),
                lambda e: e.tensor_tensor(
                    out=wdq[:, 0:1024].rearrange("p (b c) -> p b c", b=n_blk),
                    in0=par_buf.rearrange("p (b c) -> p b c", b=n_blk),
                    in1=rblk[:, 0:n_blk].unsqueeze(2).broadcast_to([128, n_blk, 32]),
                    op=ALU.mult),
            ], "VE")
            def _fin(e):
                e.tensor_scalar(out=wdq[:, 0:1024], in0=wdq[:, 0:1024], scalar1=MAGIC,
                                scalar2=MAGIC, op0=ALU.add, op1=ALU.subtract)
                e.tensor_tensor(
                    out=wdq[:, 0:1024].rearrange("p (b c) -> p b c", b=n_blk),
                    in0=wdq[:, 0:1024].rearrange("p (b c) -> p b c", b=n_blk),
                    in1=sblk[:, 0:n_blk].unsqueeze(2).broadcast_to([128, n_blk, 32]),
                    op=ALU.mult)
                e.tensor_copy(out=wh_n[:, :], in_=wdq[:, 0:1024])
                return e.tensor_tensor(out=wl_n[:, :], in0=wdq[:, 0:1024],
                                       in1=wh_n[:, :], op=ALU.subtract)
            return S.put("vector", _fin, "VE")

        sblk = sblk_g
        rblk = rblk_g

        first_mm = True
        for wi in range(2):
            for oc in range(NOCH):
                it = wi * NOCH + oc
                par = it % 2
                S.wait("sync", "VE", deq_ve[par])
                S.dma("sync", wr[par][:, :], wsrc[wi][128 * oc:128 * (oc + 1), :], f"LW{par}")
                S.wait("vector", f"LW{par}")
                S.wait("vector", "PE", tr_pe)
                deq_ve[par] = dequant_1024(wr[par][:, :])
                S.wait("tensor", "VE", deq_ve[par])
                S.wait("tensor", "SE", ev_se2[0])
                if not first_mm:
                    S.wait("tensor", "PE", mm_pe[0])
                def _tr(e):
                    for ih in range(NCH):
                        e.transpose(psF[:, 64 * ih:64 * ih + 64].bitcast(BF16),
                                    wh_n[:, 128 * ih:128 * (ih + 1)], identb_t[:, :])
                        ins = e.transpose(psF[:, 512 + 64 * ih:512 + 64 * ih + 64].bitcast(BF16),
                                          wl_n[:, 128 * ih:128 * (ih + 1)], identb_t[:, :])
                    return ins
                tr_pe = S.put("tensor", _tr, "PE")
                S.wait("scalar", "PE", tr_pe)
                S.wait("scalar", "PE", mm_pe[1])
                def _trev(e, par=par):
                    for ih in range(NCH):
                        e.copy(out=lhsTh[:, 1024 * par + 128 * ih:1024 * par + 128 * (ih + 1)],
                               in_=psF[:, 64 * ih:64 * ih + 64].bitcast(BF16))
                        ins = e.copy(out=lhsTl[:, 1024 * par + 128 * ih:1024 * par + 128 * (ih + 1)],
                                     in_=psF[:, 512 + 64 * ih:512 + 64 * ih + 64].bitcast(BF16))
                    return ins
                trev_se = S.put("scalar", _trev, "SE")

                for half in range(2):
                    S.wait("tensor", "SE", trev_se)
                    S.wait("tensor", "SE", ev_se2[half])
                    if first_mm:
                        S.wait("tensor", "VE", ve_f2a)
                    def _mm(e, par=par, half=half):
                        for pi, (lt, rx) in enumerate(((lhsTh, rxh), (lhsTh, rxl), (lhsTl, rxh))):
                            for ih in range(NCH):
                                step = pi * NCH + ih
                                for tt in range(4):
                                    ins = e.matmul(
                                        psF[:, 2048 * half + 512 * tt:2048 * half + 512 * (tt + 1)],
                                        lt[:, 1024 * par + 128 * ih:1024 * par + 128 * (ih + 1)],
                                        rx[:, ih * TOK + 2048 * half + 512 * tt:
                                           ih * TOK + 2048 * half + 512 * (tt + 1)],
                                        start=(step == 0), stop=(step == 23))
                        return ins
                    mm_pe[half] = S.put("tensor", _mm, "PE")
                    first_mm = False
                    sp = half
                    S.wait("scalar", "PE", mm_pe[half])
                    S.wait("scalar", f"PS{sp}", S.cnt[f"PS{sp}"])
                    S.wait("scalar", "VE", red_mark[sp])
                    def _evh(e, half=half, sp=sp):
                        for tt in range(4):
                            ins = e.copy(
                                out=pbig[sp][:, 512 * tt:512 * (tt + 1)],
                                in_=psF[:, 2048 * half + 512 * tt:
                                        2048 * half + 512 * (tt + 1)])
                        return ins
                    ev_se2[half] = S.put("scalar", _evh, "SE")
                    S.wait("vector", "SE", ev_se2[half])
                    ve_red = S.put("vector", lambda e, sp=sp: e.tensor_reduce(
                        out=maxacc[:, 1:2], in_=pbig[sp][:, :], axis=AX.X,
                        op=ALU.max, apply_absolute_value=True), "VE")
                    S.wait("vector", "VE", ve_red)
                    red_mark[sp] = S.chain("vector", [
                        lambda e, wi=wi: e.tensor_tensor(
                            out=maxacc[:, mcol[wi]:mcol[wi] + 1],
                            in0=maxacc[:, mcol[wi]:mcol[wi] + 1],
                            in1=maxacc[:, 1:2], op=ALU.max)], "VE")
                    S.wait("gpsimd", "SE", ev_se2[half])
                    ps_dma[sp] = S.dma(
                        "gpsimd",
                        pdst[wi][oc, :, 2048 * half:2048 * (half + 1)],
                        pbig[sp][:, :], f"PS{sp}")

        S.flush(block)
        f2s.close()
        S.wait("tensor", "VE")
        S.wait("tensor", "SE")
        pe_tr2 = S.put("tensor", lambda e: e.transpose(
            psF[0:2, 0:128], maxacc[:, 2:4], identf_t[:, :]), "PE")
        S.wait("vector", "PE", pe_tr2)
        ve_m2 = S.put("vector", lambda e: e.tensor_reduce(
            out=rows2[0:2, 0:1], in_=psF[0:2, 0:128], axis=AX.X, op=ALU.max), "VE")
        S.wait("gpsimd", "VE", ve_m2)
        S.wait("gpsimd", "PS0", S.cnt["PS0"])
        S.wait("gpsimd", "PS1", S.cnt["PS1"])
        S.wait("gpsimd", "VE", ve_consts)
        S.dma("gpsimd", ccin[1][0:1, 2:8], z8[0:1, 2:8], "GD")
        S.dma("gpsimd", ccin[1][0:1, 0:2], rows2[0:2, 0:1], "GD")
        S.wait("gpsimd", "GD")
        cc2 = S.put("gpsimd", lambda e: e.collective_compute(
            "AllReduce", ALU.max, replica_groups=[list(range(8))],
            ins=[ccin[1]], outs=[ccout[1]]), "CC")
        S.wait("gpsimd", "CC", cc2)
        S.dma("gpsimd", z8[:, :], ccout[1][:, :], "GD")
        S.wait("vector", "GD")
        ve_ar2 = S.chain("vector", [
            lambda e: e.tensor_scalar(out=rows[0:1, 3:4], in0=z8[0:1, 0:1],
                                      scalar1=rows[0:1, 1:2], scalar2=None, op0=ALU.mult),
            lambda e: e.tensor_scalar(out=rows[0:1, 4:5], in0=z8[0:1, 1:2],
                                      scalar1=rows[0:1, 1:2], scalar2=None, op0=ALU.mult),
            lambda e: e.tensor_scalar(out=rows[0:1, 5:6], in0=rows[0:1, 3:4],
                                      scalar1=cst(7)[0:1, :], scalar2=1e-12,
                                      op0=ALU.mult, op1=ALU.max),
            lambda e: e.tensor_scalar(out=rows[0:1, 8:9], in0=rows[0:1, 4:5],
                                      scalar1=cst(7)[0:1, :], scalar2=1e-12,
                                      op0=ALU.mult, op1=ALU.max),
            lambda e: e.reciprocal(out=rows[0:1, 6:7], in_=rows[0:1, 5:6]),
            lambda e: e.tensor_scalar(out=rows[0:1, 7:8], in0=rows[0:1, 6:7],
                                      scalar1=rows[0:1, 1:2], scalar2=None, op0=ALU.mult),
            lambda e: e.reciprocal(out=rows[0:1, 9:10], in_=rows[0:1, 8:9]),
            lambda e: e.tensor_scalar(out=rows[0:1, 10:11], in0=rows[0:1, 9:10],
                                      scalar1=rows[0:1, 1:2], scalar2=None, op0=ALU.mult),
        ], "VE")
        last_ve = ve_ar2
        for col, slot in ((1, 7), (2, 10), (3, 8)):
            S.wait("tensor", "VE", last_ve)
            pe_b = S.put("tensor", lambda e, slot=slot: e.matmul(
                psF[:, 0:1], ones1[:, :], rows[0:1, slot:slot + 1],
                start=True, stop=True), "PE")
            S.wait("vector", "PE", pe_b)
            last_ve = S.put("vector", lambda e, col=col: e.tensor_copy(
                out=vecb[:, col:col + 1], in_=psF[:, 0:1]), "VE")
        ve_vecb2 = last_ve

        f34 = ExitStack()
        wdres = f34.enter_context(nc.sbuf_tensor("wdres", [128, NOCH * 2048], BF16))
        f34b = ExitStack()
        BIGW = 2048
        bigA = [f34b.enter_context(nc.sbuf_tensor(f"bigA{i}", [128, BIGW], F32))
                for i in range(2)]
        bigB = [f34b.enter_context(nc.sbuf_tensor(f"bigB{i}", [128, BIGW], F32))
                for i in range(2)]
        bigC = [f34b.enter_context(nc.sbuf_tensor(f"bigC{i}", [128, BIGW], F32))
                for i in range(2)]
        cons3 = [0, 0]
        S.wait("sync", "PS0", S.cnt["PS0"])
        S.wait("sync", "PS1", S.cnt["PS1"])
        wd_tile = 0
        pe_tw = 0
        se_tw = 0
        deq_mark = 0
        for it in range(NOCH * 2):
            oc, hh = divmod(it, 2)
            par = it % 2
            c0 = BIGW * hh
            S.wait("sync", "VE", cons3[par])
            S.wait("sync", f"PS{par}", S.cnt[f"PS{par}"])
            S.dma("sync", bigA[par][:, :], pgd[oc, :, c0:c0 + BIGW], f"PL{par}")
            S.wait("vector", f"PL{par}")
            if it == 0:
                S.wait("vector", "VE", ve_vecb2)
            def _rg(e, par=par):
                e.tensor_scalar(out=bigA[par][:, :], in0=bigA[par][:, :],
                                scalar1=vecb[:, 2:3], scalar2=MAGIC,
                                op0=ALU.mult, op1=ALU.add)
                return e.tensor_scalar(out=bigA[par][:, :], in0=bigA[par][:, :],
                                       scalar1=MAGIC, scalar2=None, op0=ALU.subtract)
            ve_rg = S.put("vector", _rg, "VE")
            S.wait("scalar", "VE", ve_rg)
            se_sg2 = S.put("scalar", lambda e, par=par: e.activation(
                out=bigB[par][:, :], in_=bigA[par][:, :], func=AF.Sigmoid,
                scale=vecb[:, 3:4]), "SE")
            S.wait("vector", "SE", se_sg2)
            S.wait("vector", f"PS{par}", S.cnt[f"PS{par}"])
            def _s16p3(e, par=par):
                e.tensor_scalar(out=bigB[par][:, :], in0=bigB[par][:, :],
                                scalar1=65536.0, scalar2=MAGIC,
                                op0=ALU.mult, op1=ALU.add)
                e.tensor_scalar(out=bigB[par][:, :], in0=bigB[par][:, :],
                                scalar1=MAGIC, scalar2=65535.0,
                                op0=ALU.subtract, op1=ALU.min)
                e.tensor_tensor(out=bigC[par][:, :], in0=bigA[par][:, :],
                                in1=bigB[par][:, :], op=ALU.mult)
                return e.tensor_reduce(out=maxacc[:, 1:2], in_=bigC[par][:, :],
                                       axis=AX.X, op=ALU.max, apply_absolute_value=True)
            ve_p3 = S.put("vector", _s16p3, "VE")
            S.wait("vector", "VE", ve_p3)
            cons3[par] = S.chain("vector", [
                lambda e: e.tensor_tensor(out=maxacc[:, 4:5], in0=maxacc[:, 4:5],
                                          in1=maxacc[:, 1:2], op=ALU.max)], "VE")
            S.wait("gpsimd", "VE", cons3[par])
            S.dma("gpsimd", p3d[oc, :, c0:c0 + BIGW], bigC[par][:, :], f"PS{par}")

            if hh == 0 and wd_tile < NCH * 3:
                dt_, third = divmod(wd_tile, 3)
                wd_tile += 1
                S.wait("gpsimd", "VE", deq_mark)
                S.dma("gpsimd", wr[0][:, :],
                      wdown[128 * dt_:128 * (dt_ + 1), 1024 * third:1024 * (third + 1)], "WS")
                S.wait("vector", "WS")
                S.wait("vector", "PE", pe_tw)
                deq_mark = dequant_1024(wr[0][:, :])
                S.wait("tensor", "VE", deq_mark)
                S.wait("tensor", "SE", se_tw)
                def _trw8(e):
                    for j in range(8):
                        e.transpose(psF[:, 3072 + 64 * j:3072 + 64 * j + 64].bitcast(BF16),
                                    wh_n[:, 128 * j:128 * (j + 1)], identb_t[:, :])
                        ins = e.transpose(psF[:, 3584 + 64 * j:3584 + 64 * j + 64].bitcast(BF16),
                                          wl_n[:, 128 * j:128 * (j + 1)], identb_t[:, :])
                    return ins
                pe_tw = S.put("tensor", _trw8, "PE")
                S.wait("scalar", "PE", pe_tw)
                def _twev(e, dt_=dt_, third=third):
                    for j in range(8):
                        ihh = third * 8 + j
                        e.copy(out=wdres[:, 2048 * ihh + 128 * dt_:2048 * ihh + 128 * (dt_ + 1)],
                               in_=psF[:, 3072 + 64 * j:3072 + 64 * j + 64].bitcast(BF16))
                        ins = e.copy(
                            out=wdres[:, 2048 * ihh + 1024 + 128 * dt_:
                                      2048 * ihh + 1024 + 128 * (dt_ + 1)],
                            in_=psF[:, 3584 + 64 * j:3584 + 64 * j + 64].bitcast(BF16))
                    return ins
                se_tw = S.put("scalar", _twev, "SE")

        S.wait("tensor", "VE")
        pe_tr3 = S.put("tensor", lambda e: e.transpose(
            psF[0:1, 0:128], maxacc[:, 4:5], identf_t[:, :]), "PE")
        S.wait("vector", "PE", pe_tr3)
        ve_m3 = S.put("vector", lambda e: e.tensor_reduce(
            out=z8[0:1, 2:3], in_=psF[0:1, 0:128], axis=AX.X, op=ALU.max), "VE")
        ve_m3b = S.chain("vector", [
            lambda e: e.tensor_scalar(out=rows[0:1, 11:12], in0=rows[0:1, 8:9],
                                      scalar1=1.0 / 65536.0, scalar2=None, op0=ALU.mult),
            lambda e: e.tensor_scalar(out=z8[0:1, 0:1], in0=z8[0:1, 2:3],
                                      scalar1=rows[0:1, 11:12], scalar2=None, op0=ALU.mult),
        ], "VE")
        S.wait("gpsimd", "VE", ve_m3b)
        S.wait("gpsimd", "PS0", S.cnt["PS0"])
        S.wait("gpsimd", "PS1", S.cnt["PS1"])
        S.dma("gpsimd", ccin[2][:, :], z8[:, :], "GD")
        S.wait("gpsimd", "GD")
        cc3 = S.put("gpsimd", lambda e: e.collective_compute(
            "AllReduce", ALU.max, replica_groups=[list(range(8))],
            ins=[ccin[2]], outs=[ccout[2]]), "CC")
        S.wait("gpsimd", "CC", cc3)
        S.dma("gpsimd", z8[:, :], ccout[2][:, :], "GD")
        S.wait("vector", "GD")
        ve_ar3 = S.chain("vector", [
            lambda e: e.tensor_scalar(out=rows[0:1, 12:13], in0=z8[0:1, 0:1],
                                      scalar1=cst(7)[0:1, :], scalar2=1e-12,
                                      op0=ALU.mult, op1=ALU.max),
            lambda e: e.reciprocal(out=rows[0:1, 13:14], in_=rows[0:1, 12:13]),
            lambda e: e.tensor_scalar(out=rows[0:1, 14:15], in0=rows[0:1, 13:14],
                                      scalar1=rows[0:1, 11:12], scalar2=None, op0=ALU.mult),
        ], "VE")
        S.wait("tensor", "VE", ve_ar3)
        pe_b3 = S.put("tensor", lambda e: e.matmul(
            psF[:, 0:1], ones1[:, :], rows[0:1, 14:15], start=True, stop=True), "PE")
        S.wait("vector", "PE", pe_b3)
        ve_vecb4 = S.put("vector", lambda e: e.tensor_copy(
            out=vecb[:, 4:5], in_=psF[:, 0:1]), "VE")

        cons4 = [cons3[0], cons3[1]]
        for it in range(NOCH * 2):
            oc, hh = divmod(it, 2)
            par = it % 2
            c0 = BIGW * hh
            S.wait("sync", "VE", cons4[par])
            S.dma("sync", bigA[par][:, :], pud[oc, :, c0:c0 + BIGW], f"PL{par}")
            S.dma("sync", bigB[par][:, :], p3d[oc, :, c0:c0 + BIGW], f"PL{par}")
            S.wait("vector", f"PL{par}")
            if it == 0:
                S.wait("vector", "VE", ve_vecb4)
            S.wait("vector", f"PS{par}", S.cnt[f"PS{par}"])
            def _ruf(e, par=par):
                e.tensor_scalar(out=bigA[par][:, :], in0=bigA[par][:, :],
                                scalar1=vecb[:, 1:2], scalar2=MAGIC,
                                op0=ALU.mult, op1=ALU.add)
                e.tensor_scalar(out=bigA[par][:, :], in0=bigA[par][:, :],
                                scalar1=MAGIC, scalar2=None, op0=ALU.subtract)
                e.tensor_scalar(out=bigB[par][:, :], in0=bigB[par][:, :],
                                scalar1=vecb[:, 4:5], scalar2=MAGIC,
                                op0=ALU.mult, op1=ALU.add)
                e.tensor_scalar(out=bigB[par][:, :], in0=bigB[par][:, :],
                                scalar1=MAGIC, scalar2=None, op0=ALU.subtract)
                e.tensor_tensor(out=bigC[par][:, :], in0=bigA[par][:, :],
                                in1=bigB[par][:, :], op=ALU.mult)
                return e.tensor_reduce(out=maxacc[:, 1:2], in_=bigC[par][:, :],
                                       axis=AX.X, op=ALU.max, apply_absolute_value=True)
            ve_p4 = S.put("vector", _ruf, "VE")
            S.wait("vector", "VE", ve_p4)
            cons4[par] = S.chain("vector", [
                lambda e: e.tensor_tensor(out=maxacc[:, 5:6], in0=maxacc[:, 5:6],
                                          in1=maxacc[:, 1:2], op=ALU.max)], "VE")
            S.wait("gpsimd", "VE", cons4[par])
            S.dma("gpsimd", p4d[oc, :, c0:c0 + BIGW], bigC[par][:, :], f"PS{par}")

        S.wait("tensor", "VE")
        pe_tr4 = S.put("tensor", lambda e: e.transpose(
            psF[0:1, 0:128], maxacc[:, 5:6], identf_t[:, :]), "PE")
        S.wait("vector", "PE", pe_tr4)
        ve_m4 = S.put("vector", lambda e: e.tensor_reduce(
            out=z8[0:1, 2:3], in_=psF[0:1, 0:128], axis=AX.X, op=ALU.max), "VE")
        ve_m4b = S.chain("vector", [
            lambda e: e.tensor_scalar(out=rows[0:1, 15:16], in0=rows[0:1, 12:13],
                                      scalar1=rows[0:1, 5:6], scalar2=None, op0=ALU.mult),
            lambda e: e.tensor_scalar(out=z8[0:1, 0:1], in0=z8[0:1, 2:3],
                                      scalar1=rows[0:1, 15:16], scalar2=None, op0=ALU.mult),
        ], "VE")
        S.wait("gpsimd", "VE", ve_m4b)
        S.wait("gpsimd", "PS0", S.cnt["PS0"])
        S.wait("gpsimd", "PS1", S.cnt["PS1"])
        S.dma("gpsimd", ccin[3][:, :], z8[:, :], "GD")
        S.wait("gpsimd", "GD")
        cc4 = S.put("gpsimd", lambda e: e.collective_compute(
            "AllReduce", ALU.max, replica_groups=[list(range(8))],
            ins=[ccin[3]], outs=[ccout[3]]), "CC")
        S.wait("gpsimd", "CC", cc4)
        S.dma("gpsimd", z8[:, :], ccout[3][:, :], "GD")
        S.wait("vector", "GD")
        ve_ar4 = S.chain("vector", [
            lambda e: e.tensor_scalar(out=rows[0:1, 0:1], in0=z8[0:1, 0:1],
                                      scalar1=cst(7)[0:1, :], scalar2=1e-12,
                                      op0=ALU.mult, op1=ALU.max),
            lambda e: e.reciprocal(out=rows[0:1, 2:3], in_=rows[0:1, 0:1]),
            lambda e: e.tensor_scalar(out=rows[0:1, 3:4], in0=rows[0:1, 2:3],
                                      scalar1=rows[0:1, 15:16], scalar2=None, op0=ALU.mult),
        ], "VE")
        last_ve = ve_ar4
        for col, slot in ((5, 3), (6, 0)):
            S.wait("tensor", "VE", last_ve)
            pe_b = S.put("tensor", lambda e, slot=slot: e.matmul(
                psF[:, 0:1], ones1[:, :], rows[0:1, slot:slot + 1],
                start=True, stop=True), "PE")
            S.wait("vector", "PE", pe_b)
            last_ve = S.put("vector", lambda e, col=col: e.tensor_copy(
                out=vecb[:, col:col + 1], in_=psF[:, 0:1]), "VE")
        def _g2(e):
            for s2 in range(NSEQ):
                for d in range(NCH):
                    ins = e.tensor_scalar(out=g2vec[:, 2 * d + s2:2 * d + s2 + 1],
                                          in0=mod_ap(5, d, s2),
                                          scalar1=vecb[:, 6:7], scalar2=None, op0=ALU.mult)
            return ins
        ve_g2 = S.chain("vector", [lambda e: e.engine_nop(), _g2], "VE")

        f34b.close()
        pe_f5 = 0
        pe_par = [0, 0]
        ve_ro = [0, 0]
        ve_x2used = 0
        first_f5mm = True

        def xg_load(g):
            s_, gg = divmod(g, NG)
            t0 = gg * G
            for d in range(NCH):
                S.dma("sync", xg[:, 512 * d:512 * (d + 1)],
                      x2d[s_, 128 * d:128 * (d + 1), t0:t0 + G], "XL")

        S.wait("sync", "ST", S.cnt["ST"])
        xg_load(0)
        for g in range(NGT):
            s_, gg = divmod(g, NG)
            t0 = gg * G
            gcol = s_ * T + t0
            for oc in range(NOCH):
                par = oc % 2
                S.wait("gpsimd", "VE", ve_ro[par])
                S.dma("gpsimd", p4_t[par][:, :], p4d[oc, :, gcol:gcol + G], f"WD{par}")
                S.wait("vector", f"WD{par}")
                S.wait("vector", "VE", ve_g2)
                S.wait("vector", "PE", pe_par[par])
                def _ro(e, par=par):
                    e.tensor_scalar(out=bufD[:, 0:G], in0=p4_t[par][:, :],
                                    scalar1=vecb[:, 5:6], scalar2=MAGIC,
                                    op0=ALU.mult, op1=ALU.add)
                    e.tensor_scalar(out=bufD[:, 0:G], in0=bufD[:, 0:G],
                                    scalar1=MAGIC, scalar2=None, op0=ALU.subtract)
                    e.tensor_copy(out=roh[par][:, :], in_=bufD[:, 0:G])
                    return e.tensor_tensor(out=rol[par][:, :], in0=bufD[:, 0:G],
                                           in1=roh[par][:, :], op=ALU.subtract)
                ve_ro[par] = S.put("vector", _ro, "VE")
                S.wait("tensor", "VE", ve_ro[par])
                if first_f5mm:
                    S.wait("tensor", "SE", S.cnt["SE"])
                    first_f5mm = False
                if oc < 2 and g > 0:
                    S.wait("tensor", "VE", ve_x2used)
                def _dmm(e, par=par, oc=oc):
                    for pi, (lofs, rx) in enumerate(((0, roh), (0, rol), (1024, roh))):
                        for d in range(NCH):
                            ins = e.matmul(
                                psF[:, 512 * d:512 * (d + 1)],
                                wdres[:, 2048 * oc + lofs + 128 * d:
                                      2048 * oc + lofs + 128 * (d + 1)],
                                rx[par][:, :],
                                start=(oc == 0 and pi == 0), stop=(oc == NOCH - 1 and pi == 2))
                    return ins
                pe_f5 = S.put("tensor", _dmm, "PE")
                pe_par[par] = pe_f5
            S.wait("vector", "PE", pe_f5)
            S.wait("vector", "XL", S.cnt["XL"])
            S.wait("vector", "YS", S.cnt["YS"])
            for db in range(2):
                def _yev5(e, s_=s_, db=db):
                    for d in range(4 * db, 4 * db + 4):
                        ins = e.scalar_tensor_tensor(
                            out=xg[:, 512 * d:512 * (d + 1)],
                            in0=psF[:, 512 * d:512 * (d + 1)],
                            scalar=g2vec[:, 2 * d + s_:2 * d + s_ + 1],
                            in1=xg[:, 512 * d:512 * (d + 1)],
                            op0=ALU.mult, op1=ALU.add)
                    return ins
                ve_y = S.put("vector", _yev5, "VE")
                S.wait("sync", "VE", ve_y)
                for d in range(4 * db, 4 * db + 4):
                    S.dma("sync", yTd[s_, 128 * d:128 * (d + 1), t0:t0 + G],
                          xg[:, 512 * d:512 * (d + 1)], "YS")
            ve_x2used = S.cnt["VE"]
            if g + 1 < NGT:
                S.wait("sync", "YS", S.cnt["YS"])
                xg_load(g + 1)

        if DEBUG_OUT:
            S.wait("sync", "VE", S.cnt["VE"])
            S.dma("sync", scald[0:1, 0:16], rows[0:1, 0:16], "ST")
        S.wait("sync", "LD", S.cnt["LD"])
        S.wait("sync", "ST", S.cnt["ST"])
        S.wait("sync", "YS", S.cnt["YS"])
        S.wait("sync", "XL", S.cnt["XL"])
        S.wait("sync", "WS", S.cnt["WS"])
        S.wait("gpsimd", "GD", S.cnt["GD"])
        S.wait("gpsimd", "PS0", S.cnt["PS0"])
        S.wait("gpsimd", "PS1", S.cnt["PS1"])
        S.flush(block)
        f34.close()
        ffn.close()
        S.check()
    return nc



def prep_core(inputs, c):
    import ml_dtypes
    f32 = np.float32
    x = np.asarray(inputs["L0_attn_x"])[2 * c:2 * c + 2]
    ind2 = np.zeros((128, 2), f32)
    ind2[0:64, 0] = 1.0
    ind2[64:128, 1] = 1.0
    indq = np.zeros((16, 1024), f32)
    for ch in range(8):
        for mm_ in range(128):
            indq[2 * ch + mm_ // 64, ch * 128 + mm_] = 1.0
    ind16 = np.zeros((128, 128), f32)
    for ch in range(8):
        for p in range(128):
            ind16[p, ch * 16 + 2 * ch + p // 64] = 1.0
    perm = PERM
    colperm = np.concatenate([np.arange(64 * h, 64 * h + 64) for h in perm])
    m = {
        "xt": x.transpose(0, 2, 1),
        "wqT": np.asarray(inputs["L0_attn_wq"]).T[:, colperm],
        "wkT": np.asarray(inputs["L0_attn_wk"]).T,
        "wvT": np.asarray(inputs["L0_attn_wv"]).T,
        "woT": np.asarray(inputs["L0_attn_wo"]).T[colperm, :],
        "wup": np.asarray(inputs["L0_ffn_w_up"]),
        "wgate": np.asarray(inputs["L0_ffn_w_gate"]),
        "wdown": np.asarray(inputs["L0_ffn_w_down"]),
        "wmod": np.asarray(inputs["L0_wmod"]),
        "cTd": np.asarray(inputs["L0_c"])[2 * c:2 * c + 2].T,
        "qnwd": np.tile(np.asarray(inputs["L0_attn_q_norm_w"]).reshape(-1), 2)[:, None],
        "knwd": np.tile(np.asarray(inputs["L0_attn_k_norm_w"]).reshape(-1), 2)[:, None],
        "identfd": np.eye(128, dtype=f32),
        "identbd": np.eye(128, dtype=ml_dtypes.bfloat16),
        "ind2d": ind2,
        "p2d": ind2.T,
        "indqd": indq,
        "ind16d": ind16,
    }
    return {k: np.ascontiguousarray(np.asarray(v, dtype=v.dtype if hasattr(v, "dtype") else f32))
            for k, v in m.items()}


_cache = {}


def kernel(**inputs):
    key = (PHASES, DEBUG_OUT)
    if key not in _cache:
        _cache[key] = build_program()
    nc = _cache[key]
    in_maps = [prep_core(inputs, c) for c in range(8)]
    res = run_bass_kernel_spmd(nc, in_maps, core_ids=list(range(8)))
    outs = [res.results[c]["yTd"].transpose(0, 2, 1) for c in range(8)]
    out = np.concatenate(outs, axis=0).astype(np.float32)
    kernel.dbg = res.results
    return out


# version 06 (final)
# speedup vs baseline: 1.5969x; 1.0513x over previous
import os
import numpy as np
from contextlib import ExitStack

import concourse.bass as bass
import concourse.mybir as mybir
from concourse.bass_utils import run_bass_kernel_spmd

F32 = mybir.dt.float32
BF16 = mybir.dt.bfloat16
I16 = mybir.dt.int16
ALU = mybir.AluOpType
AF = mybir.ActivationFunctionType
AX = mybir.AxisListType

MAGIC = float(np.float32(1.5 * 2 ** 23))
DEBUG_OUT = False
PHASES = "full"
KSTOP = ""

DM, NCH, T, NSEQ, NG, G, NQT = 1024, 8, 2048, 2, 4, 512, 4
H, KVH, HD, DFF, NOCH = 16, 4, 64, 3072, 24
TOK = NSEQ * T
NGT = NSEQ * NG
PERM = [0, 4, 1, 5, 2, 6, 3, 7, 8, 12, 9, 13, 10, 14, 11, 15]
ENGS = ("sync", "tensor", "vector", "scalar", "gpsimd")
SEM = {}


class Sched:
    def __init__(self):
        self.streams = {k: [] for k in ENGS}
        self.meta = {k: [] for k in ENGS}
        self.cnt = {}

    def newsem(self, name):
        self.cnt[name] = 0

    def wait(self, eng, sem, val=None):
        v = self.cnt[sem] if val is None else val
        if v > 0:
            self.streams[eng].append(lambda e, s=sem, v=v: e.wait_ge(SEM[s], v))
            self.meta[eng].append(("wait", sem, v, eng))
        return v

    def put(self, eng, build, sem=None, inc=1):
        if sem is None:
            self.streams[eng].append(lambda e, b=build: b(e))
            self.meta[eng].append(("op", None, 0, eng))
            return None
        self.cnt[sem] += inc
        v = self.cnt[sem]
        self.streams[eng].append(lambda e, b=build, s=sem, i=inc: b(e).then_inc(SEM[s], i))
        self.meta[eng].append(("inc", sem, v, eng))
        return v

    def chain(self, eng, builds, sem):
        last = None
        for b in builds:
            if last is not None:
                self.wait(eng, sem, last)
            last = self.put(eng, b, sem)
        return last

    def dma(self, eng, out, in_, sem):
        return self.put(eng, lambda e, o=out, i=in_: e.dma_start(out=o, in_=i), sem, 16)

    def flush(self, block):
        for eng in ENGS:
            st = self.streams[eng]
            if st:
                getattr(block, eng)(lambda e, st=list(st): [f(e) for f in st])
                self.streams[eng] = []

    def check(self):
        pos = {k: 0 for k in ENGS}
        cur = {s: 0 for s in self.cnt}
        progress = True
        while progress:
            progress = False
            for eng in ENGS:
                while pos[eng] < len(self.meta[eng]):
                    kind, sem, v, _ = self.meta[eng][pos[eng]]
                    if kind == "wait" and cur[sem] < v:
                        break
                    if kind == "inc":
                        cur[sem] = max(cur[sem], v)
                    pos[eng] += 1
                    progress = True
        stuck = {k: (pos[k], len(self.meta[k])) for k in ENGS if pos[k] < len(self.meta[k])}
        if stuck:
            detail = {k: self.meta[k][pos[k]] for k in stuck}
            raise RuntimeError(f"deadlock: {stuck}; blocked on {detail}; sems {cur}")


def build_program():
    nc = bass.Bass("TRN2", target_bir_lowering=False, detect_race_conditions=False)
    S = Sched()
    for s in ("LD", "LDC", "LDK", "LDW0", "LDW1", "LW0", "LW1", "PS0", "PS1",
              "PL0", "PL1", "WD0", "WD1", "WS", "ST", "GD", "XL", "YS",
              "PE", "VE", "SE", "GE", "CC"):
        S.newsem(s)

    ei = lambda n, sh, dt=F32: nc.dram_tensor(n, sh, dt, kind="ExternalInput").ap()
    xt = ei("xt", [NSEQ, DM, T])
    wqT, wkT = ei("wqT", [DM, DM]), ei("wkT", [DM, KVH * HD])
    wvT, woT = ei("wvT", [DM, KVH * HD]), ei("woT", [DM, DM])
    wup, wgate = ei("wup", [DFF, DM]), ei("wgate", [DFF, DM])
    wdown, wmod = ei("wdown", [DM, DFF]), ei("wmod", [DM, 6 * DM])
    cTd = ei("cTd", [DM, NSEQ])
    qnwd, knwd = ei("qnwd", [128, 1]), ei("knwd", [128, 1])
    identfd, identbd = ei("identfd", [128, 128]), ei("identbd", [128, 128], BF16)
    ind2d, p2d, indqd = ei("ind2d", [128, 2]), ei("p2d", [2, 128]), ei("indqd", [16, 1024])
    ind16d = ei("ind16d", [128, 128])

    yTd = nc.dram_tensor("yTd", [NSEQ, DM, T], F32, kind="ExternalOutput").ap()
    dbg = "ExternalOutput" if DEBUG_OUT else "Internal"
    x2d = nc.dram_tensor("x2d", [NSEQ, DM, T], F32, kind=dbg).ap()
    xhd = nc.dram_tensor("xhd", [NSEQ, DM, T], F32, kind=dbg).ap()
    scald = nc.dram_tensor("scald", [1, 16], F32, kind=dbg).ap()
    pud = nc.dram_tensor("pud", [NOCH, 128, TOK], F32, kind=dbg).ap()
    pgd = nc.dram_tensor("pgd", [NOCH, 128, TOK], F32, kind=dbg).ap()
    rud = nc.dram_tensor("rud", [NOCH, 128, TOK], I16, kind="Internal").ap()
    p3d = nc.dram_tensor("p3d", [NOCH, 128, TOK], F32, kind=dbg).ap()
    p4d = nc.dram_tensor("p4d", [NOCH, 128, TOK], F32, kind=dbg).ap()
    wdThd = nc.dram_tensor("wdThd", [NOCH, 128, DM], BF16, kind="Internal").ap()
    wdTld = nc.dram_tensor("wdTld", [NOCH, 128, DM], BF16, kind="Internal").ap()
    mrst_d = nc.dram_tensor("mrst_d", [NSEQ, 2, T], F32, kind="Internal").ap()
    srd = nc.dram_tensor("srd", [16, 128], F32, kind="Internal").ap()
    qdbg = nc.dram_tensor("qdbg", [NCH, 128, G], F32, kind=dbg).ap()
    kdbg = nc.dram_tensor("kdbg", [2, 128, T], F32, kind=dbg).ap()
    ydbg = nc.dram_tensor("ydbg", [128, NCH * G], F32, kind=dbg).ap()
    qsdbg = nc.dram_tensor("qsdbg", [16, G], F32, kind=dbg).ap()
    wodbg = nc.dram_tensor("wodbg", [128, 2048], F32, kind=dbg).ap()
    aodbg = nc.dram_tensor("aodbg", [NCH, 128, G], F32, kind=dbg).ap()
    moddbg1 = nc.dram_tensor("moddbg1", [128, 96], F32, kind=dbg).ap()
    moddbg2 = nc.dram_tensor("moddbg2", [128, 96], F32, kind=dbg).ap()
    ydbg2 = nc.dram_tensor("ydbg2", [128, NCH * G], F32, kind=dbg).ap()
    ccin = [nc.dram_tensor(f"cc{i}in", [1, 8], F32).ap() for i in range(4)]
    ccout = [nc.dram_tensor(f"cc{i}out", [1, 8], F32).ap() for i in range(4)]

    es = ExitStack()
    with es:
        for name in S.cnt:
            SEM[name] = es.enter_context(nc.semaphore(name))
        block = es.enter_context(nc.Block())
        sbuf = lambda n, sh, dt=F32: es.enter_context(nc.sbuf_tensor(n, sh, dt))

        ones128 = sbuf("ones128", [128, 1])
        ones1 = sbuf("ones1", [1, 128])
        csts = sbuf("csts", [128, 8])
        qnw_t, knw_t = sbuf("qnw_t", [128, 1]), sbuf("knw_t", [128, 1])
        identf_t = sbuf("identf_t", [128, 128])
        identb_t = sbuf("identb_t", [128, 128], BF16)
        ind2_t, p2_t = sbuf("ind2_t", [128, 2]), sbuf("p2_t", [2, 128])
        modT = sbuf("modT", [128, 96])
        siluc, ctile = sbuf("siluc", [128, 16]), sbuf("ctile", [128, 16])
        rows = sbuf("rows", [1, 16])
        z8 = sbuf("z8", [1, 8])
        vecb = sbuf("vecb", [128, 8])
        avec, bvec = sbuf("avec", [128, 16]), sbuf("bvec", [128, 16])
        g2vec = sbuf("g2vec", [128, 16])
        maxacc = sbuf("maxacc", [128, 8])
        bufD = sbuf("bufD", [128, 2 * G])
        att = ExitStack()
        sba = lambda n, sh, dt=F32: att.enter_context(nc.sbuf_tensor(n, sh, dt))
        indq_t = sba("indq_t", [16, 1024])
        ind16_t = sba("ind16_t", [128, 128])
        hl_lo, hl_hi = sba("hl_lo", [1, 128]), sba("hl_hi", [1, 128])
        mask_diag, mask_far = sba("mask_diag", [128, 128]), sba("mask_far", [128, 128])
        mrow, vrow, rrow = sba("mrow", [1, G]), sba("vrow", [1, G]), sba("rrow", [1, G])

        cst = lambda i: csts[:, i:i + 1]

        for dst, src in ((qnw_t, qnwd), (knw_t, knwd), (identf_t, identfd),
                         (identb_t, identbd), (ind2_t, ind2d), (p2_t, p2d),
                         (indq_t, indqd), (ind16_t, ind16d)):
            S.dma("sync", dst[:, :], src[:, :], "LDK")
        S.dma("sync", ctile[:, :].rearrange("p (a s) -> p a s", a=NCH),
              cTd.rearrange("(a p) s -> p a s", p=128), "LDK")
        ld_consts = S.cnt["LDK"]

        def _consts(e):
            e.memset(ones128[:, :], 1.0)
            e.memset(ones1[:, :], 1.0)
            for i, v in enumerate((0.125, 1e-5, 1e-6, 1.0 / 64.0, -0.5, 1.0 / 1024.0,
                                   float(np.float32(1.0 / 7.0)),
                                   float(np.float32(1.0 / 32767.0)))):
                e.memset(csts[:, i:i + 1], v)
            e.memset(mask_diag[:, :], 1.0)
            e.memset(mask_far[:, :], 1.0)
            e.memset(z8[:, :], 0.0)
            e.memset(maxacc[:, :], 0.0)
            e.memset(hl_lo[0:1, 0:64], 1.0)
            e.memset(hl_lo[0:1, 64:128], 0.0)
            e.memset(hl_hi[0:1, 0:64], 0.0)
            e.memset(hl_hi[0:1, 64:128], 1.0)
            return e.engine_nop()
        ve_consts = S.put("vector", _consts, "VE")

        S.wait("gpsimd", "VE", ve_consts)
        S.put("gpsimd", lambda e: e.affine_select(
            mask_diag[:, :], mask_diag[:, :], pattern=[[1, 128]],
            compare_op=ALU.is_ge, fill=0.0, channel_multiplier=-1), "GE")
        ge_masks = S.put("gpsimd", lambda e: e.affine_select(
            mask_far[:, :], mask_far[:, :], pattern=[[-1, 128]],
            compare_op=ALU.is_ge, fill=0.0, channel_multiplier=1, base=-1), "GE")

        S.wait("scalar", "LDK", ld_consts)
        se_sig = S.put("scalar", lambda e: e.activation(
            out=siluc[:, :], in_=ctile[:, :], func=AF.Sigmoid), "SE")
        S.wait("vector", "SE", se_sig)
        S.wait("vector", "LDK", ld_consts)
        ve_silu = S.put("vector", lambda e: e.tensor_mul(
            siluc[:, :], siluc[:, :], ctile[:, :]), "VE")

        ph0 = ExitStack()
        wmt = ph0.enter_context(nc.sbuf_tensor("wmt", [128, NCH * 128], F32))
        ps_mod = ph0.enter_context(nc.psum_tensor("ps_mod", [128, 2], F32))
        for oc in range(48):
            if oc > 0:
                S.wait("sync", "PE", S.cnt["PE"])
            for d in range(NCH):
                S.dma("sync", wmt[:, 128 * d:128 * (d + 1)],
                      wmod[128 * d:128 * (d + 1), 128 * oc:128 * (oc + 1)], "LDC")
            S.wait("tensor", "LDC")
            S.wait("tensor", "VE", ve_silu)
            if oc > 0:
                S.wait("tensor", "VE")
            def _mm(e, oc=oc):
                for d in range(NCH):
                    ins = e.matmul(ps_mod[:, :], wmt[:, 128 * d:128 * (d + 1)],
                                   siluc[:, 2 * d:2 * d + 2],
                                   start=(d == 0), stop=(d == NCH - 1))
                return ins
            S.put("tensor", _mm, "PE")
            S.wait("vector", "PE")
            S.put("vector", lambda e, oc=oc: e.tensor_copy(
                out=modT[:, 2 * oc:2 * oc + 2], in_=ps_mod[:, :]), "VE")

        ops = [lambda e: e.tensor_scalar(out=modT[:, 16:32], in0=modT[:, 16:32],
                                         scalar1=1.0, scalar2=None, op0=ALU.add),
               lambda e: e.tensor_scalar(out=modT[:, 64:80], in0=modT[:, 64:80],
                                         scalar1=1.0, scalar2=None, op0=ALU.add)]
        for c0 in (32, 80):
            sl = modT[:, c0:c0 + 16]
            t, m, m2 = g2vec[:, 0:16], avec[:, 0:16], bvec[:, 0:16]
            ops += [
                lambda e, sl=sl, t=t: e.tensor_scalar(out=t, in0=sl, scalar1=-1.0,
                                                      scalar2=None, op0=ALU.mult),
                lambda e, sl=sl, t=t: e.tensor_tensor(out=t, in0=sl, in1=t, op=ALU.max),
                lambda e, t=t: e.tensor_scalar(out=t, in0=t, scalar1=0.25,
                                               scalar2=None, op0=ALU.max),
                lambda e, sl=sl, m=m: e.tensor_scalar(out=m, in0=sl, scalar1=0.0,
                                                      scalar2=None, op0=ALU.is_ge),
                lambda e, m=m, m2=m2: e.tensor_scalar(out=m2, in0=m, scalar1=2.0,
                                                      scalar2=-1.0, op0=ALU.mult, op1=ALU.add),
                lambda e, sl=sl, t=t, m2=m2: e.tensor_mul(sl, t, m2),
            ]
        ve_mod = S.chain("vector", ops, "VE")
        if DEBUG_OUT:
            S.wait("sync", "VE", ve_mod)
            S.dma("sync", moddbg1[:, :], modT[:, :], "ST")
        S.flush(block)
        ph0.close()
        if KSTOP == "mod":
            S.wait("sync", "LDK", S.cnt["LDK"])
            S.wait("sync", "LDC", S.cnt["LDC"])
            S.wait("sync", "VE", S.cnt["VE"])
            S.flush(block)
            S.check()
            return nc

        def mod_ap(p, ch, s):
            c = (8 * p + ch) * 2 + s
            return modT[:, c:c + 1]

        bufA = sba("bufA", [128, NCH * G])
        bufB = sba("bufB", [128, NCH * G])
        bufC = sba("bufC", [128, NCH * G])
        ysb = sba("ysb", [128, NCH * G])
        kvp = [sba(f"kvp{i}", [128, T]) for i in range(2)]
        qp = [sba(f"qp{i}", [128, G]) for i in range(NCH)]
        vext = sba("vext", [128, 16 * 260])
        attnT = sba("attnT", [128, 640])
        qssr = sba("qssr", [16, G])
        srow16 = sba("srow16", [16, 128])
        recip16 = sba("recip16", [16, 128])
        sr65 = sba("sr65", [65, 2048])
        wk_s = sba("wk_s", [128, NCH * 256])
        wv_s = sba("wv_s", [128, NCH * 256])
        wq_s = sba("wq_s", [128, NCH * 1024])
        wo_g = sba("wo_g", [128, 2 * 1024])
        ps_sc = att.enter_context(nc.psum_tensor("ps_sc", [128, 640], F32))
        ps_y = att.enter_context(nc.psum_tensor("ps_y", [128, 2048], F32))
        ps_a = att.enter_context(nc.psum_tensor("ps_a", [128, 512], F32))
        ps_b = att.enter_context(nc.psum_tensor("ps_b", [128, 512], F32))

        Atile = lambda d: bufA[:, G * d:G * (d + 1)]
        Btile = lambda d: bufB[:, G * d:G * (d + 1)]
        Ctile = lambda d: bufC[:, G * d:G * (d + 1)]

        S.wait("sync", "PE", S.cnt["PE"])
        for d in range(NCH):
            S.dma("sync", wk_s[:, 256 * d:256 * (d + 1)], wkT[128 * d:128 * (d + 1), :], "LDC")
            S.dma("sync", wv_s[:, 256 * d:256 * (d + 1)], wvT[128 * d:128 * (d + 1), :], "LDC")
            S.dma("sync", wq_s[:, 1024 * d:1024 * (d + 1)], wqT[128 * d:128 * (d + 1), :], "LDC")
        ld_attw = S.cnt["LDC"]
        S.wait("vector", "VE", ve_mod)
        ve_vext1 = S.put("vector", lambda e: e.memset(
            vext[:, :].rearrange("p (t k c) -> p t k c", t=16, k=4)[:, :, :, 64:65], 1.0), "VE")

        def ln_stats(src_tiles, wait_ve=None):
            pe_mm = 0
            for d in range(NCH):
                half = bufD[:, G * (d % 2):G * (d % 2) + G]
                if wait_ve is not None and d == 0:
                    S.wait("scalar", "VE", wait_ve)
                if d >= 2:
                    S.wait("scalar", "PE", pe_mm_prev)
                se_sq = S.put("scalar", lambda e, d=d, half=half: e.activation(
                    out=half, in_=src_tiles[d], func=AF.Square), "SE")
                S.wait("tensor", "SE", se_sq)
                def _mm(e, d=d, half=half):
                    e.matmul(ps_sc[0:1, 0:G], ones128[:, :], src_tiles[d],
                             start=(d == 0), stop=(d == NCH - 1))
                    return e.matmul(ps_y[0:1, 0:G], ones128[:, :], half,
                                    start=(d == 0), stop=(d == NCH - 1))
                pe_mm_prev = pe_mm
                pe_mm = S.put("tensor", _mm, "PE")
            S.wait("vector", "PE", pe_mm)
            S.wait("vector", "GD")
            def _rows(e):
                e.tensor_scalar(out=mrow[:, :], in0=ps_sc[0:1, 0:G],
                                scalar1=cst(5)[0:1, :], scalar2=None, op0=ALU.mult)
                e.tensor_scalar(out=vrow[:, :], in0=ps_y[0:1, 0:G],
                                scalar1=cst(5)[0:1, :], scalar2=None, op0=ALU.mult)
                e.tensor_mul(rrow[:, :], mrow[:, :], mrow[:, :])
                return e.tensor_sub(vrow[:, :], vrow[:, :], rrow[:, :])
            ve_rows = S.put("vector", _rows, "VE")
            S.wait("scalar", "VE", ve_rows)
            def _rst(e):
                e.activation(out=rrow[:, :], in_=vrow[:, :], func=AF.Ln,
                             scale=1.0, bias=cst(1)[0:1, :])
                return e.activation(out=rrow[:, :], in_=rrow[:, :], func=AF.Exp,
                                    scale=cst(4)[0:1, :])
            return S.put("scalar", _rst, "SE"), ve_rows

        def bcast_mr(m_ap, r_ap, wait_se=None, wait_ve=None):
            if wait_se is not None:
                S.wait("tensor", "SE", wait_se)
            if wait_ve is not None:
                S.wait("tensor", "VE", wait_ve)
            def _bc(e):
                e.matmul(ps_sc[:, 0:G], ones1[:, :], m_ap, start=True, stop=True)
                return e.matmul(ps_y[:, 0:G], ones1[:, :], r_ap, start=True, stop=True)
            return S.put("tensor", _bc, "PE")

        def h1_build(x_tiles, out_tiles, s, pe_bc):
            S.wait("vector", "PE", pe_bc)
            def _h1(e, s=s):
                for d in range(NCH):
                    e.tensor_sub(out_tiles[d], x_tiles[d], ps_sc[:, 0:G])
                    e.tensor_mul(out_tiles[d], out_tiles[d], ps_y[:, 0:G])
                    ins = e.tensor_scalar(out=out_tiles[d], in0=out_tiles[d],
                                          scalar1=mod_ap(1, d, s), scalar2=mod_ap(0, d, s),
                                          op0=ALU.mult, op1=ALU.add)
                return ins
            return S.put("vector", _h1, "VE")

        prev = {"VE": 0, "PE": 0}

        A2STOPS = ('a2g1', 'a2q', 'a2att', 'a2sc', 'a2av', 'a2sr')
        seqs = [0] if KSTOP in ('a1', 'a2') or KSTOP in A2STOPS else list(range(NSEQ))
        for s in seqs:
            for g in range(NG):
                t0 = g * G
                S.wait("sync", "VE", prev["VE"])
                S.wait("sync", "PE", prev["PE"])
                for d in range(NCH):
                    S.dma("sync", Atile(d), xt[s, 128 * d:128 * (d + 1), t0:t0 + G], "LD")
                S.wait("scalar", "LD")
                S.wait("tensor", "LDC", ld_attw)
                S.wait("tensor", "LD")
                se_r, ve_rows = ln_stats([Atile(d) for d in range(NCH)],
                                         wait_ve=prev["VE"])
                S.wait("gpsimd", "SE", se_r)
                S.wait("gpsimd", "VE", ve_rows)
                S.dma("gpsimd", mrst_d[s, 0:1, t0:t0 + G], mrow[:, :], "GD")
                S.dma("gpsimd", mrst_d[s, 1:2, t0:t0 + G], rrow[:, :], "GD")
                pe_bc = bcast_mr(mrow[:, :], rrow[:, :], wait_se=se_r, wait_ve=ve_rows)
                ve_h1 = h1_build([Atile(d) for d in range(NCH)],
                                 [Btile(d) for d in range(NCH)], s, pe_bc)

                for kv in range(2):
                    S.wait("tensor", "VE", ve_h1)
                    if kv == 1:
                        S.wait("tensor", "VE")
                    def _kmm(e, kv=kv):
                        for d in range(NCH):
                            ins = e.matmul(ps_a[:, 0:G],
                                           wk_s[:, 256 * d + 128 * kv:256 * d + 128 * (kv + 1)],
                                           Btile(d), start=(d == 0), stop=(d == NCH - 1))
                        return ins
                    pe_k = S.put("tensor", _kmm, "PE")
                    S.wait("vector", "PE", pe_k)
                    ve_kev = S.put("vector", lambda e: e.tensor_copy(
                        out=bufD[:, 0:G], in_=ps_a[:, 0:G]), "VE")
                    S.wait("scalar", "VE", ve_kev)
                    se_ksq = S.put("scalar", lambda e: e.activation(
                        out=bufD[:, G:2 * G], in_=bufD[:, 0:G], func=AF.Square), "SE")
                    S.wait("tensor", "SE", se_ksq)
                    pe_kss = S.put("tensor", lambda e: e.matmul(
                        ps_b[0:2, 0:G], ind2_t[:, :], bufD[:, G:2 * G],
                        start=True, stop=True), "PE")
                    S.wait("vector", "PE", pe_kss)
                    ve_kssr = S.put("vector", lambda e: e.tensor_copy(
                        out=qssr[0:2, :], in_=ps_b[0:2, 0:G]), "VE")
                    S.wait("scalar", "VE", ve_kssr)
                    def _krst(e):
                        e.activation(out=qssr[0:2, :], in_=qssr[0:2, :], func=AF.Ln,
                                     scale=cst(3)[0:2, :], bias=cst(2)[0:2, :])
                        return e.activation(out=qssr[0:2, :], in_=qssr[0:2, :],
                                            func=AF.Exp, scale=cst(4)[0:2, :])
                    se_krst = S.put("scalar", _krst, "SE")
                    S.wait("tensor", "SE", se_krst)
                    pe_kbc = S.put("tensor", lambda e: e.matmul(
                        ps_b[:, 0:G], p2_t[:, :], qssr[0:2, :], start=True, stop=True), "PE")
                    S.wait("vector", "PE", pe_kbc)
                    def _knrm(e, kv=kv, t0=t0):
                        e.tensor_mul(bufD[:, 0:G], bufD[:, 0:G], ps_b[:, 0:G])
                        return e.tensor_scalar(
                            out=kvp[kv][:, t0:t0 + G], in0=bufD[:, 0:G],
                            scalar1=knw_t[:, :], scalar2=None, op0=ALU.mult)
                    S.put("vector", _knrm, "VE")

                for ts_ in range(4):
                    ti = g * 4 + ts_
                    S.wait("tensor", "VE")
                    def _vmm(e, ts_=ts_):
                        for d in range(NCH):
                            ins = e.matmul(ps_a[:, 0:256],
                                           Btile(d)[:, 128 * ts_:128 * (ts_ + 1)],
                                           wv_s[:, 256 * d:256 * (d + 1)],
                                           start=(d == 0), stop=(d == NCH - 1))
                        return ins
                    pe_v = S.put("tensor", _vmm, "PE")
                    S.wait("vector", "PE", pe_v)
                    S.wait("vector", "VE", ve_vext1)
                    def _vev(e, ti=ti):
                        dst = vext[:, 260 * ti:260 * (ti + 1)].rearrange(
                            "p (k c) -> p k c", k=4)[:, :, 0:64]
                        src = ps_a[:, 0:256].rearrange("p (k c) -> p k c", k=4)
                        return e.tensor_copy(out=dst, in_=src)
                    S.put("vector", _vev, "VE")
                prev = {"VE": S.cnt["VE"], "PE": S.cnt["PE"]}

            if KSTOP == "a1":
                break
            for g in [0] if KSTOP in A2STOPS else range(NG):
                t0 = g * G
                S.wait("sync", "VE", prev["VE"])
                S.wait("sync", "PE", prev["PE"])
                for d in range(NCH):
                    S.dma("sync", Atile(d), xt[s, 128 * d:128 * (d + 1), t0:t0 + G], "LD")
                S.wait("sync", "GD")
                S.wait("sync", "PE")
                S.dma("sync", mrow[:, :], mrst_d[s, 0:1, t0:t0 + G], "LD")
                S.dma("sync", rrow[:, :], mrst_d[s, 1:2, t0:t0 + G], "LD")
                for yc in range(NCH):
                    S.dma("sync", wo_g[:, 128 * yc:128 * (yc + 1)],
                          woT[128 * yc:128 * (yc + 1), 0:128], "LDW0")
                S.wait("tensor", "LD")
                pe_bc = bcast_mr(mrow[:, :], rrow[:, :], wait_ve=prev["VE"])
                ve_h1 = h1_build([Atile(d) for d in range(NCH)],
                                 [Btile(d) for d in range(NCH)], s, pe_bc)

                for qc in range(NCH):
                    S.wait("tensor", "VE", ve_h1)
                    if qc > 0:
                        S.wait("tensor", "VE")
                    def _qmm(e, qc=qc):
                        for d in range(NCH):
                            ins = e.matmul(ps_a[:, 0:G],
                                           wq_s[:, 1024 * d + 128 * qc:1024 * d + 128 * (qc + 1)],
                                           Btile(d), start=(d == 0), stop=(d == NCH - 1))
                        return ins
                    pe_q = S.put("tensor", _qmm, "PE")
                    S.wait("vector", "PE", pe_q)
                    ve_qev = S.put("vector", lambda e, qc=qc: e.tensor_copy(
                        out=Ctile(qc), in_=ps_a[:, 0:G]), "VE")
                    S.wait("scalar", "VE", ve_qev)
                    se_qsq = S.put("scalar", lambda e, qc=qc: e.activation(
                        out=bufD[:, 0:G], in_=Ctile(qc), func=AF.Square), "SE")
                    S.wait("tensor", "SE", se_qsq)
                    pe_qss = S.put("tensor", lambda e, qc=qc: e.matmul(
                        ps_b[0:16, 0:G], ind16_t[:, 16 * qc:16 * (qc + 1)], bufD[:, 0:G],
                        start=(qc == 0), stop=(qc == NCH - 1)), "PE")
                S.wait("vector", "PE", pe_qss)
                ve_qssr = S.put("vector", lambda e: e.tensor_copy(
                    out=qssr[:, :], in_=ps_b[0:16, 0:G]), "VE")
                S.wait("scalar", "VE")
                def _qrst(e):
                    e.activation(out=qssr[:, :], in_=qssr[:, :], func=AF.Ln,
                                 scale=cst(3)[0:16, :], bias=cst(2)[0:16, :])
                    return e.activation(out=qssr[:, :], in_=qssr[:, :],
                                        func=AF.Exp, scale=cst(4)[0:16, :])
                se_qrst = S.put("scalar", _qrst, "SE")
                for qc in range(NCH):
                    S.wait("tensor", "SE", se_qrst)
                    S.wait("tensor", "VE")
                    pe_qbc = S.put("tensor", lambda e, qc=qc: e.matmul(
                        ps_b[:, 0:G], indq_t[:, 128 * qc:128 * (qc + 1)],
                        qssr[:, :], start=True, stop=True), "PE")
                    S.wait("vector", "PE", pe_qbc)
                    def _qn(e, qc=qc):
                        e.tensor_mul(bufD[:, 0:G], Ctile(qc)[:, :], ps_b[:, 0:G])
                        return e.tensor_scalar(
                            out=qp[qc][:, :], in0=bufD[:, 0:G],
                            scalar1=qnw_t[:, :], scalar2=None, op0=ALU.mult)
                    S.put("vector", _qn, "VE")
                ve_qn = S.cnt["VE"]
                S.wait("vector", "GE", ge_masks)
                if KSTOP == "a2q":
                    continue

                for qt in range(NQT):
                    qi = g * NQT + qt
                    kts = list(range(max(0, qi - 4), qi + 1))
                    n = len(kts)
                    se_exp_prev = None
                    ve_mask_prev = None
                    pe_av_last = None
                    for hi in range(H + 1):
                        if hi < H:
                            r = hi
                            h = PERM[r]
                            S.wait("tensor", "VE", ve_qn)
                            if se_exp_prev is not None:
                                S.wait("tensor", "SE", se_exp_prev)
                            if hi == 0:
                                S.wait("tensor", "VE")
                                S.wait("tensor", "SE")
                            def _sc(e, h=h, r=r, qt=qt, kts=kts):
                                ko = 64 * ((h // 4) % 2)
                                for j, kt in enumerate(kts):
                                    ins = e.matmul(
                                        ps_sc[:, 128 * j:128 * (j + 1)],
                                        kvp[(h // 4) // 2][ko:ko + 64, 128 * kt:128 * (kt + 1)],
                                        qp[r // 2][64 * (r % 2):64 * (r % 2) + 64,
                                                   128 * qt:128 * (qt + 1)],
                                        start=True, stop=True)
                                return ins
                            pe_sc = S.put("tensor", _sc, "PE")
                        if hi > 0 and KSTOP != "a2sc":
                            rp = hi - 1
                            hpv = PERM[rp]
                            S.wait("tensor", "VE", ve_mask_prev)
                            def _av(e, rp=rp, hpv=hpv, kts=kts, n=n):
                                for j, kt in enumerate(kts):
                                    v0 = 260 * kt + 65 * (hpv // 4)
                                    ins = e.matmul(
                                        ps_y[0:65, 128 * rp:128 * (rp + 1)],
                                        vext[:, v0:v0 + 65],
                                        attnT[:, 128 * j:128 * (j + 1)],
                                        start=(j == 0), stop=(j == n - 1))
                                return ins
                            pe_av_last = S.put("tensor", _av, "PE")
                        if hi < H:
                            S.wait("scalar", "PE", pe_av_last if pe_av_last is not None else pe_sc)
                            se_exp_prev = S.put("scalar", lambda e, n=n: e.activation(
                                out=attnT[:, 0:128 * n], in_=ps_sc[:, 0:128 * n],
                                func=AF.Exp, scale=cst(0)), "SE")
                            S.wait("vector", "SE", se_exp_prev)
                            def _msk(e, n=n, qi=qi):
                                ins = e.tensor_mul(attnT[:, 128 * (n - 1):128 * n],
                                                   attnT[:, 128 * (n - 1):128 * n],
                                                   mask_diag[:, :])
                                if qi >= 4:
                                    ins = e.tensor_mul(attnT[:, 0:128],
                                                       attnT[:, 0:128], mask_far[:, :])
                                return ins
                            ve_mask_prev = S.put("vector", _msk, "VE")

                    if KSTOP in ("a2sc", "a2av"):
                        continue
                    S.wait("scalar", "PE", pe_av_last)
                    S.wait("scalar", "GD")
                    se_sr = S.put("scalar", lambda e: e.copy(
                        out=sr65[:, :], in_=ps_y[0:65, 0:2048]), "SE")
                    S.wait("gpsimd", "SE", se_sr)
                    S.dma("gpsimd", srd.rearrange("r q -> (r q)").rearrange("(p f) -> p f", p=1),
                          sr65[64:65, 0:2048], "GD")
                    S.wait("gpsimd", "GD")
                    S.dma("gpsimd", srow16[:, :], srd[:, :], "GD")
                    if KSTOP == "a2sr":
                        continue
                    def _yev(e, qt=qt):
                        for r in range(0, H, 2):
                            dst = ysb[0:64, 512 * (r // 2) + 128 * qt:512 * (r // 2) + 128 * (qt + 1)]
                            ins = e.tensor_copy(out=dst, in_=sr65[0:64, 128 * r:128 * (r + 1)])
                        return ins
                    S.wait("vector", "SE", se_sr)
                    se_yev = S.put("vector", _yev, "VE")
                    for r in range(1, H, 2):
                        S.dma("gpsimd",
                              ysb[64:128, 512 * (r // 2) + 128 * qt:512 * (r // 2) + 128 * (qt + 1)],
                              sr65[0:64, 128 * r:128 * (r + 1)], "GD")
                    gd_yev = S.cnt["GD"]
                    S.wait("vector", "GD", gd_yev)
                    ve_rec = S.put("vector", lambda e: e.reciprocal(
                        out=recip16[:, :], in_=srow16[:, :]), "VE")
                    for half in range(2):
                        S.wait("tensor", "VE", ve_rec)
                        if half == 0:
                            S.wait("tensor", "VE", se_yev)
                        else:
                            S.wait("tensor", "VE")
                        def _rb(e, half=half):
                            for qc in range(4 * half, 4 * half + 4):
                                ins = e.matmul(ps_b[:, 128 * (qc % 4):128 * (qc % 4 + 1)],
                                               indq_t[:, 128 * qc:128 * (qc + 1)],
                                               recip16[:, :], start=True, stop=True)
                            return ins
                        pe_rb = S.put("tensor", _rb, "PE")
                        S.wait("vector", "PE", pe_rb)
                        if half == 0:
                            S.wait("vector", "GD", gd_yev)
                        def _nrm(e, half=half, qt=qt):
                            for qc in range(4 * half, 4 * half + 4):
                                sl = ysb[:, 512 * qc + 128 * qt:512 * qc + 128 * (qt + 1)]
                                ins = e.tensor_mul(sl, sl, ps_b[:, 128 * (qc % 4):128 * (qc % 4 + 1)])
                            return ins
                        S.put("vector", _nrm, "VE")

                if KSTOP in ("a2att", "a2sc", "a2av", "a2sr"):
                    continue
                if DEBUG_OUT and s == 0 and g == 0:
                    S.wait("sync", "VE", S.cnt["VE"])
                    S.dma("sync", qsdbg[:, :], qssr[:, :], "ST")
                    for qc in range(NCH):
                        S.dma("sync", qdbg[qc], qp[qc][:, :], "ST")
                    for i in range(2):
                        S.dma("sync", kdbg[i], kvp[i][:, :], "ST")
                    S.dma("sync", ydbg[:, :], ysb[:, :], "ST")
                ve_norm_all = S.cnt["VE"]
                pe_o = None
                for dd in range(NCH):
                    if dd > 0:
                        S.wait("sync", "PE", pe_o)
                        for yc in range(NCH):
                            S.dma("sync", wo_g[:, 128 * yc:128 * (yc + 1)],
                                  woT[128 * yc:128 * (yc + 1), 128 * dd:128 * (dd + 1)], "LDW0")
                    S.wait("tensor", "VE", ve_norm_all)
                    if dd > 0:
                        S.wait("tensor", "VE")
                    S.wait("tensor", "LDW0", S.cnt["LDW0"])
                    def _omm(e, dd=dd):
                        for yc in range(NCH):
                            ins = e.matmul(ps_a[:, 0:G],
                                           wo_g[:, 128 * yc:128 * (yc + 1)],
                                           ysb[:, 512 * yc:512 * (yc + 1)],
                                           start=(yc == 0), stop=(yc == NCH - 1))
                        return ins
                    pe_o = S.put("tensor", _omm, "PE")
                    S.wait("vector", "PE", pe_o)
                    def _x2(e, d=dd, s=s):
                        return e.scalar_tensor_tensor(
                            out=Btile(d), in0=ps_a[:, 0:G], scalar=mod_ap(2, d, s),
                            in1=Atile(d), op0=ALU.mult, op1=ALU.add)
                    ve_x2 = S.put("vector", _x2, "VE")
                    S.wait("sync", "VE", ve_x2)
                    S.dma("sync", x2d[s, 128 * dd:128 * (dd + 1), t0:t0 + G], Btile(dd), "ST")

                se_r2, _ = ln_stats([Btile(d) for d in range(NCH)], wait_ve=S.cnt["VE"])
                pe_bc2 = bcast_mr(mrow[:, :], rrow[:, :], wait_se=se_r2)
                S.wait("vector", "PE", pe_bc2)
                def _xh(e, s=s):
                    for d in range(NCH):
                        e.tensor_sub(Ctile(d), Btile(d), ps_sc[:, 0:G])
                        e.tensor_mul(Ctile(d), Ctile(d), ps_y[:, 0:G])
                        e.tensor_scalar(out=bufD[:, 0:G], in0=Ctile(d),
                                        scalar1=mod_ap(4, d, s), scalar2=mod_ap(3, d, s),
                                        op0=ALU.mult, op1=ALU.add)
                        e.tensor_reduce(out=maxacc[:, 1:2], in_=bufD[:, 0:G],
                                        axis=AX.X, op=ALU.max, apply_absolute_value=True)
                        ins = e.tensor_tensor(out=maxacc[:, 0:1], in0=maxacc[:, 0:1],
                                              in1=maxacc[:, 1:2], op=ALU.max)
                    return ins
                ve_xh = S.put("vector", _xh, "VE")
                S.wait("sync", "VE", ve_xh)
                for d in range(NCH):
                    S.dma("sync", xhd[s, 128 * d:128 * (d + 1), t0:t0 + G], Ctile(d), "ST")
                prev = {"VE": S.cnt["VE"], "PE": S.cnt["PE"]}

        if KSTOP in ("a1", "a2", "a2q") or KSTOP in A2STOPS:
            S.wait("sync", "LD", S.cnt["LD"])
            S.wait("sync", "LDC", S.cnt["LDC"])
            S.wait("sync", "ST", S.cnt["ST"])
            S.wait("gpsimd", "GD", S.cnt["GD"])
            S.wait("sync", "VE", S.cnt["VE"])
            S.wait("sync", "PE", S.cnt["PE"])
            S.wait("sync", "SE", S.cnt["SE"])
            S.flush(block)
            att.close()
            S.check()
            return nc
        S.wait("tensor", "VE")
        pe_tr = S.put("tensor", lambda e: e.transpose(
            ps_b[0:1, 0:128], maxacc[:, 0:1], identf_t[:, :]), "PE")
        S.wait("vector", "PE", pe_tr)
        ve_mh = S.put("vector", lambda e: e.tensor_reduce(
            out=z8[0:1, 0:1], in_=ps_b[0:1, 0:128], axis=AX.X, op=ALU.max), "VE")
        S.wait("gpsimd", "VE", ve_mh)
        S.dma("gpsimd", ccin[0][:, :], z8[:, :], "GD")
        S.wait("gpsimd", "GD")
        cc1 = S.put("gpsimd", lambda e: e.collective_compute(
            "AllReduce", ALU.max, replica_groups=[list(range(8))],
            ins=[ccin[0]], outs=[ccout[0]]), "CC")
        S.wait("gpsimd", "CC", cc1)
        S.dma("gpsimd", z8[:, :], ccout[0][:, :], "GD")
        S.wait("vector", "GD")
        ve_sx = S.chain("vector", [
            lambda e: e.tensor_scalar(out=rows[0:1, 1:2], in0=z8[0:1, 0:1],
                                      scalar1=cst(7)[0:1, :], scalar2=1e-12,
                                      op0=ALU.mult, op1=ALU.max),
            lambda e: e.reciprocal(out=rows[0:1, 2:3], in_=rows[0:1, 1:2]),
        ], "VE")
        S.wait("tensor", "VE", ve_sx)
        pe_sxb = S.put("tensor", lambda e: e.matmul(
            ps_b[:, 0:1], ones1[:, :], rows[0:1, 2:3], start=True, stop=True), "PE")
        S.wait("vector", "PE", pe_sxb)
        def _avec(e):
            for s in range(NSEQ):
                for d in range(NCH):
                    e.tensor_scalar(out=avec[:, 2 * d + s:2 * d + s + 1],
                                    in0=mod_ap(4, d, s),
                                    scalar1=vecb[:, 0:1], scalar2=None, op0=ALU.mult)
                    ins = e.tensor_scalar(out=bvec[:, 2 * d + s:2 * d + s + 1],
                                          in0=mod_ap(3, d, s),
                                          scalar1=vecb[:, 0:1], scalar2=None, op0=ALU.mult)
            return ins
        ve_avec = S.chain("vector", [
            lambda e: e.tensor_copy(out=vecb[:, 0:1], in_=ps_b[:, 0:1]),
            _avec,
        ], "VE")
        if DEBUG_OUT:
            S.wait("sync", "VE", ve_avec)
            S.dma("sync", scald[0:1, 0:8], z8[:, :], "ST")
            S.dma("sync", moddbg2[:, :], modT[:, :], "ST")

        S.flush(block)
        att.close()

        if PHASES == "attn":
            S.wait("sync", "ST", S.cnt["ST"])
            for s in range(NSEQ):
                S.dma("sync", yTd[s], x2d[s], "ST")
            S.wait("sync", "LD", S.cnt["LD"])
            S.wait("sync", "ST", S.cnt["ST"])
            S.wait("gpsimd", "GD", S.cnt["GD"])
            S.flush(block)
            S.check()
            return nc


        ffn = ExitStack()
        sbf = lambda n, sh, dt=F32: ffn.enter_context(nc.sbuf_tensor(n, sh, dt))
        xga = [sbf(f"xga{i}", [128, G]) for i in range(2)]
        wr = [sbf(f"wr{i}", [128, 1024]) for i in range(2)]
        wdq = bufD
        wh_n = sbf("wh_n", [128, 1024], BF16)
        wl_n = sbf("wl_n", [128, 1024], BF16)
        lhsTh = sbf("lhsTh", [128, 2 * 1024], BF16)
        lhsTl = sbf("lhsTl", [128, 2 * 1024], BF16)
        xg = sbf("xg", [128, TOK])
        p4_t = [sbf(f"p4_t{i}", [128, G]) for i in range(2)]
        roh = [sbf(f"roh{i}", [128, G], BF16) for i in range(2)]
        rol = [sbf(f"rol{i}", [128, G], BF16) for i in range(2)]
        sblk_g = sbf("sblk_g", [128, 96])
        rblk_g = sbf("rblk_g", [128, 96])
        rows2 = sbf("rows2", [2, 1])
        psF = ffn.enter_context(nc.psum_tensor("psF", [128, 4096], F32))
        f2s = ExitStack()
        rxh = f2s.enter_context(nc.sbuf_tensor("rxh", [128, NCH * TOK], BF16))
        rxl = f2s.enter_context(nc.sbuf_tensor("rxl", [128, NCH * TOK], BF16))
        pbig = [f2s.enter_context(nc.sbuf_tensor(f"pbig{i}", [128, 2048], F32))
                for i in range(2)]

        st_attn = S.cnt["ST"]

        ve_f2a = 0
        ve_xpar = [0, 0]
        S.wait("sync", "ST", st_attn)
        for g in range(NGT):
            s_, gg = divmod(g, NG)
            t0 = gg * G
            for d in range(NCH):
                par = (g * NCH + d) % 2
                S.wait("sync", "VE", ve_xpar[par])
                S.dma("sync", xga[par][:, :],
                      xhd[s_, 128 * d:128 * (d + 1), t0:t0 + G], f"LDW{par}")
                S.wait("vector", f"LDW{par}")
                if g == 0 and d == 0:
                    S.wait("vector", "VE", ve_avec)
                def _rx(e, s_=s_, g=g, d=d, par=par):
                    col = d * TOK + s_ * T + (g % NG) * G
                    t = xga[par][:, :]
                    e.tensor_scalar(out=t, in0=t,
                                    scalar1=avec[:, 2 * d + s_:2 * d + s_ + 1],
                                    scalar2=bvec[:, 2 * d + s_:2 * d + s_ + 1],
                                    op0=ALU.mult, op1=ALU.add)
                    e.tensor_scalar(out=t, in0=t, scalar1=MAGIC, scalar2=MAGIC,
                                    op0=ALU.add, op1=ALU.subtract)
                    e.tensor_copy(out=rxh[:, col:col + G], in_=t)
                    ins = e.tensor_tensor(out=rxl[:, col:col + G], in0=t,
                                          in1=rxh[:, col:col + G], op=ALU.subtract)
                    return ins
                ve_xpar[par] = S.put("vector", _rx, "VE")
        ve_f2a = S.cnt["VE"]

        wsrc = [wup, wgate]
        pdst = [pud, pgd]
        mcol = [2, 3]
        deq_ve = [0, 0]
        tr_pe = 0
        ev_se2 = [0, 0]
        red_mark = [0, 0]
        mm_pe = {0: 0, 1: 0}
        ps_dma = [0, 0]

        def dequant_1024(par_buf, ve_sem_tag="VE"):
            n_blk = 32
            S.put("vector", lambda e: e.tensor_reduce(
                out=sblk[:, 0:n_blk],
                in_=par_buf.rearrange("p (b c) -> p b c", b=n_blk),
                axis=AX.X, op=ALU.max, apply_absolute_value=True), "VE")
            S.chain("vector", [
                lambda e: e.tensor_scalar(out=sblk[:, 0:n_blk], in0=sblk[:, 0:n_blk],
                                          scalar1=cst(6), scalar2=1e-12,
                                          op0=ALU.mult, op1=ALU.max),
                lambda e: e.reciprocal(out=rblk[:, 0:n_blk], in_=sblk[:, 0:n_blk]),
                lambda e: e.tensor_tensor(
                    out=wdq[:, 0:1024].rearrange("p (b c) -> p b c", b=n_blk),
                    in0=par_buf.rearrange("p (b c) -> p b c", b=n_blk),
                    in1=rblk[:, 0:n_blk].unsqueeze(2).broadcast_to([128, n_blk, 32]),
                    op=ALU.mult),
            ], "VE")
            def _fin(e):
                e.tensor_scalar(out=wdq[:, 0:1024], in0=wdq[:, 0:1024], scalar1=MAGIC,
                                scalar2=MAGIC, op0=ALU.add, op1=ALU.subtract)
                e.tensor_tensor(
                    out=wdq[:, 0:1024].rearrange("p (b c) -> p b c", b=n_blk),
                    in0=wdq[:, 0:1024].rearrange("p (b c) -> p b c", b=n_blk),
                    in1=sblk[:, 0:n_blk].unsqueeze(2).broadcast_to([128, n_blk, 32]),
                    op=ALU.mult)
                e.tensor_copy(out=wh_n[:, :], in_=wdq[:, 0:1024])
                return e.tensor_tensor(out=wl_n[:, :], in0=wdq[:, 0:1024],
                                       in1=wh_n[:, :], op=ALU.subtract)
            return S.put("vector", _fin, "VE")

        sblk = sblk_g
        rblk = rblk_g

        first_mm = True
        pend_red = []
        for wi in range(2):
            for oc in range(NOCH):
                it = wi * NOCH + oc
                par = it % 2
                S.wait("sync", "VE", deq_ve[par])
                S.dma("sync", wr[par][:, :], wsrc[wi][128 * oc:128 * (oc + 1), :], f"LW{par}")
                S.wait("vector", f"LW{par}")
                S.wait("vector", "PE", tr_pe)
                deq_ve[par] = dequant_1024(wr[par][:, :])
                for emit in pend_red:
                    emit()
                pend_red = []
                S.wait("tensor", "VE", deq_ve[par])
                S.wait("tensor", "SE", ev_se2[0])
                if not first_mm:
                    S.wait("tensor", "PE", mm_pe[0])
                def _tr(e):
                    for ih in range(NCH):
                        e.transpose(psF[:, 64 * ih:64 * ih + 64].bitcast(BF16),
                                    wh_n[:, 128 * ih:128 * (ih + 1)], identb_t[:, :])
                        ins = e.transpose(psF[:, 512 + 64 * ih:512 + 64 * ih + 64].bitcast(BF16),
                                          wl_n[:, 128 * ih:128 * (ih + 1)], identb_t[:, :])
                    return ins
                tr_pe = S.put("tensor", _tr, "PE")
                S.wait("scalar", "PE", tr_pe)
                S.wait("scalar", "PE", mm_pe[1])
                def _trev(e, par=par):
                    for ih in range(NCH):
                        e.copy(out=lhsTh[:, 1024 * par + 128 * ih:1024 * par + 128 * (ih + 1)],
                               in_=psF[:, 64 * ih:64 * ih + 64].bitcast(BF16))
                        ins = e.copy(out=lhsTl[:, 1024 * par + 128 * ih:1024 * par + 128 * (ih + 1)],
                                     in_=psF[:, 512 + 64 * ih:512 + 64 * ih + 64].bitcast(BF16))
                    return ins
                trev_se = S.put("scalar", _trev, "SE")

                for half in range(2):
                    S.wait("tensor", "SE", trev_se)
                    S.wait("tensor", "SE", ev_se2[half])
                    if first_mm:
                        S.wait("tensor", "VE", ve_f2a)
                    def _mm(e, par=par, half=half):
                        for pi, (lt, rx) in enumerate(((lhsTh, rxh), (lhsTh, rxl), (lhsTl, rxh))):
                            for ih in range(NCH):
                                step = pi * NCH + ih
                                for tt in range(4):
                                    ins = e.matmul(
                                        psF[:, 2048 * half + 512 * tt:2048 * half + 512 * (tt + 1)],
                                        lt[:, 1024 * par + 128 * ih:1024 * par + 128 * (ih + 1)],
                                        rx[:, ih * TOK + 2048 * half + 512 * tt:
                                           ih * TOK + 2048 * half + 512 * (tt + 1)],
                                        start=(step == 0), stop=(step == 23))
                        return ins
                    mm_pe[half] = S.put("tensor", _mm, "PE")
                    first_mm = False
                    sp = half
                    S.wait("scalar", "PE", mm_pe[half])
                    S.wait("scalar", f"PS{sp}", S.cnt[f"PS{sp}"])
                    S.wait("scalar", "VE", red_mark[sp])
                    def _evh(e, half=half, sp=sp):
                        for tt in range(4):
                            ins = e.copy(
                                out=pbig[sp][:, 512 * tt:512 * (tt + 1)],
                                in_=psF[:, 2048 * half + 512 * tt:
                                        2048 * half + 512 * (tt + 1)])
                        return ins
                    ev_se2[half] = S.put("scalar", _evh, "SE")
                    def _mkred(sp=sp, wi=wi, ev=None):
                        def emit():
                            S.wait("vector", "SE", ev)
                            ve_red = S.put("vector", lambda e: e.tensor_reduce(
                                out=maxacc[:, 1:2], in_=pbig[sp][:, :], axis=AX.X,
                                op=ALU.max, apply_absolute_value=True), "VE")
                            S.wait("vector", "VE", ve_red)
                            red_mark[sp] = S.chain("vector", [
                                lambda e: e.tensor_tensor(
                                    out=maxacc[:, mcol[wi]:mcol[wi] + 1],
                                    in0=maxacc[:, mcol[wi]:mcol[wi] + 1],
                                    in1=maxacc[:, 1:2], op=ALU.max)], "VE")
                        return emit
                    pend_red.append(_mkred(ev=ev_se2[half]))
                    S.wait("gpsimd", "SE", ev_se2[half])
                    ps_dma[sp] = S.dma(
                        "gpsimd",
                        pdst[wi][oc, :, 2048 * half:2048 * (half + 1)],
                        pbig[sp][:, :], f"PS{sp}")

        for emit in pend_red:
            emit()
        pend_red = []
        S.flush(block)
        f2s.close()
        S.wait("tensor", "VE")
        S.wait("tensor", "SE")
        pe_tr2 = S.put("tensor", lambda e: e.transpose(
            psF[0:2, 0:128], maxacc[:, 2:4], identf_t[:, :]), "PE")
        S.wait("vector", "PE", pe_tr2)
        ve_m2 = S.put("vector", lambda e: e.tensor_reduce(
            out=rows2[0:2, 0:1], in_=psF[0:2, 0:128], axis=AX.X, op=ALU.max), "VE")
        S.wait("gpsimd", "VE", ve_m2)
        S.wait("gpsimd", "PS0", S.cnt["PS0"])
        S.wait("gpsimd", "PS1", S.cnt["PS1"])
        S.wait("gpsimd", "VE", ve_consts)
        S.dma("gpsimd", ccin[1][0:1, 2:8], z8[0:1, 2:8], "GD")
        S.dma("gpsimd", ccin[1][0:1, 0:2], rows2[0:2, 0:1], "GD")
        S.wait("gpsimd", "GD")
        cc2 = S.put("gpsimd", lambda e: e.collective_compute(
            "AllReduce", ALU.max, replica_groups=[list(range(8))],
            ins=[ccin[1]], outs=[ccout[1]]), "CC")
        S.wait("gpsimd", "CC", cc2)
        S.dma("gpsimd", z8[:, :], ccout[1][:, :], "GD")
        S.wait("vector", "GD")
        ve_ar2 = S.chain("vector", [
            lambda e: e.tensor_scalar(out=rows[0:1, 3:4], in0=z8[0:1, 0:1],
                                      scalar1=rows[0:1, 1:2], scalar2=None, op0=ALU.mult),
            lambda e: e.tensor_scalar(out=rows[0:1, 4:5], in0=z8[0:1, 1:2],
                                      scalar1=rows[0:1, 1:2], scalar2=None, op0=ALU.mult),
            lambda e: e.tensor_scalar(out=rows[0:1, 5:6], in0=rows[0:1, 3:4],
                                      scalar1=cst(7)[0:1, :], scalar2=1e-12,
                                      op0=ALU.mult, op1=ALU.max),
            lambda e: e.tensor_scalar(out=rows[0:1, 8:9], in0=rows[0:1, 4:5],
                                      scalar1=cst(7)[0:1, :], scalar2=1e-12,
                                      op0=ALU.mult, op1=ALU.max),
            lambda e: e.reciprocal(out=rows[0:1, 6:7], in_=rows[0:1, 5:6]),
            lambda e: e.tensor_scalar(out=rows[0:1, 7:8], in0=rows[0:1, 6:7],
                                      scalar1=rows[0:1, 1:2], scalar2=None, op0=ALU.mult),
            lambda e: e.reciprocal(out=rows[0:1, 9:10], in_=rows[0:1, 8:9]),
            lambda e: e.tensor_scalar(out=rows[0:1, 10:11], in0=rows[0:1, 9:10],
                                      scalar1=rows[0:1, 1:2], scalar2=None, op0=ALU.mult),
        ], "VE")
        last_ve = ve_ar2
        for col, slot in ((1, 7), (2, 10), (3, 8)):
            S.wait("tensor", "VE", last_ve)
            pe_b = S.put("tensor", lambda e, slot=slot: e.matmul(
                psF[:, 0:1], ones1[:, :], rows[0:1, slot:slot + 1],
                start=True, stop=True), "PE")
            S.wait("vector", "PE", pe_b)
            last_ve = S.put("vector", lambda e, col=col: e.tensor_copy(
                out=vecb[:, col:col + 1], in_=psF[:, 0:1]), "VE")
        ve_vecb2 = last_ve

        f34 = ExitStack()
        wdres = f34.enter_context(nc.sbuf_tensor("wdres", [128, NOCH * 2048], BF16))
        f34b = ExitStack()
        BIGW = 2048
        bigA = [f34b.enter_context(nc.sbuf_tensor(f"bigA{i}", [128, BIGW], F32))
                for i in range(2)]
        bigB = [f34b.enter_context(nc.sbuf_tensor(f"bigB{i}", [128, BIGW], F32))
                for i in range(2)]
        bigC = [f34b.enter_context(nc.sbuf_tensor(f"bigC{i}", [128, BIGW], F32))
                for i in range(2)]
        cons3 = [0, 0]
        S.wait("sync", "PS0", S.cnt["PS0"])
        S.wait("sync", "PS1", S.cnt["PS1"])
        wd_tile = 0
        pe_tw = 0
        se_tw = 0
        deq_mark = 0
        for it in range(NOCH * 2):
            oc, hh = divmod(it, 2)
            par = it % 2
            c0 = BIGW * hh
            S.wait("sync", "VE", cons3[par])
            S.wait("sync", f"PS{par}", S.cnt[f"PS{par}"])
            S.dma("sync", bigA[par][:, :], pgd[oc, :, c0:c0 + BIGW], f"PL{par}")
            S.wait("vector", f"PL{par}")
            if it == 0:
                S.wait("vector", "VE", ve_vecb2)
            def _rg(e, par=par):
                e.tensor_scalar(out=bigA[par][:, :], in0=bigA[par][:, :],
                                scalar1=vecb[:, 2:3], scalar2=MAGIC,
                                op0=ALU.mult, op1=ALU.add)
                return e.tensor_scalar(out=bigA[par][:, :], in0=bigA[par][:, :],
                                       scalar1=MAGIC, scalar2=None, op0=ALU.subtract)
            ve_rg = S.put("vector", _rg, "VE")
            S.wait("scalar", "VE", ve_rg)
            se_sg2 = S.put("scalar", lambda e, par=par: e.activation(
                out=bigB[par][:, :], in_=bigA[par][:, :], func=AF.Sigmoid,
                scale=vecb[:, 3:4]), "SE")
            S.wait("vector", "SE", se_sg2)
            S.wait("vector", f"PS{par}", S.cnt[f"PS{par}"])
            def _s16p3(e, par=par):
                e.tensor_scalar(out=bigB[par][:, :], in0=bigB[par][:, :],
                                scalar1=65536.0, scalar2=MAGIC,
                                op0=ALU.mult, op1=ALU.add)
                e.tensor_scalar(out=bigB[par][:, :], in0=bigB[par][:, :],
                                scalar1=MAGIC, scalar2=65535.0,
                                op0=ALU.subtract, op1=ALU.min)
                e.tensor_tensor(out=bigC[par][:, :], in0=bigA[par][:, :],
                                in1=bigB[par][:, :], op=ALU.mult)
                return e.tensor_reduce(out=maxacc[:, 1:2], in_=bigC[par][:, :],
                                       axis=AX.X, op=ALU.max, apply_absolute_value=True)
            ve_p3 = S.put("vector", _s16p3, "VE")
            S.wait("vector", "VE", ve_p3)
            cons3[par] = S.chain("vector", [
                lambda e: e.tensor_tensor(out=maxacc[:, 4:5], in0=maxacc[:, 4:5],
                                          in1=maxacc[:, 1:2], op=ALU.max)], "VE")
            S.wait("gpsimd", "VE", cons3[par])
            S.dma("gpsimd", p3d[oc, :, c0:c0 + BIGW], bigC[par][:, :], f"PS{par}")

            if hh == 0 and wd_tile < NCH * 3:
                dt_, third = divmod(wd_tile, 3)
                wd_tile += 1
                S.wait("gpsimd", "VE", deq_mark)
                S.dma("gpsimd", wr[0][:, :],
                      wdown[128 * dt_:128 * (dt_ + 1), 1024 * third:1024 * (third + 1)], "WS")
                S.wait("vector", "WS")
                S.wait("vector", "PE", pe_tw)
                deq_mark = dequant_1024(wr[0][:, :])
                S.wait("tensor", "VE", deq_mark)
                S.wait("tensor", "SE", se_tw)
                def _trw8(e):
                    for j in range(8):
                        e.transpose(psF[:, 3072 + 64 * j:3072 + 64 * j + 64].bitcast(BF16),
                                    wh_n[:, 128 * j:128 * (j + 1)], identb_t[:, :])
                        ins = e.transpose(psF[:, 3584 + 64 * j:3584 + 64 * j + 64].bitcast(BF16),
                                          wl_n[:, 128 * j:128 * (j + 1)], identb_t[:, :])
                    return ins
                pe_tw = S.put("tensor", _trw8, "PE")
                S.wait("scalar", "PE", pe_tw)
                def _twev(e, dt_=dt_, third=third):
                    for j in range(8):
                        ihh = third * 8 + j
                        e.copy(out=wdres[:, 2048 * ihh + 128 * dt_:2048 * ihh + 128 * (dt_ + 1)],
                               in_=psF[:, 3072 + 64 * j:3072 + 64 * j + 64].bitcast(BF16))
                        ins = e.copy(
                            out=wdres[:, 2048 * ihh + 1024 + 128 * dt_:
                                      2048 * ihh + 1024 + 128 * (dt_ + 1)],
                            in_=psF[:, 3584 + 64 * j:3584 + 64 * j + 64].bitcast(BF16))
                    return ins
                se_tw = S.put("scalar", _twev, "SE")

        S.wait("tensor", "VE")
        pe_tr3 = S.put("tensor", lambda e: e.transpose(
            psF[0:1, 0:128], maxacc[:, 4:5], identf_t[:, :]), "PE")
        S.wait("vector", "PE", pe_tr3)
        ve_m3 = S.put("vector", lambda e: e.tensor_reduce(
            out=z8[0:1, 2:3], in_=psF[0:1, 0:128], axis=AX.X, op=ALU.max), "VE")
        ve_m3b = S.chain("vector", [
            lambda e: e.tensor_scalar(out=rows[0:1, 11:12], in0=rows[0:1, 8:9],
                                      scalar1=1.0 / 65536.0, scalar2=None, op0=ALU.mult),
            lambda e: e.tensor_scalar(out=z8[0:1, 0:1], in0=z8[0:1, 2:3],
                                      scalar1=rows[0:1, 11:12], scalar2=None, op0=ALU.mult),
        ], "VE")
        S.wait("gpsimd", "VE", ve_m3b)
        S.wait("gpsimd", "PS0", S.cnt["PS0"])
        S.wait("gpsimd", "PS1", S.cnt["PS1"])
        S.dma("gpsimd", ccin[2][:, :], z8[:, :], "GD")
        S.wait("gpsimd", "GD")
        cc3 = S.put("gpsimd", lambda e: e.collective_compute(
            "AllReduce", ALU.max, replica_groups=[list(range(8))],
            ins=[ccin[2]], outs=[ccout[2]]), "CC")
        S.wait("gpsimd", "CC", cc3)
        S.dma("gpsimd", z8[:, :], ccout[2][:, :], "GD")
        S.wait("vector", "GD")
        ve_ar3 = S.chain("vector", [
            lambda e: e.tensor_scalar(out=rows[0:1, 12:13], in0=z8[0:1, 0:1],
                                      scalar1=cst(7)[0:1, :], scalar2=1e-12,
                                      op0=ALU.mult, op1=ALU.max),
            lambda e: e.reciprocal(out=rows[0:1, 13:14], in_=rows[0:1, 12:13]),
            lambda e: e.tensor_scalar(out=rows[0:1, 14:15], in0=rows[0:1, 13:14],
                                      scalar1=rows[0:1, 11:12], scalar2=None, op0=ALU.mult),
        ], "VE")
        S.wait("tensor", "VE", ve_ar3)
        pe_b3 = S.put("tensor", lambda e: e.matmul(
            psF[:, 0:1], ones1[:, :], rows[0:1, 14:15], start=True, stop=True), "PE")
        S.wait("vector", "PE", pe_b3)
        ve_vecb4 = S.put("vector", lambda e: e.tensor_copy(
            out=vecb[:, 4:5], in_=psF[:, 0:1]), "VE")

        cons4 = [cons3[0], cons3[1]]
        for it in range(NOCH * 2):
            oc, hh = divmod(it, 2)
            par = it % 2
            c0 = BIGW * hh
            S.wait("sync", "VE", cons4[par])
            S.dma("sync", bigA[par][:, :], pud[oc, :, c0:c0 + BIGW], f"PL{par}")
            S.dma("sync", bigB[par][:, :], p3d[oc, :, c0:c0 + BIGW], f"PL{par}")
            S.wait("vector", f"PL{par}")
            if it == 0:
                S.wait("vector", "VE", ve_vecb4)
            S.wait("vector", f"PS{par}", S.cnt[f"PS{par}"])
            def _ruf(e, par=par):
                e.tensor_scalar(out=bigA[par][:, :], in0=bigA[par][:, :],
                                scalar1=vecb[:, 1:2], scalar2=MAGIC,
                                op0=ALU.mult, op1=ALU.add)
                e.tensor_scalar(out=bigA[par][:, :], in0=bigA[par][:, :],
                                scalar1=MAGIC, scalar2=None, op0=ALU.subtract)
                e.tensor_scalar(out=bigB[par][:, :], in0=bigB[par][:, :],
                                scalar1=vecb[:, 4:5], scalar2=MAGIC,
                                op0=ALU.mult, op1=ALU.add)
                e.tensor_scalar(out=bigB[par][:, :], in0=bigB[par][:, :],
                                scalar1=MAGIC, scalar2=None, op0=ALU.subtract)
                e.tensor_tensor(out=bigC[par][:, :], in0=bigA[par][:, :],
                                in1=bigB[par][:, :], op=ALU.mult)
                return e.tensor_reduce(out=maxacc[:, 1:2], in_=bigC[par][:, :],
                                       axis=AX.X, op=ALU.max, apply_absolute_value=True)
            ve_p4 = S.put("vector", _ruf, "VE")
            S.wait("vector", "VE", ve_p4)
            cons4[par] = S.chain("vector", [
                lambda e: e.tensor_tensor(out=maxacc[:, 5:6], in0=maxacc[:, 5:6],
                                          in1=maxacc[:, 1:2], op=ALU.max)], "VE")
            S.wait("gpsimd", "VE", cons4[par])
            S.dma("gpsimd", p4d[oc, :, c0:c0 + BIGW], bigC[par][:, :], f"PS{par}")

        S.wait("tensor", "VE")
        pe_tr4 = S.put("tensor", lambda e: e.transpose(
            psF[0:1, 0:128], maxacc[:, 5:6], identf_t[:, :]), "PE")
        S.wait("vector", "PE", pe_tr4)
        ve_m4 = S.put("vector", lambda e: e.tensor_reduce(
            out=z8[0:1, 2:3], in_=psF[0:1, 0:128], axis=AX.X, op=ALU.max), "VE")
        ve_m4b = S.chain("vector", [
            lambda e: e.tensor_scalar(out=rows[0:1, 15:16], in0=rows[0:1, 12:13],
                                      scalar1=rows[0:1, 5:6], scalar2=None, op0=ALU.mult),
            lambda e: e.tensor_scalar(out=z8[0:1, 0:1], in0=z8[0:1, 2:3],
                                      scalar1=rows[0:1, 15:16], scalar2=None, op0=ALU.mult),
        ], "VE")
        S.wait("gpsimd", "VE", ve_m4b)
        S.wait("gpsimd", "PS0", S.cnt["PS0"])
        S.wait("gpsimd", "PS1", S.cnt["PS1"])
        S.dma("gpsimd", ccin[3][:, :], z8[:, :], "GD")
        S.wait("gpsimd", "GD")
        cc4 = S.put("gpsimd", lambda e: e.collective_compute(
            "AllReduce", ALU.max, replica_groups=[list(range(8))],
            ins=[ccin[3]], outs=[ccout[3]]), "CC")
        S.wait("gpsimd", "CC", cc4)
        S.dma("gpsimd", z8[:, :], ccout[3][:, :], "GD")
        S.wait("vector", "GD")
        ve_ar4 = S.chain("vector", [
            lambda e: e.tensor_scalar(out=rows[0:1, 0:1], in0=z8[0:1, 0:1],
                                      scalar1=cst(7)[0:1, :], scalar2=1e-12,
                                      op0=ALU.mult, op1=ALU.max),
            lambda e: e.reciprocal(out=rows[0:1, 2:3], in_=rows[0:1, 0:1]),
            lambda e: e.tensor_scalar(out=rows[0:1, 3:4], in0=rows[0:1, 2:3],
                                      scalar1=rows[0:1, 15:16], scalar2=None, op0=ALU.mult),
        ], "VE")
        last_ve = ve_ar4
        for col, slot in ((5, 3), (6, 0)):
            S.wait("tensor", "VE", last_ve)
            pe_b = S.put("tensor", lambda e, slot=slot: e.matmul(
                psF[:, 0:1], ones1[:, :], rows[0:1, slot:slot + 1],
                start=True, stop=True), "PE")
            S.wait("vector", "PE", pe_b)
            last_ve = S.put("vector", lambda e, col=col: e.tensor_copy(
                out=vecb[:, col:col + 1], in_=psF[:, 0:1]), "VE")
        def _g2(e):
            for s2 in range(NSEQ):
                for d in range(NCH):
                    ins = e.tensor_scalar(out=g2vec[:, 2 * d + s2:2 * d + s2 + 1],
                                          in0=mod_ap(5, d, s2),
                                          scalar1=vecb[:, 6:7], scalar2=None, op0=ALU.mult)
            return ins
        ve_g2 = S.chain("vector", [lambda e: e.engine_nop(), _g2], "VE")

        f34b.close()
        pe_f5 = 0
        pe_par = [0, 0]
        ve_ro = [0, 0]
        ve_x2used = 0
        first_f5mm = True

        def xg_load(g):
            s_, gg = divmod(g, NG)
            t0 = gg * G
            for d in range(NCH):
                S.dma("sync", xg[:, 512 * d:512 * (d + 1)],
                      x2d[s_, 128 * d:128 * (d + 1), t0:t0 + G], "XL")

        S.wait("sync", "ST", S.cnt["ST"])
        xg_load(0)
        for g in range(NGT):
            s_, gg = divmod(g, NG)
            t0 = gg * G
            gcol = s_ * T + t0
            for oc in range(NOCH):
                par = oc % 2
                S.wait("gpsimd", "VE", ve_ro[par])
                S.dma("gpsimd", p4_t[par][:, :], p4d[oc, :, gcol:gcol + G], f"WD{par}")
                S.wait("vector", f"WD{par}")
                S.wait("vector", "VE", ve_g2)
                S.wait("vector", "PE", pe_par[par])
                def _ro(e, par=par):
                    e.tensor_scalar(out=bufD[:, 0:G], in0=p4_t[par][:, :],
                                    scalar1=vecb[:, 5:6], scalar2=MAGIC,
                                    op0=ALU.mult, op1=ALU.add)
                    e.tensor_scalar(out=bufD[:, 0:G], in0=bufD[:, 0:G],
                                    scalar1=MAGIC, scalar2=None, op0=ALU.subtract)
                    e.tensor_copy(out=roh[par][:, :], in_=bufD[:, 0:G])
                    return e.tensor_tensor(out=rol[par][:, :], in0=bufD[:, 0:G],
                                           in1=roh[par][:, :], op=ALU.subtract)
                ve_ro[par] = S.put("vector", _ro, "VE")
                S.wait("tensor", "VE", ve_ro[par])
                if first_f5mm:
                    S.wait("tensor", "SE", S.cnt["SE"])
                    first_f5mm = False
                if oc < 2 and g > 0:
                    S.wait("tensor", "VE", ve_x2used)
                def _dmm(e, par=par, oc=oc):
                    for pi, (lofs, rx) in enumerate(((0, roh), (0, rol), (1024, roh))):
                        for d in range(NCH):
                            ins = e.matmul(
                                psF[:, 512 * d:512 * (d + 1)],
                                wdres[:, 2048 * oc + lofs + 128 * d:
                                      2048 * oc + lofs + 128 * (d + 1)],
                                rx[par][:, :],
                                start=(oc == 0 and pi == 0), stop=(oc == NOCH - 1 and pi == 2))
                    return ins
                pe_f5 = S.put("tensor", _dmm, "PE")
                pe_par[par] = pe_f5
            S.wait("vector", "PE", pe_f5)
            S.wait("vector", "XL", S.cnt["XL"])
            S.wait("vector", "YS", S.cnt["YS"])
            for db in range(2):
                def _yev5(e, s_=s_, db=db):
                    for d in range(4 * db, 4 * db + 4):
                        ins = e.scalar_tensor_tensor(
                            out=xg[:, 512 * d:512 * (d + 1)],
                            in0=psF[:, 512 * d:512 * (d + 1)],
                            scalar=g2vec[:, 2 * d + s_:2 * d + s_ + 1],
                            in1=xg[:, 512 * d:512 * (d + 1)],
                            op0=ALU.mult, op1=ALU.add)
                    return ins
                ve_y = S.put("vector", _yev5, "VE")
                S.wait("sync", "VE", ve_y)
                for d in range(4 * db, 4 * db + 4):
                    S.dma("sync", yTd[s_, 128 * d:128 * (d + 1), t0:t0 + G],
                          xg[:, 512 * d:512 * (d + 1)], "YS")
            ve_x2used = S.cnt["VE"]
            if g + 1 < NGT:
                S.wait("sync", "YS", S.cnt["YS"])
                xg_load(g + 1)

        if DEBUG_OUT:
            S.wait("sync", "VE", S.cnt["VE"])
            S.dma("sync", scald[0:1, 0:16], rows[0:1, 0:16], "ST")
        S.wait("sync", "LD", S.cnt["LD"])
        S.wait("sync", "ST", S.cnt["ST"])
        S.wait("sync", "YS", S.cnt["YS"])
        S.wait("sync", "XL", S.cnt["XL"])
        S.wait("sync", "WS", S.cnt["WS"])
        S.wait("gpsimd", "GD", S.cnt["GD"])
        S.wait("gpsimd", "PS0", S.cnt["PS0"])
        S.wait("gpsimd", "PS1", S.cnt["PS1"])
        S.flush(block)
        f34.close()
        ffn.close()
        S.check()
    return nc



def prep_core(inputs, c):
    import ml_dtypes
    f32 = np.float32
    x = np.asarray(inputs["L0_attn_x"])[2 * c:2 * c + 2]
    ind2 = np.zeros((128, 2), f32)
    ind2[0:64, 0] = 1.0
    ind2[64:128, 1] = 1.0
    indq = np.zeros((16, 1024), f32)
    for ch in range(8):
        for mm_ in range(128):
            indq[2 * ch + mm_ // 64, ch * 128 + mm_] = 1.0
    ind16 = np.zeros((128, 128), f32)
    for ch in range(8):
        for p in range(128):
            ind16[p, ch * 16 + 2 * ch + p // 64] = 1.0
    perm = PERM
    colperm = np.concatenate([np.arange(64 * h, 64 * h + 64) for h in perm])
    m = {
        "xt": x.transpose(0, 2, 1),
        "wqT": np.asarray(inputs["L0_attn_wq"]).T[:, colperm],
        "wkT": np.asarray(inputs["L0_attn_wk"]).T,
        "wvT": np.asarray(inputs["L0_attn_wv"]).T,
        "woT": np.asarray(inputs["L0_attn_wo"]).T[colperm, :],
        "wup": np.asarray(inputs["L0_ffn_w_up"]),
        "wgate": np.asarray(inputs["L0_ffn_w_gate"]),
        "wdown": np.asarray(inputs["L0_ffn_w_down"]),
        "wmod": np.asarray(inputs["L0_wmod"]),
        "cTd": np.asarray(inputs["L0_c"])[2 * c:2 * c + 2].T,
        "qnwd": np.tile(np.asarray(inputs["L0_attn_q_norm_w"]).reshape(-1), 2)[:, None],
        "knwd": np.tile(np.asarray(inputs["L0_attn_k_norm_w"]).reshape(-1), 2)[:, None],
        "identfd": np.eye(128, dtype=f32),
        "identbd": np.eye(128, dtype=ml_dtypes.bfloat16),
        "ind2d": ind2,
        "p2d": ind2.T,
        "indqd": indq,
        "ind16d": ind16,
    }
    return {k: np.ascontiguousarray(np.asarray(v, dtype=v.dtype if hasattr(v, "dtype") else f32))
            for k, v in m.items()}


_cache = {}


def kernel(**inputs):
    key = (PHASES, DEBUG_OUT)
    if key not in _cache:
        _cache[key] = build_program()
    nc = _cache[key]
    in_maps = [prep_core(inputs, c) for c in range(8)]
    res = run_bass_kernel_spmd(nc, in_maps, core_ids=list(range(8)))
    outs = [res.results[c]["yTd"].transpose(0, 2, 1) for c in range(8)]
    out = np.concatenate(outs, axis=0).astype(np.float32)
    kernel.dbg = res.results
    return out
